# version 09
# baseline (speedup 1.0000x reference)
import numpy as np
import concourse.bass as bass
import concourse.mybir as mybir
from concourse import library_config
from concourse.library_overlay import lower_extended_insts

F32 = mybir.dt.float32
BF16 = mybir.dt.bfloat16
I16 = mybir.dt.int16
U32 = mybir.dt.uint32
ALU = mybir.AluOpType
AF = mybir.ActivationFunctionType
AX = mybir.AxisListType
ENGS = ("sync", "tensor", "scalar", "vector", "gpsimd")


def make_cfg(L=8192, TC=512, moe=True, cap=640):
    c = {}
    c["moe"] = moe
    c["B"] = 2; c["L"] = L; c["DM"] = 1024; c["DI"] = 2048; c["N"] = 16
    c["R"] = 64; c["E"] = 8; c["F"] = 4096; c["CAP"] = cap; c["NC"] = 8
    c["T"] = c["B"] * L
    c["TSL"] = c["T"] // c["NC"]
    c["DSL"] = c["DI"] // c["NC"]
    c["NT"] = c["DSL"] * c["N"] // 128
    c["TC"] = TC
    c["NCH"] = c["T"] // TC
    c["CPB"] = L // TC
    c["TT"] = c["TSL"] // 128
    assert c["TSL"] % 128 == 0 and L % TC == 0 and c["TSL"] % TC == 0
    return c


class MK:
    def __init__(self, nc):
        self.nc = nc
        self.ops = []
        self.used = set()
        self.esem = {e: nc.alloc_semaphore("es_" + e) for e in ENGS}
        self.dsem = {e: nc.alloc_semaphore("ds_" + e) for e in ("sync", "scalar", "gpsimd")}

    def _rec(self, eng, fn, after, kind):
        tid = len(self.ops)
        after = [a for a in after if a is not None]
        if kind in ("d", "x"):
            self.used.add(tid)
        import traceback as _tb
        st = _tb.extract_stack()
        lines = [f.lineno for f in st if f.filename.endswith("mk.py") or "kernel" in f.filename]
        self.ops.append(dict(id=tid, eng=eng, fn=fn, after=list(after), kind=kind,
                             src=str(lines[-3:-1])))
        for a in after:
            ao = self.ops[a]
            if not (ao["kind"] == "c" and ao["eng"] == eng):
                self.used.add(a)
        return tid

    def op(self, eng, fn, after=()):
        return self._rec(eng, fn, after, "c")

    def dma(self, eng, dst, src, after=(), transpose=False):
        def fn(h, d=dst, s=src, t=transpose):
            if t:
                return h.dma_start(out=d, in_=s, transpose=True)
            return h.dma_start(out=d, in_=s)
        return self._rec(eng, fn, after, "d")

    def dmaop(self, eng, fn, after=()):
        return self._rec(eng, fn, after, "d")

    def cc(self, fn, after=()):
        return self._rec("gpsimd", fn, after, "x")

    def join_all(self, eng="gpsimd"):
        last = {}
        for o in self.ops:
            last[o["eng"]] = o["id"]
        afters = [v for k, v in last.items()]
        return self.op(eng, lambda h: h.nop(), after=afters)

    def fence(self, tok):
        for e in ENGS:
            self.op(e, lambda h: h.nop(), after=[tok])

    def finalize(self, name="prog"):
        nc = self.nc
        cnt = {}
        val = {}
        for o in self.ops:
            if o["id"] in self.used:
                if o["kind"] == "c":
                    sem, inc = self.esem[o["eng"]], 1
                elif o["kind"] == "d":
                    sem, inc = self.dsem[o["eng"]], 16
                else:
                    sem, inc = self.esem[o["eng"]], 1
                key = sem.name
                cnt[key] = cnt.get(key, 0) + inc
                val[o["id"]] = (sem, cnt[key], inc)
        per_eng = {e: [o for o in self.ops if o["eng"] == e] for e in ENGS}
        used = self.used

        def make_stream(eng):
            ops = per_eng[eng]
            def stream(h):
                lastw = {}
                for o in ops:
                    for a in o["after"]:
                        ao = self.ops[a]
                        if ao["kind"] == "c" and ao["eng"] == eng:
                            continue
                        sem, v, _ = val[a]
                        if lastw.get(sem.name, -1) >= v:
                            continue
                        h.wait_ge(sem, v)
                        lastw[sem.name] = v
                    try:
                        inst = o["fn"](h)
                    except Exception as ex:
                        raise RuntimeError(f"emit failed op={o['id']} eng={eng} kind={o['kind']} src={o.get('src')}") from ex
                    if o["id"] in used:
                        sem, v, inc = val[o["id"]]
                        assert inst is not None, f"op {o['id']} on {eng} returned None but is awaited"
                        inst.then_inc(sem, inc)
            return stream

        with nc.Block(name) as blk:
            for e in ENGS:
                if per_eng[e]:
                    getattr(blk, e)(make_stream(e))
        lower_extended_insts(nc)


def build_nc(c):
    nc = bass.Bass("TRN2", target_bir_lowering=False, detect_race_conditions=False)
    DM, DI, N, R, E, F, CAP = c["DM"], c["DI"], c["N"], c["R"], c["E"], c["F"], c["CAP"]
    T, TSL, DSL, NT, TC, NCH, TT = c["T"], c["TSL"], c["DSL"], c["NT"], c["TC"], c["NCH"], c["TT"]
    NC, CPB = c["NC"], c["CPB"]
    DT2 = DSL // 128
    FCH = TSL // TC
    TPC = TC // 128
    NKI = TSL // 16
    rg = [list(range(NC))]

    def inp(name, shape, dtype=F32):
        return nc.dram_tensor(name, list(shape), dtype, kind="ExternalInput")
    x_sl = inp("x_sl", [TSL, DM]); normw = inp("normw", [128, DM])
    winp = inp("winp", [DM, 2 * DSL]); convw = inp("convw", [DSL, 4])
    convb = inp("convb", [DSL, 1]); xpw = inp("xpw", [DSL, 96])
    dtw = inp("dtw", [R, DSL]); dtb = inp("dtb", [DSL, 1])
    alog = inp("alog", [128, NT]); dvec = inp("dvec", [DSL, 1])
    outw = inp("outw", [DI, DM]); wr = inp("wr", [DM, E])
    wi_e = inp("wi_e", [DM, F]); wo_e = inp("wo_e", [F, DM])
    meta = inp("meta", [128, 8])
    out_t = nc.dram_tensor("out", [TSL, DM], F32, kind="ExternalOutput")

    xn_loc = nc.dram_tensor("xn_loc", [TSL, DM], BF16)
    xn_all = nc.dram_tensor("xn_all", [NC, TSL, DM], BF16)
    ucb_h = nc.dram_tensor("ucb_h", [DSL, T], BF16)
    zs_h = nc.dram_tensor("zs_h", [DSL, T], BF16)
    xdp = nc.dram_tensor("xdp", [96, T], F32)
    xdr = nc.dram_tensor("xdr", [96, T], F32)
    y3_loc = nc.dram_tensor("y3_loc", [NC, DSL, TSL], BF16)
    y3_all = nc.dram_tensor("y3_all", [NC, DSL, TSL], BF16)
    x2_h = nc.dram_tensor("x2_h", [TSL, DM], F32)
    x2_loc = nc.dram_tensor("x2_loc", [TSL, DM], BF16)
    x2_all = nc.dram_tensor("x2_all", [NC, TSL, DM], BF16)
    rt_loc = nc.dram_tensor("rt_loc", [2, TSL], I16)
    rt_all = nc.dram_tensor("rt_all", [NC, 2, TSL], I16)
    cnt_loc = nc.dram_tensor("cnt_loc", [16, 1], F32)
    cnt_all = nc.dram_tensor("cnt_all", [NC, 16, 1], F32)
    oall_h = nc.dram_tensor("oall_h", [128, 1], F32)
    gidx_h = nc.dram_tensor("gidx_h", [2, TSL], F32)
    lsm_h = nc.dram_tensor("lsm_h", [2, 128, 8 * TSL // 128], I16)
    tok_h = nc.dram_tensor("tok_h", [2, CAP], I16)
    git_h = nc.dram_tensor("git_h", [16, 2 * (CAP // 16)], I16)
    wi_b = nc.dram_tensor("wi_b", [DM, F], BF16)
    wo_b = nc.dram_tensor("wo_b", [F, DM], BF16)
    gip_h = nc.dram_tensor("gip_h", [16, 2 * (TSL // 16)], I16)
    eo_loc = nc.dram_tensor("eo_loc", [CAP, 2, DM], BF16)
    if c.get("dbg"):
        dbg_lg = nc.dram_tensor("dbg_lg", [8, TSL], F32)
        dbg_pr = nc.dram_tensor("dbg_pr", [TSL, 8], F32)
        dbg_i8 = nc.dram_tensor("dbg_i8", [TSL, 8], F32)
    eo_all = nc.dram_tensor("eo_all", [NC, CAP, 2, DM], BF16)
    xnf = xn_all.ap().rearrange("a b c -> (a b) c")
    x2f = x2_all.ap().rearrange("a b c -> (a b) c")
    eof = eo_all.ap().rearrange("a b g d -> (a b g) d")
    rtf = rt_all.ap().rearrange("a b c -> (a b c)")

    P = np.zeros((128, 128), np.float32)
    for c2 in range(NC):
        for k2 in range(2):
            for cc_ in range(NC):
                for k in range(2):
                    if (c2 // 4) == (cc_ // 4) and (k2 < k or (k2 == k and c2 < cc_)):
                        for e2 in range(E):
                            P[16 * c2 + 8 * k2 + e2, 16 * cc_ + 8 * k + e2] = 1.0
    pref_t = nc.inline_tensor(P, name="prefix")
    e640_t = nc.inline_tensor(
        np.array([[float(CAP) * (r % 8)] for r in range(16)], np.float32), name="e640")
    NJ = TT
    ids = []
    for gg in range(2):
        a = np.zeros((1, 128 * 4 * 2 * NJ), np.int16)
        ix = 0
        for p in range(128):
            for cc_ in range(4):
                for k in range(2):
                    for j in range(NJ):
                        a[0, ix] = TSL * (4 * gg + cc_) + 128 * j + p + 1
                        ix += 1
        ids.append(nc.inline_tensor(a, name=f"ids{gg}"))

    SB_ALIGN = 64
    _base0 = nc.lookup_mloc(nc.alloc_sbuf_tensor("arena_mark", [128, 16], F32)).addr + 64
    _base0 = ((_base0 + SB_ALIGN - 1) // SB_ALIGN) * SB_ALIGN
    _cur = {"perm": _base0, "g": 0}
    _gbase = [None]

    def _mk_alloc(group):
        def a(name, shape, dtype):
            import math as _math
            nel = 1
            for s_ in shape[1:]:
                nel *= s_
            width = nel * mybir.dt.size(dtype)
            width = ((width + SB_ALIGN - 1) // SB_ALIGN) * SB_ALIGN
            if group == "perm":
                off = _cur["perm"]; _cur["perm"] += width
                assert _gbase[0] is None, "perm alloc after group start"
            else:
                if _gbase[0] is None:
                    _gbase[0] = _cur["perm"]
                off = _gbase[0] + _cur["g"]; _cur["g"] += width
            t = nc.alloc_sbuf_tensor_at(name, list(shape), dtype, offset=off)
            lim = 229000
            assert off + width <= lim, f"sbuf overflow {name}: {off+width} > {lim}"
            return t
        return a

    def _group_reset():
        _cur["g"] = 0

    sbP = _mk_alloc("perm")
    sb = _mk_alloc("g")
    consts = sbP("consts", [128, 4], F32)
    fil = sbP("fil", [128, 2], F32)
    A_col = sbP("A_col", [128, NT], F32)
    dtb_c = sbP("dtb_c", [128, DT2], F32)
    dvec_c = sbP("dvec_c", [128, DT2], F32)
    cb_c = sbP("cb_c", [128, DT2], F32)
    nwb = sbP("nwb", [128, DM], F32)
    erowb = sbP("erowb", [128, 8], F32)
    metab = sbP("metab", [128, 8], F32)
    idnf = sbP("idnf", [128, 128], F32)
    pref_s = sbP("pref_s", [128, 128], F32)
    e640_c = sbP("e640_c", [16, 1], F32)
    onesb = sbP("onesb", [128, 128], BF16)
    onesf = sbP("onesf", [128, max(TC, TSL // 8) if False else TC], F32)
    carry = sbP("carry", [128, NT], F32)
    gates_p = sbP("gates_p", [128, 2 * TT], F32)
    gidx_p = sbP("gidx_p", [128, 2 * NKI], I16)
    wr_st = sbP("wr_st", [128, 8 * 8], BF16)
    convw_s = sb("convw_s", [128, DT2 * 4], F32)
    wrep = sb("wrep", [128, DT2 * 4 * 128], BF16)
    win_st = sb("win_st", [128, 8 * 4 * 128], BF16)
    xpw_st = sb("xpw_st", [128, DT2 * 96], BF16)
    dtw_st = sb("dtw_st", [64, DT2 * 128], F32)
    dgw = sb("dgw", [128, DT2 * 4 * 128], BF16)
    rep_st = sb("rep_st", [128, NT * 128], BF16)
    nsm_st = sb("nsm_st", [128, NT * 128], BF16)
    bsel = sb("bsel", [96, 2 * 128], F32)

    pb = [nc.alloc_psum_tensor(f"pb{i}", [128, 512], F32) for i in range(6)]
    pbw = nc.alloc_psum_tensor("pbw", [128, 1024], F32)

    m = MK(nc)
    OP, DMA, CC = m.op, m.dma, m.cc

    def FIL():
        return OP("vector", lambda h: h.tensor_copy(fil[:, 0:1], fil[:, 1:2]))


    def _maybe_dump(lcl):
        fn = c.get("dump")
        if fn is None:
            return
        ap = fn(lcl)
        m.dma("gpsimd", out_t.ap().rearrange("a b -> (a b)")[0:ap.nelem()].rearrange(
            "(a b) -> a b", b=DM) if False else out_t[0:ap.shape[0], 0:ap.shape[1]], ap,
            after=[m.ops[-1]["id"]])
    tA = DMA("sync", A_col[:, :], alog[:, :])
    t_small = []
    for mm in range(DT2):
        t_small.append(DMA("sync", dtb_c[:, mm:mm + 1], dtb[mm * 128:(mm + 1) * 128, :]))
        t_small.append(DMA("sync", dvec_c[:, mm:mm + 1], dvec[mm * 128:(mm + 1) * 128, :]))
        t_small.append(DMA("sync", cb_c[:, mm:mm + 1], convb[mm * 128:(mm + 1) * 128, :]))
        t_small.append(DMA("sync", convw_s[:, mm * 4:(mm + 1) * 4], convw[mm * 128:(mm + 1) * 128, :]))
    t_small.append(DMA("sync", pref_s[:, :], pref_t[:, :]))
    t_small.append(DMA("sync", e640_c[:, :], e640_t[:, :]))
    t_nw = DMA("sync", nwb[:, :], normw[:, :])
    t_mt = DMA("sync", metab[:, :], meta[:, :])
    t_win = []
    for kk in range(8):
        for mm in range(4):
            t_win.append(DMA("gpsimd", win_st[:, (kk * 4 + mm) * 128:(kk * 4 + mm + 1) * 128],
                             winp[kk * 128:(kk + 1) * 128, mm * 128:(mm + 1) * 128]))
    t_xpw = [DMA("gpsimd", xpw_st[:, kk * 96:(kk + 1) * 96], xpw[kk * 128:(kk + 1) * 128, :])
             for kk in range(DT2)]
    t_dtw = [DMA("sync", dtw_st[:, mm * 128:(mm + 1) * 128], dtw[:, mm * 128:(mm + 1) * 128])
             for mm in range(DT2)]
    t_wr = [DMA("gpsimd", wr_st[:, kk * 8:(kk + 1) * 8], wr[kk * 128:(kk + 1) * 128, :])
            for kk in range(8)]
    t_wib = [DMA("gpsimd", wi_b[q * (DM // 4):(q + 1) * (DM // 4), :],
                 wi_e[q * (DM // 4):(q + 1) * (DM // 4), :], after=t_wr[-1:])
             for q in range(4)]
    t_wob = [DMA("gpsimd", wo_b[q * (F // 4):(q + 1) * (F // 4), :],
                 wo_e[q * (F // 4):(q + 1) * (F // 4), :], after=[t_wib[-1]])
             for q in range(4)]

    OP("vector", lambda h: h.memset(fil[:, :], 0.0))
    OP("vector", lambda h: h.memset(consts[:, 0:1], 1.0))
    OP("vector", lambda h: h.memset(consts[:, 1:2], 1e-5))
    OP("vector", lambda h: h.memset(consts[:, 2:3], -1.0))
    OP("vector", lambda h: h.memset(consts[:, 3:4], 0.0))
    t_ones = OP("vector", lambda h: h.memset(onesb[:, :], 1.0))
    t_onesf = OP("vector", lambda h: h.memset(onesf[:, :], 1.0))
    t_car0 = OP("vector", lambda h: h.memset(carry[:, :], 0.0))
    t_gip0 = OP("vector", lambda h: h.memset(gidx_p[:, :], 0))
    t_wrep = []
    for mm in range(DT2):
        for j in range(4):
            t_wrep.append(OP("vector", lambda h, mm=mm, j=j: h.tensor_copy(
                wrep[:, (mm * 4 + j) * 128:(mm * 4 + j + 1) * 128],
                convw_s[:, mm * 4 + j:mm * 4 + j + 1].broadcast_to([128, 128])),
                after=t_small))

    t_io = OP("gpsimd", lambda h: h.iota(erowb[:, :], [[1, 8]], base=0, channel_multiplier=0, allow_small_or_imprecise_dtypes=True))
    t_lib = OP("gpsimd", lambda h: h.load_library(library_config.mlp), after=[t_io])
    t_nwb = t_nw
    t_mb = t_mt
    t_idnf = OP("gpsimd", lambda h: h.affine_select(idnf[:, :], onesf[:, 0:128], [[-1, 128]],
                ALU.is_equal, 0.0, base=0, channel_multiplier=1), after=[t_onesf])
    t_idnb = t_idnf
    t_dgw = []
    for mm in range(DT2):
        for j in range(4):
            t_dgw.append(OP("gpsimd", lambda h, mm=mm, j=j: h.affine_select(
                dgw[:, (mm * 4 + j) * 128:(mm * 4 + j + 1) * 128],
                wrep[:, (mm * 4 + j) * 128:(mm * 4 + j + 1) * 128],
                [[-1, 128]], ALU.is_equal, 0.0, base=0, channel_multiplier=1),
                after=[t_wrep[mm * 4 + j]]))
    t_rep = []
    for i in range(NT):
        t_rep.append(OP("gpsimd", lambda h, i=i: h.affine_select(
            rep_st[:, i * 128:(i + 1) * 128].rearrange("p (a b) -> p a b", a=8),
            onesb[:, :].rearrange("p (a b) -> p a b", a=8),
            [[-1, 8], [0, 16]], ALU.is_equal, 0.0,
            base=-8 * (i % (NT // DT2)), channel_multiplier=1), after=[t_ones]))
    t_bsel = []
    for w_, bs in ((0, -64), (1, -80)):
        t_bsel.append(OP("gpsimd", lambda h, w=w_, b=bs: h.affine_select(
            bsel[:, w * 128:(w + 1) * 128].rearrange("p (a b) -> p a b", a=8),
            onesf[0:96, 0:128].rearrange("p (a b) -> p a b", a=8),
            [[0, 8], [-1, 16]], ALU.is_equal, 0.0, base=b, channel_multiplier=1),
            after=[t_onesf]))

    t_A1 = OP("scalar", lambda h: h.activation(A_col[:, :], A_col[:, :], AF.Exp), after=[tA])
    t_A = OP("scalar", lambda h: h.mul(A_col[:, :], A_col[:, :], -1.0))

    NTD0 = NT // DT2
    nsmtmp = sb("nsmtmp", [128, 128], BF16)
    t_nsm = []
    for i in range(NT):
        b0 = 16 * 8 * (i % NTD0)
        OP("gpsimd", lambda h, i=i, b0=b0: h.affine_select(
            nsmtmp[:, :], onesb[:, :], [[-16, 128]], ALU.is_ge, 0.0,
            base=b0, channel_multiplier=1), after=[t_ones])
        t_nsm.append(OP("gpsimd", lambda h, i=i, b0=b0: h.affine_select(
            nsm_st[:, i * 128:(i + 1) * 128], nsmtmp[:, :], [[16, 128]], ALU.is_ge, 0.0,
            base=15 - b0, channel_multiplier=-1)))

    t_initdone = m.join_all()
    m.fence(t_initdone)

    if c.get("upto") == "init":
        _maybe_dump(locals())
        t_end = m.join_all(); m.fence(t_end); m.finalize(); return nc, m, locals()
    xt0 = sb("xt0", [128, DM], F32)
    sqd = sb("sqd", [128, DM], F32)
    st0 = sb("st0", [128, 4], F32)
    xnb0 = sb("xnb0", [128, DM], BF16)
    t_st = None
    t_use = None
    t_ms_prev = None
    t_sts = []
    for i in range(TT):
        t_ld = DMA("sync", xt0[:, :], x_sl[i * 128:(i + 1) * 128, :],
                   after=[t_use] if t_use else [])
        t_sq = OP("scalar", lambda h: h.activation(sqd[:, :], xt0[:, :], AF.Square,
                  accum_out=st0[:, 0:1]), after=[t_ld, t_ms_prev])
        t_ms = OP("vector", lambda h: h.tensor_scalar(st0[:, 1:2], st0[:, 0:1],
                  1.0 / DM, 1e-5, ALU.mult, ALU.add), after=[t_sq])
        t_ms_prev = t_ms
        OP("vector", lambda h: h.tensor_copy(st0[:, 2:3], st0[:, 0:1]))
        t_iv = OP("vector", lambda h: h.reciprocal(st0[:, 2:3], st0[:, 1:2]))
        t_rt = OP("scalar", lambda h: h.activation(st0[:, 3:4], st0[:, 2:3], AF.Sqrt),
                  after=[t_iv])
        t_ap = OP("vector", lambda h: h.tensor_scalar(sqd[:, :], xt0[:, :], st0[:, 3:4],
                  None, ALU.mult), after=[t_rt])
        t_use = OP("vector", lambda h: h.tensor_tensor(xnb0[:, :], sqd[:, :], nwb[:, :],
                   ALU.mult), after=[t_nwb, t_st])
        t_st = DMA("sync", xn_loc[i * 128:(i + 1) * 128, :], xnb0[:, :], after=[t_use])
        t_sts.append(t_st)
    t_agxn = CC(lambda h: h.collective_compute(
        "AllGather", ALU.bypass, replica_groups=rg,
        ins=[xn_loc.ap().opt()], outs=[xn_all.ap().opt()]), after=t_sts)

    if c.get("upto") == "p0":
        _maybe_dump(locals())
        t_end = m.join_all(); m.fence(t_end); m.finalize(); return nc, m, locals()
    xnf_t = sb("xnf_t", [128, 2 * 8 * TC], BF16)
    u_sb = sb("u_sb", [128, DT2 * (TC + 4)], BF16)
    u_sbS = sb("u_sbS", [128, DT2 * (TC + 4)], BF16)
    tailb = sb("tailb", [128, DT2 * 3], BF16)
    ucf = sb("ucf", [128, DT2 * TC], F32)
    ucbt = sb("ucbt", [128, DT2 * TC], BF16)
    zsbt = sb("zsbt", [128, DT2 * TC], BF16)
    xps = sb("xps", [96, TC], F32)
    tk = dict(xn=[None] * 8, uev=[None, None], zev=[None, None], cvev=[None, None],
              ucb=[None, None], tails=[None, None], xpev=None, halo=[None, None],
              ipmm=[None, None, None, None], cvmm=[None, None], xpmm=None,
              zst=[None, None], ucst=[None, None], xpst=None)
    t_xp_sts = []
    ipmm_hist = {0: [], 1: []}
    P1SUB = c.get("p1sub")
    for k in range(NCH):
        xb = (k % 2) * 8 * TC
        for j in range(8):
            dep = [t_agxn] + ipmm_hist[k % 2]
            tk["xn"][j] = DMA("sync" if j % 2 == 0 else "scalar",
                              xnf_t[:, xb + j * TC:xb + (j + 1) * TC],
                              xnf[k * TC:(k + 1) * TC, j * 128:(j + 1) * 128],
                              after=dep, transpose=True)
        if P1SUB == "ld":
            continue
        for mm in range(DT2):
            off = mm * (TC + 4)
            dep = [t for t in tk["cvmm"] if t]
            if k % CPB == 0:
                tk["halo"][mm] = OP("vector", lambda h, off=off: h.memset(
                    u_sb[:, off + 1:off + 4], 0.0), after=dep)
            else:
                tk["halo"][mm] = OP("vector", lambda h, mm=mm, off=off: h.tensor_copy(
                    u_sb[:, off + 1:off + 4], tailb[:, mm * 3:(mm + 1) * 3]),
                    after=dep + [tk["tails"][mm]])
        for mm in range(4):
            bank = pb[mm % 2]
            dep = list(tk["xn"])
            prev_ev = tk["uev"][mm % 2] if mm < 2 else tk["zev"][mm % 2]
            if prev_ev:
                dep.append(prev_ev)
            if mm < 2:
                dep.append(tk["halo"][mm])

            def ipmm(h, mm=mm, bank=bank, xb=xb):
                r = None
                for kk in range(8):
                    r = h.matmul(bank[:, :TC], win_st[:, (kk * 4 + mm) * 128:(kk * 4 + mm + 1) * 128],
                                 xnf_t[:, xb + kk * TC:xb + (kk + 1) * TC], start=(kk == 0), stop=(kk == 7))
                return r
            tk["ipmm"][mm] = OP("tensor", ipmm, after=dep)
            if mm < 2:
                off = mm * (TC + 4)
                tk["uev"][mm] = OP("scalar", lambda h, off=off, bank=bank: h.copy(
                    u_sb[:, off + 4:off + 4 + TC], bank[:, :TC]), after=[tk["ipmm"][mm]])
            else:
                tk["zev"][mm - 2] = OP("scalar", lambda h, mm=mm, bank=bank: h.activation(
                    zsbt[:, (mm - 2) * TC:(mm - 1) * TC], bank[:, :TC], AF.Silu),
                    after=[tk["ipmm"][mm], tk["zst"][mm - 2]])
        ipmm_hist[k % 2] = [t for t in tk["ipmm"] if t]
        if P1SUB == "ip":
            continue
        for mm in range(DT2):
            off = mm * (TC + 4)
            t_shf = OP("vector", lambda h, mm=mm, off=off: h.tensor_copy(
                u_sbS[:, off:off + TC + 3], u_sb[:, off + 1:off + 4 + TC]),
                after=[tk["uev"][mm], tk["halo"][mm]] + ([tk["cvmm"][mm]] if tk["cvmm"][mm] else []))
            dep = [tk["uev"][mm], tk["halo"][mm], t_shf]
            if tk["cvev"][mm]:
                dep.append(tk["cvev"][mm])

            def cvmm(h, mm=mm, off=off):
                r = None
                for j in range(4):
                    if (1 + j) % 2 == 0:
                        mv = u_sb[:, off + 1 + j:off + 1 + j + TC]
                    else:
                        mv = u_sbS[:, off + j:off + j + TC]
                    r = h.matmul(pb[2][:, :TC], dgw[:, (mm * 4 + j) * 128:(mm * 4 + j + 1) * 128],
                                 mv, start=(j == 0), stop=(j == 3))
                return r
            tk["cvmm"][mm] = OP("tensor", cvmm, after=dep + t_dgw)
            tk["cvev"][mm] = OP("scalar", lambda h, mm=mm: h.activation(
                ucf[:, mm * TC:(mm + 1) * TC], pb[2][:, :TC], AF.Silu,
                bias=cb_c[:, mm:mm + 1]), after=[tk["cvmm"][mm]])
            tk["ucb"][mm] = OP("vector", lambda h, mm=mm: h.tensor_copy(
                ucbt[:, mm * TC:(mm + 1) * TC], ucf[:, mm * TC:(mm + 1) * TC]),
                after=[tk["cvev"][mm], tk["ucst"][mm], tk["xpmm"]])
            tk["tails"][mm] = OP("vector", lambda h, mm=mm, off=off: h.tensor_copy(
                tailb[:, mm * 3:(mm + 1) * 3], u_sb[:, off + TC + 1:off + TC + 4]),
                after=[tk["uev"][mm]])
        if P1SUB == "cv":
            continue
        dep = [tk["ucb"][0], tk["ucb"][1]]
        if tk["xpev"]:
            dep.append(tk["xpev"])

        def xpmm(h):
            r = None
            for kk in range(DT2):
                r = h.matmul(pb[3][0:96, :TC], xpw_st[:, kk * 96:(kk + 1) * 96],
                             ucbt[:, kk * TC:(kk + 1) * TC], start=(kk == 0), stop=(kk == DT2 - 1))
            return r
        tk["xpmm"] = OP("tensor", xpmm, after=dep)
        tk["xpev"] = OP("vector", lambda h: h.tensor_copy(xps[:, :], pb[3][0:96, :TC]),
                        after=[tk["xpmm"], tk["xpst"]])
        for mm in range(DT2):
            tk["zst"][mm] = DMA("sync", zs_h[mm * 128:(mm + 1) * 128, k * TC:(k + 1) * TC],
                zsbt[:, mm * TC:(mm + 1) * TC], after=[tk["zev"][mm]])
            tk["ucst"][mm] = DMA("sync", ucb_h[mm * 128:(mm + 1) * 128, k * TC:(k + 1) * TC],
                ucbt[:, mm * TC:(mm + 1) * TC], after=[tk["ucb"][mm]])
        tk["xpst"] = DMA("sync", xdp[:, k * TC:(k + 1) * TC], xps[:, :],
                         after=[tk["xpev"]])
        t_xp_sts.append(tk["xpst"])
    if P1SUB:
        _maybe_dump(locals())
        t_end = m.join_all(); m.fence(t_end); m.finalize(); return nc, m, locals()
    t_ar = CC(lambda h: h.collective_compute(
        "AllReduce", ALU.add, replica_groups=rg,
        ins=[xdp.ap().opt()], outs=[xdr.ap().opt()]), after=t_xp_sts)

    if c.get("upto") == "p1":
        _maybe_dump(locals())
        t_end = m.join_all(); m.fence(t_end); m.finalize(); return nc, m, locals()
    xdl = sb("xdl", [96, TC], F32)
    brep = sb("brep", [128, TC], F32)
    crep = sb("crep", [128, TC], F32)
    dt_s = sb("dt_s", [128, DT2 * TC], F32)
    v_s = sb("v_s", [128, DT2 * TC], BF16)
    ucb2 = sb("ucb2", [128, DT2 * TC], BF16)
    zsl2 = sb("zsl2", [128, DT2 * TC], BF16)
    tmp1 = sb("tmp1", [128, DT2 * TC], F32)
    tmp2 = sb("tmp2", [128, DT2 * TC], F32)
    dth_s = sb("dth_s", [128, DT2 * TC], BF16)
    dtl_s = sb("dtl_s", [128, DT2 * TC], BF16)
    dtf_t = sb("dtf_t", [128, TC], F32)
    dA_s = sb("dA_s", [128, 4 * TC], F32)
    dBu_s = sb("dBu_s", [128, 4 * TC], F32)
    h_s = sb("h_s", [128, 4 * TC], F32)
    hC_s = sb("hC_s", [128, 4 * TC], BF16)
    y2_s = sb("y2_s", [128, DT2 * TC], F32)
    y3_s = sb("y3_s", [128, DT2 * TC], BF16)
    NTD = NT // DT2
    p2 = dict(ld=[None] * 5, bc=None, dtp=[None, None], xb=[None, None], ab=[None, None],
              dtv=[None, None], dtl=[None, None], vv=[None, None],
              r1=[None] * NT, r2=[None] * NT,
              lastr1=[None, None], lastr2=[None, None],
              exp=[None] * NT, dbu=[None] * NT, scan=[None] * NT,
              hc=[None] * NT, nsumt=[None] * NT, y2=[None, None], y3=[None, None],
              y3st=[None, None], bcev=None)
    t_y3_sts = []
    P2L = c.get("p2sub", 9)
    t_libstd = OP("gpsimd", lambda h: h.load_library(library_config.standard),
                  after=[t_lib])
    for k in range(NCH):
        dep_b = [p2["dtp"][1]] if p2["dtp"][1] else []
        p2["ld"][0] = DMA("sync", xdl[:, :], xdr[:, k * TC:(k + 1) * TC],
                          after=[t_ar] + dep_b)
        for mm in range(DT2):
            p2["ld"][1 + mm] = DMA("sync", ucb2[:, mm * TC:(mm + 1) * TC],
                                   ucb_h[mm * 128:(mm + 1) * 128, k * TC:(k + 1) * TC],
                                   after=[p2["y2"][mm]] if p2["y2"][mm] else [])
            p2["ld"][3 + mm] = DMA("sync", zsl2[:, mm * TC:(mm + 1) * TC],
                                   zs_h[mm * 128:(mm + 1) * 128, k * TC:(k + 1) * TC],
                                   after=[p2["y3"][mm]] if p2["y3"][mm] else [])
        if P2L < 1:
            continue
        dep = [p2["ld"][0]] + ([p2["bcev"]] if p2["bcev"] else [])
        t_b = OP("tensor", lambda h: h.matmul(pbw[:, 0:TC], bsel[:, 0:128], xdl[:, :],
                 start=True, stop=True), after=dep + t_bsel)
        t_c = OP("tensor", lambda h: h.matmul(pbw[:, 512:512 + TC], bsel[:, 128:256], xdl[:, :],
                 start=True, stop=True), after=dep)
        t_bev = OP("scalar", lambda h: h.copy(brep[:, :], pbw[:, 0:TC]), after=[t_b])
        t_cev = OP("scalar", lambda h: h.copy(crep[:, :], pbw[:, 512:512 + TC]), after=[t_c])
        p2["bc"] = t_cev
        if P2L < 2:
            continue
        for mm in range(DT2):
            dep = [p2["ld"][0]]
            if mm == 0 and p2["y2"][0]:
                dep.append(p2["y2"][0])
            if mm == 1:
                dep.append(p2["xb"][0])
            p2["dtp"][mm] = OP("tensor", lambda h, mm=mm: h.matmul(
                pb[4][:, :TC], dtw_st[:, mm * 128:(mm + 1) * 128], xdl[0:64, :],
                start=True, stop=True), after=dep)
            sl_m = mm * TC
            p2["xb"][mm] = OP("vector", lambda h, sl_m=sl_m: h.tensor_scalar(
                tmp1[:, sl_m:sl_m + TC], pb[4][:, :TC], dtb_c[:, mm:mm + 1], None, ALU.add),
                after=[p2["dtp"][mm], p2["ab"][mm]])
            t_ab = OP("scalar", lambda h, sl_m=sl_m: h.activation(
                tmp2[:, sl_m:sl_m + TC], tmp1[:, sl_m:sl_m + TC], AF.Abs),
                after=[p2["xb"][mm], p2["dtv"][mm]])
            p2["ab"][mm] = t_ab
            t_ea = OP("scalar", lambda h, sl_m=sl_m: h.activation(
                tmp2[:, sl_m:sl_m + TC], tmp2[:, sl_m:sl_m + TC], AF.Exp,
                scale=consts[:, 2:3]))
            t_sp = OP("scalar", lambda h, sl_m=sl_m: h.activation(
                tmp2[:, sl_m:sl_m + TC], tmp2[:, sl_m:sl_m + TC], AF.Ln,
                bias=consts[:, 0:1]))
            t_rl = OP("vector", lambda h, sl_m=sl_m: h.tensor_scalar(
                tmp1[:, sl_m:sl_m + TC], tmp1[:, sl_m:sl_m + TC], 0.0,
                None, ALU.max), after=[t_ab])
            dep = [t_sp, t_rl]
            p2["dtv"][mm] = OP("vector", lambda h, mm=mm, sl_m=sl_m: h.tensor_tensor(
                dt_s[:, mm * TC:(mm + 1) * TC], tmp2[:, sl_m:sl_m + TC],
                tmp1[:, sl_m:sl_m + TC], ALU.add), after=dep)
            dep = [p2["dtv"][mm]]
            if p2["lastr1"][mm]:
                dep.append(p2["lastr1"][mm])
            t_dh = OP("vector", lambda h, mm=mm: h.tensor_copy(
                dth_s[:, mm * TC:(mm + 1) * TC], dt_s[:, mm * TC:(mm + 1) * TC]),
                after=dep)
            OP("vector", lambda h, mm=mm: h.tensor_copy(
                dtf_t[:, :], dth_s[:, mm * TC:(mm + 1) * TC]))
            OP("vector", lambda h, mm=mm: h.tensor_tensor(
                dtf_t[:, :], dt_s[:, mm * TC:(mm + 1) * TC], dtf_t[:, :], ALU.subtract))
            p2["dtl"][mm] = OP("vector", lambda h, mm=mm: h.tensor_copy(
                dtl_s[:, mm * TC:(mm + 1) * TC], dtf_t[:, :]))
            dep = [p2["ld"][1 + mm]]
            if p2["lastr2"][mm]:
                dep.append(p2["lastr2"][mm])
            p2["vv"][mm] = OP("vector", lambda h, mm=mm: h.tensor_tensor(
                v_s[:, mm * TC:(mm + 1) * TC], dt_s[:, mm * TC:(mm + 1) * TC],
                ucb2[:, mm * TC:(mm + 1) * TC], ALU.mult), after=dep)
        if P2L < 3:
            continue
        if k % CPB == 0:
            t_car = OP("vector", lambda h: h.memset(carry[:, :], 0.0))
        else:
            t_car = None
        def emit_rep(i):
            mm = i // NTD
            bnk_d = pb[0] if i % 2 == 0 else pb[1]
            bnk_v = pb[2] if i % 2 == 0 else pb[3]
            dep = [p2["dtl"][mm]]
            if i >= 2 and p2["exp"][i - 2]:
                dep.append(p2["exp"][i - 2])
            elif i < 2 and p2["exp"][NT - 2 + i]:
                dep.append(p2["exp"][NT - 2 + i])
            OP("tensor", lambda h, i=i, mm=mm, b=bnk_d: h.matmul(
                b[:, :TC], rep_st[:, i * 128:(i + 1) * 128],
                dth_s[:, mm * TC:(mm + 1) * TC], start=True, stop=False), after=dep)
            t_r1 = OP("tensor", lambda h, i=i, mm=mm, b=bnk_d: h.matmul(
                b[:, :TC], rep_st[:, i * 128:(i + 1) * 128],
                dtl_s[:, mm * TC:(mm + 1) * TC], start=False, stop=True))
            p2["r1"][i] = t_r1
            if i % NTD == NTD - 1:
                p2["lastr1"][mm] = t_r1
            dep = [p2["vv"][mm]]
            if i >= 2 and p2["dbu"][i - 2]:
                dep.append(p2["dbu"][i - 2])
            elif i < 2 and p2["dbu"][NT - 2 + i]:
                dep.append(p2["dbu"][NT - 2 + i])
            t_r2 = OP("tensor", lambda h, i=i, mm=mm, b=bnk_v: h.matmul(
                b[:, :TC], rep_st[:, i * 128:(i + 1) * 128],
                v_s[:, mm * TC:(mm + 1) * TC], start=True, stop=True), after=dep)
            p2["r2"][i] = t_r2
            if i % NTD == NTD - 1:
                p2["lastr2"][mm] = t_r2

        def emit_expdbu(i):
            sl = (i % 4) * TC
            bnk_d = pb[0] if i % 2 == 0 else pb[1]
            bnk_v = pb[2] if i % 2 == 0 else pb[3]
            dep = [p2["r1"][i], t_A]
            if i >= 4 and p2["scan"][i - 4]:
                dep.append(p2["scan"][i - 4])
            elif i < 4 and p2["scan"][NT - 4 + i]:
                dep.append(p2["scan"][NT - 4 + i])
            p2["exp"][i] = OP("scalar", lambda h, i=i, sl=sl, b=bnk_d: h.activation(
                dA_s[:, sl:sl + TC], b[:, :TC], AF.Exp, scale=A_col[:, i:i + 1]),
                after=dep)
            if P2L < 5:
                return
            dep = [p2["r2"][i], t_bev]
            if i >= 4 and p2["scan"][i - 4]:
                dep.append(p2["scan"][i - 4])
            elif i < 4 and p2["scan"][NT - 4 + i]:
                dep.append(p2["scan"][NT - 4 + i])
            p2["dbu"][i] = OP("vector", lambda h, i=i, sl=sl, b=bnk_v: h.tensor_tensor(
                dBu_s[:, sl:sl + TC], b[:, :TC], brep[:, :], ALU.mult), after=dep)

        def emit_scan(i):
            sl = (i % 4) * TC
            dep = [p2["exp"][i], p2["dbu"][i]]
            if t_car is not None:
                dep.append(t_car)
            if i >= 4 and p2["hc"][i - 4]:
                dep.append(p2["hc"][i - 4])
            elif i < 4 and p2["hc"][NT - 4 + i]:
                dep.append(p2["hc"][NT - 4 + i])
            p2["scan"][i] = OP("vector", lambda h, i=i, sl=sl: h.tensor_tensor_scan(
                h_s[:, sl:sl + TC], dA_s[:, sl:sl + TC], dBu_s[:, sl:sl + TC],
                carry[:, i:i + 1], ALU.mult, ALU.add), after=dep)

        def emit_hc(i):
            sl = (i % 4) * TC
            dep = [t_cev, p2["scan"][i]]
            if i >= 4 and p2["nsumt"][i - 4]:
                dep.append(p2["nsumt"][i - 4])
            elif i < 4 and p2["nsumt"][NT - 4 + i]:
                dep.append(p2["nsumt"][NT - 4 + i])
            p2["hc"][i] = OP("gpsimd", lambda h, i=i, sl=sl: h.tensor_tensor(
                hC_s[:, sl:sl + TC], h_s[:, sl:sl + TC], crep[:, :], ALU.mult), after=dep)
            OP("vector", lambda h, i=i, sl=sl: h.tensor_copy(
                carry[:, i:i + 1], h_s[:, sl + TC - 1:sl + TC]))

        def emit_nsumt(i):
            mm = i // NTD
            first = (i % NTD == 0)
            dep = [p2["hc"][i]]
            if first:
                dep.append(t_nsm[-1])
                if p2["y2"][mm]:
                    dep.append(p2["y2"][mm])
                if mm == 0:
                    dep.append(p2["xb"][1])
            p2["nsumt"][i] = OP("tensor", lambda h, i=i, mm=mm, first=first: h.matmul(
                pb[4 + mm][:, :TC], nsm_st[:, i * 128:(i + 1) * 128],
                hC_s[:, (i % 4) * TC:(i % 4) * TC + TC],
                start=first, stop=(i % NTD == NTD - 1)), after=dep)

        for t in range(NT + 4):
            if t < NT:
                emit_rep(t)
            if P2L >= 4 and 1 <= t <= NT:
                emit_expdbu(t - 1)
            if P2L >= 6 and 2 <= t <= NT + 1:
                emit_scan(t - 2)
            if P2L >= 7 and 3 <= t <= NT + 2:
                emit_hc(t - 3)
            if P2L >= 8 and 4 <= t <= NT + 3:
                emit_nsumt(t - 4)
        if P2L < 9:
            continue
        for mm in range(DT2):
            p2["y2"][mm] = OP("vector", lambda h, mm=mm: h.scalar_tensor_tensor(
                y2_s[:, mm * TC:(mm + 1) * TC], ucb2[:, mm * TC:(mm + 1) * TC],
                dvec_c[:, mm:mm + 1], pb[4 + mm][:, :TC], ALU.mult, ALU.add),
                after=[p2["nsumt"][(mm + 1) * NTD - 1], p2["ld"][1 + mm]])
            p2["y3"][mm] = OP("vector", lambda h, mm=mm: h.tensor_tensor(
                y3_s[:, mm * TC:(mm + 1) * TC], y2_s[:, mm * TC:(mm + 1) * TC],
                zsl2[:, mm * TC:(mm + 1) * TC], ALU.mult),
                after=[p2["ld"][3 + mm], p2["y3st"][mm]])
            _s = (k * TC) // TSL
            _c0 = (k * TC) % TSL
            p2["y3st"][mm] = DMA("sync", y3_loc[_s, mm * 128:(mm + 1) * 128, _c0:_c0 + TC],
                                 y3_s[:, mm * TC:(mm + 1) * TC], after=[p2["y3"][mm]])
            t_y3_sts.append(p2["y3st"][mm])
        p2["bcev"] = p2["hc"][NT - 1]
    from concourse.aot_env import allow_ici_all_to_all

    def _a2a_y3(h):
        with allow_ici_all_to_all():
            return h.collective_compute(
                "AllToAll", ALU.bypass, replica_groups=rg,
                ins=[y3_loc.ap().opt()], outs=[y3_all.ap().opt()])
    if P2L < 9:
        _maybe_dump(locals())
        t_end = m.join_all(); m.fence(t_end); m.finalize(); return nc, m, locals()
    t_agy3 = CC(_a2a_y3, after=t_y3_sts)

    if c.get("upto") == "p2":
        _maybe_dump(locals())
        t_end = m.join_all(); m.fence(t_end); m.finalize(); return nc, m, locals()
    t_g1end = m.join_all()
    m.fence(t_g1end)
    _group_reset()
    y3t = sb("y3t", [128, 16 * TC], BF16)
    outw_st = sb("outw_st", [128, 16 * 8 * 128], BF16)
    xmf = sb("xmf", [128, 8 * TC], F32)
    xr_sb = sb("xr_sb", [128, DM], F32)
    xt3 = sb("xt3", [128, DM], F32)
    st3 = sb("st3", [128, 6], F32)
    sq3 = sb("sq3", [128, DM], F32)
    x2t = sb("x2t", [128, DM], F32)
    x2b = sb("x2b", [128, DM], BF16)
    t_outw = []
    for kk in range(16):
        for mm2 in range(8):
            t_outw.append(DMA("gpsimd", outw_st[:, (kk * 8 + mm2) * 128:(kk * 8 + mm2 + 1) * 128],
                              outw[kk * 128:(kk + 1) * 128, mm2 * 128:(mm2 + 1) * 128],
                              after=[t_agy3] if kk == 0 and mm2 == 0 else []))
    pid_g = nc.partition_id()
    p3 = dict(mv=[None] * 16, op=[None] * 8, ev=[None] * 8, tr=[None] * 8,
              xr=None, s1=None, s2=None)
    t_x2_sts = []
    for j in range(FCH):
        for kk in range(16):
            dep = [t_agy3]
            if p3["op"][7]:
                dep.append(p3["op"][7])
            p3["mv"][kk] = DMA("sync", y3t[:, kk * TC:(kk + 1) * TC],
                               y3_all[kk // 2, (kk % 2) * 128:(kk % 2) * 128 + 128,
                                      j * TC:(j + 1) * TC],
                               after=dep)
        for mm2 in range(8):
            bank = pb[mm2 % 2]
            dep = list(p3["mv"]) + t_outw
            if p3["ev"][mm2 % 2 + 6] if False else False:
                pass
            if mm2 >= 2:
                dep.append(p3["ev"][mm2 - 2])
            elif p3["ev"][6 + mm2]:
                dep.append(p3["ev"][6 + mm2])

            def opmm(h, mm2=mm2, bank=bank):
                r2 = None
                for kk in range(16):
                    r2 = h.matmul(bank[:, :TC], outw_st[:, (kk * 8 + mm2) * 128:(kk * 8 + mm2 + 1) * 128],
                                  y3t[:, kk * TC:(kk + 1) * TC], start=(kk == 0), stop=(kk == 15))
                return r2
            p3["op"][mm2] = OP("tensor", opmm, after=dep)
            p3["ev"][mm2] = OP("scalar", lambda h, mm2=mm2, bank=bank: h.copy(
                xmf[:, mm2 * TC:(mm2 + 1) * TC], bank[:, :TC]), after=[p3["op"][mm2]])
        for tt in range(TPC):
            ti = j * TPC + tt
            xl_dep = [p3["xr"]] if p3["xr"] else []
            if c.get("p3pre") and p3["s1"]:
                xl_dep.append(p3["s1"])
            t_xl = DMA("sync", xt3[:, :], x_sl[ti * 128:(ti + 1) * 128, :],
                       after=xl_dep)
            trs = []
            for mm2 in range(8):
                dep = [p3["ev"][mm2]]
                if p3["xr"]:
                    dep.append(p3["xr"])
                trs.append(OP("tensor", lambda h, mm2=mm2, tt=tt: h.transpose(
                    pbw[:, mm2 * 128:(mm2 + 1) * 128],
                    xmf[:, mm2 * TC + tt * 128:mm2 * TC + tt * 128 + 128], idnb[:, :])
                    if False else h.transpose(
                    pbw[:, mm2 * 128:(mm2 + 1) * 128],
                    xmf[:, mm2 * TC + tt * 128:mm2 * TC + tt * 128 + 128], idnf[:, :]),
                    after=dep + [t_idnf]))
            p3["xr"] = OP("vector", lambda h: h.tensor_tensor(
                xr_sb[:, :], pbw[:, 0:DM], xt3[:, :], ALU.add), after=trs + [t_xl])
            t_sq = OP("scalar", lambda h: h.activation(sq3[:, :], xr_sb[:, :], AF.Square,
                      accum_out=st3[:, 0:1]), after=[p3["xr"]])
            t_mu = OP("vector", lambda h: h.reduce_sum(st3[:, 1:2], xr_sb[:, :], AX.X))
            t_a = OP("vector", lambda h: h.tensor_scalar(st3[:, 2:3], st3[:, 0:1],
                     1.0 / DM, 1e-5, ALU.mult, ALU.add), after=[t_sq])
            t_mu2 = OP("vector", lambda h: h.tensor_scalar(st3[:, 1:2], st3[:, 1:2],
                       1.0 / DM, None, ALU.mult))
            FIL()
            t_b2 = OP("vector", lambda h: h.tensor_tensor(st3[:, 3:4], st3[:, 1:2],
                      st3[:, 1:2], ALU.mult))
            FIL()
            t_va = OP("vector", lambda h: h.tensor_tensor(st3[:, 2:3], st3[:, 2:3],
                      st3[:, 3:4], ALU.subtract))
            t_sd = OP("scalar", lambda h: h.activation(st3[:, 4:5], st3[:, 2:3], AF.Sqrt),
                      after=[t_va])
            t_rs = OP("vector", lambda h: h.reciprocal(st3[:, 5:6], st3[:, 4:5]),
                      after=[t_sd])
            FIL()
            t_x2 = OP("vector", lambda h: h.tensor_scalar(x2t[:, :], xr_sb[:, :],
                      st3[:, 1:2], st3[:, 5:6], ALU.subtract, ALU.mult),
                      after=[p3["s1"]])
            t_x2b = OP("vector", lambda h: h.tensor_copy(x2b[:, :], x2t[:, :]),
                       after=[p3["s2"]])
            if c.get("p3pre"):
                t_s1 = DMA("sync", x2_h[ti * 128:(ti + 1) * 128, :], xr_sb[:, :],
                           after=[p3["xr"], t_x2])
            else:
                t_s1 = DMA("sync", x2_h[ti * 128:(ti + 1) * 128, :], x2t[:, :], after=[t_x2])
            t_s2 = DMA("sync", x2_loc[ti * 128:(ti + 1) * 128, :], x2b[:, :], after=[t_x2b])
            p3["s1"], p3["s2"] = t_s1, t_s2
            t_x2_sts += [t_s1, t_s2]
            if not c.get("moe", True):
                t_s3 = DMA("sync", out_t[ti * 128:(ti + 1) * 128, :], x2t[:, :],
                           after=[t_x2])
                p3["s1"] = t_s3
                t_x2_sts.append(t_s3)
    if not c.get("moe", True):
        t_end = m.join_all()
        m.fence(t_end)
        m.finalize()
        return nc, m, locals()
    t_agx2 = CC(lambda h: h.collective_compute(
        "AllGather", ALU.bypass, replica_groups=rg,
        ins=[x2_loc.ap().opt()], outs=[x2_all.ap().opt()]), after=t_x2_sts)

    if c.get("upto") == "p3":
        _maybe_dump(locals())
        t_end = m.join_all(); m.fence(t_end); m.finalize(); return nc, m, locals()
    x2fm = sb("x2fm", [128, 8 * TSL], BF16)
    lgf = sb("lgf", [8, TSL], F32)
    lt_sb = sb("lt_sb", [128, 8], F32)
    st4 = sb("st4", [128, 4], F32)
    pr_sb = sb("pr_sb", [128, 8], F32)
    m8_sb = sb("m8_sb", [128, 8], F32)
    i8_sb = sb("i8_sb", [128, 8], U32)
    if8 = sb("if8", [128, 8], F32)
    mtm = sb("mtm", [128, 16], F32)
    Mg = sb("Mg", [16, TSL], F32)
    ones16 = sb("ones16", [16, TSL], F32)
    posi = sb("posi", [16, TSL], F32)
    kp = sb("kp", [16, TSL], F32)
    slotM = sb("slotM", [16, TSL], F32)
    cnt_sb = sb("cnt_sb", [16, 1], F32)
    cnts128 = sb("cnts128", [128, 1], F32)
    of32 = sb("of32", [128, 1], F32)
    offs = sb("offs", [16, 1], F32)
    sltk = sb("sltk", [128, 2], F32)
    keepk = sb("keepk", [128, 2], F32)
    rowid = sb("rowid", [128, 2], F32)
    smk = sb("smk", [128, 2], F32)
    rt_sb = sb("rt_sb", [2, TSL], I16)
    gid_sb = sb("gid_sb", [2, TSL], F32)
    gi128 = sb("gi128", [128, 16], F32)
    NKIc = TSL // 16

    t_o16 = OP("vector", lambda h: h.memset(ones16[:, :], 1.0))
    x2t4 = sb("x2t4", [128, DM], F32)
    p3r = dict(tr=[None, None], ld=None, ev=[None] * 8, soft=None, mg=[None] * 100)
    t_lg_dep = []
    for ti in range(TT):
        t_l = DMA("sync", x2t4[:, :], x2_h[ti * 128:(ti + 1) * 128, :],
                  after=[p3r["tr"][0], p3r["tr"][1]] + t_x2_sts)
        for q in range(8):
            bnk = pb[2 + q % 2]
            dep = [t_l, t_idnf]
            prev_user = p3r["ev"][q - 2] if q >= 2 else p3r["ev"][q % 2 + 6]
            if prev_user:
                dep.append(prev_user)
            tq = OP("tensor", lambda h, q=q, bnk=bnk: h.transpose(
                bnk[:, 0:128], x2t4[:, q * 128:(q + 1) * 128], idnf[:, :]), after=dep)
            p3r["tr"][q % 2] = tq
            p3r["ev"][q] = OP("vector", lambda h, q=q, ti=ti, bnk=bnk: h.tensor_copy(
                x2fm[:, q * TSL + ti * 128:q * TSL + ti * 128 + 128], bnk[:, 0:128]),
                after=[tq])
    t_lg_dep = [t for t in p3r["ev"] if t]
    NFC = TSL // 512 if TSL >= 512 else 1
    FCW = min(512, TSL)
    t_lgev = []
    for fc in range(NFC):
        def lgmm(h, fc=fc):
            r2 = None
            for kk in range(8):
                r2 = h.matmul(pb[4][0:8, 0:FCW], wr_st[:, kk * 8:(kk + 1) * 8],
                              x2fm[:, kk * TSL + fc * FCW:kk * TSL + (fc + 1) * FCW],
                              start=(kk == 0), stop=(kk == 7))
            return r2
        dep = t_lg_dep + t_wr + (t_lgev[-1:] if t_lgev else [])
        t_mm = OP("tensor", lgmm, after=dep)
        t_lgev.append(OP("vector", lambda h, fc=fc: h.tensor_copy(
            lgf[:, fc * FCW:(fc + 1) * FCW], pb[4][0:8, 0:FCW]), after=[t_mm]))
    if c.get("dbg"):
        DMA("sync", dbg_lg[:, :], lgf[:, :], after=[t_lgev[-1]])
    t_mtr = None
    t_gsave = [None] * TT
    for ti in range(TT):
        dep = [t_lgev[-1]]
        if t_mtr:
            dep.append(t_mtr)
        t_tr = OP("tensor", lambda h, ti=ti: h.transpose(
            pb[5][0:128, 0:8], lgf[:, ti * 128:(ti + 1) * 128], idnf[0:8, 0:8]), after=dep)
        t_le = OP("vector", lambda h: h.tensor_copy(lt_sb[:, :], pb[5][0:128, 0:8]),
                  after=[t_tr])
        t_mtr = t_le
        FIL()
        t_rm = OP("vector", lambda h: h.reduce_max(st4[:, 0:1], lt_sb[:, :], AX.X))
        FIL()
        t_nr = OP("vector", lambda h: h.tensor_scalar(st4[:, 1:2], st4[:, 0:1], -1.0,
                  None, ALU.mult))
        t_ex = OP("scalar", lambda h: h.activation(pr_sb[:, :], lt_sb[:, :], AF.Exp,
                  bias=st4[:, 1:2]), after=[t_nr])
        t_sm = OP("vector", lambda h: h.reduce_sum(st4[:, 2:3], pr_sb[:, :], AX.X),
                  after=[t_ex])
        FIL()
        t_rc = OP("vector", lambda h: h.reciprocal(st4[:, 3:4], st4[:, 2:3]))
        FIL()
        t_pr = OP("vector", lambda h: h.tensor_scalar(pr_sb[:, :], pr_sb[:, :],
                  st4[:, 3:4], None, ALU.mult))
        FIL()
        t_m8 = OP("vector", lambda h: h.max(out=m8_sb[:, :], in_=pr_sb[:, :]))
        FIL()
        t_i8 = OP("vector", lambda h: h.max_index(i8_sb[:, :], m8_sb[:, :], pr_sb[:, :]))
        FIL()
        t_if = OP("vector", lambda h: h.tensor_copy(if8[:, :], i8_sb[:, :]))
        t_gs = OP("vector", lambda h, ti=ti: h.tensor_copy(
            gates_p[:, 2 * ti:2 * ti + 2], m8_sb[:, 0:2]))
        t_gsave[ti] = t_gs
        if c.get("dbg"):
            DMA("sync", dbg_pr[ti * 128:(ti + 1) * 128, :], pr_sb[:, :], after=[t_m8])
            DMA("sync", dbg_i8[ti * 128:(ti + 1) * 128, :], if8[:, :], after=[t_if])
        t_m0 = OP("vector", lambda h: h.tensor_tensor(
            mtm[:, 0:8], if8[:, 0:1].broadcast_to([128, 8]), erowb[:, :], ALU.is_equal),
            after=[t_mb])
        t_m1 = OP("vector", lambda h: h.tensor_tensor(
            mtm[:, 8:16], if8[:, 1:2].broadcast_to([128, 8]), erowb[:, :], ALU.is_equal))
        t_trm = OP("tensor", lambda h, ti=ti: h.transpose(
            pb[5][0:16, 0:128], mtm[:, :], idnf[:, :]), after=[t_m1, t_m0])
        t_mgev = OP("vector", lambda h, ti=ti: h.tensor_copy(
            Mg[:, ti * 128:(ti + 1) * 128], pb[5][0:16, 0:128]), after=[t_trm])
        p3r["mg"][ti] = t_mgev
        t_mtr = t_mgev
    t_cs = OP("vector", lambda h: h.tensor_tensor_scan(
        posi[:, :], ones16[:, :], Mg[:, :], 0.0, ALU.mult, ALU.add),
        after=[t_o16] + [p3r["mg"][t] for t in range(TT)])
    FIL()
    t_ct = OP("vector", lambda h: h.tensor_copy(cnt_sb[:, :], posi[:, TSL - 1:TSL]))
    t_cst = DMA("sync", cnt_loc[:, :], cnt_sb[:, :], after=[t_ct])
    t_agc = CC(lambda h: h.collective_compute(
        "AllGather", ALU.bypass, replica_groups=rg,
        ins=[cnt_loc.ap().opt()], outs=[cnt_all.ap().opt()]), after=[t_cst])
    t_lc = DMA("gpsimd", cnts128[:, :], cnt_all.ap().rearrange("a b c -> (a b) c"),
               after=[t_agc])
    t_off = OP("tensor", lambda h: h.matmul(pb[4][0:128, 0:1], pref_s[:, :],
               cnts128[:, :], start=True, stop=True), after=[t_lc, t_lgev[-1]])
    t_oev = OP("vector", lambda h: h.tensor_copy(of32[:, :], pb[4][0:128, 0:1]),
               after=[t_off])
    t_ost = DMA("sync", oall_h[:, :], of32[:, :], after=[t_oev])
    t_lof = DMA("gpsimd", offs[:, :], oall_h.ap()[bass.ts(pid_g, 16), :], after=[t_ost])
    t_om = OP("vector", lambda h: h.tensor_scalar(offs[:, :], offs[:, :], -1.0,
              None, ALU.add), after=[t_lof])
    FIL()
    t_pg = OP("vector", lambda h: h.tensor_scalar(posi[:, :], posi[:, :], offs[:, 0:1],
              None, ALU.add), after=[t_om, t_cs])
    t_kl = OP("vector", lambda h: h.tensor_single_scalar(kp[:, :], posi[:, :], float(CAP - 1), ALU.is_le))
    t_kp = OP("vector", lambda h: h.tensor_tensor(kp[:, :], kp[:, :], Mg[:, :], ALU.mult))
    t_sc = OP("vector", lambda h: h.tensor_scalar(slotM[:, :], posi[:, :], float(CAP - 1),
              e640_c[:, 0:1], ALU.min, ALU.add))
    t_sM = OP("vector", lambda h: h.tensor_tensor(slotM[:, :], slotM[:, :], Mg[:, :], ALU.mult))
    t_pb3u = None
    t_pb5u = None
    t_rtst = []
    for ti in range(TT):
        t_t1 = OP("tensor", lambda h, ti=ti: h.transpose(
            pb[3][0:128, 0:16], slotM[:, ti * 128:(ti + 1) * 128], idnf[0:16, 0:16]),
            after=[t_sM] + ([t_pb3u] if t_pb3u else []))
        t_t2 = OP("tensor", lambda h, ti=ti: h.transpose(
            pb[2][0:128, 0:16], kp[:, ti * 128:(ti + 1) * 128], idnf[0:16, 0:16]),
            after=[t_kp] + ([t_pb5u] if t_pb5u else []))
        t_r1 = OP("vector", lambda h: h.tensor_reduce(
            sltk[:, :], pb[3][0:128, 0:16].rearrange("p (a b) -> p a b", a=2),
            AX.X, ALU.add), after=[t_t1])
        t_pb3u = t_r1
        t_r2 = OP("vector", lambda h: h.tensor_reduce(
            keepk[:, :], pb[2][0:128, 0:16].rearrange("p (a b) -> p a b", a=2),
            AX.X, ALU.add), after=[t_t2])
        t_pb5u = t_r2
        FIL()
        t_gk = OP("vector", lambda h, ti=ti: h.tensor_tensor(
            gates_p[:, 2 * ti:2 * ti + 2], gates_p[:, 2 * ti:2 * ti + 2], keepk[:, :],
            ALU.mult), after=[t_r2, t_gsave[ti]])
        t_ri = OP("vector", lambda h: h.tensor_scalar(rowid[:, :], sltk[:, :], 2.0,
                  metab[:, 1:2], ALU.mult, ALU.add), after=[t_r1, t_mb])
        t_s1 = OP("vector", lambda h: h.tensor_scalar(smk[:, :], sltk[:, :], 1.0,
                  None, ALU.add))
        FIL()
        t_s2 = OP("vector", lambda h: h.tensor_tensor(smk[:, :], smk[:, :], keepk[:, :], ALU.mult))
        FIL()
        t_s3 = OP("vector", lambda h: h.tensor_scalar(smk[:, :], smk[:, :], -1.0,
                  None, ALU.add))
        t_t3 = OP("tensor", lambda h, ti=ti: h.transpose(
            pb[3][0:2, 256:384], smk[:, :], idnf[:, :]), after=[t_s3, t_r1])
        t_e3 = OP("vector", lambda h, ti=ti: h.tensor_copy(
            rt_sb[:, ti * 128:(ti + 1) * 128], pb[3][0:2, 256:384]), after=[t_t3])
        t_pb3u = t_e3
        t_t4 = OP("tensor", lambda h, ti=ti: h.transpose(
            pb[2][0:2, 256:384], rowid[:, :], idnf[:, :]), after=[t_ri, t_r2])
        t_e4 = OP("vector", lambda h, ti=ti: h.tensor_copy(
            gid_sb[:, ti * 128:(ti + 1) * 128], pb[2][0:2, 256:384]), after=[t_t4])
        t_pb5u = t_e4
    t_rtst = [DMA("sync", rt_loc[:, :], rt_sb[:, :], after=[t_pb3u])]
    t_gist = DMA("sync", gidx_h[:, :], gid_sb[:, :], after=[t_pb5u])
    t_agrt = CC(lambda h: h.collective_compute(
        "AllGather", ALU.bypass, replica_groups=rg,
        ins=[rt_loc.ap().opt()], outs=[rt_all.ap().opt()]), after=t_rtst)
    t_gi = []
    for k in range(2):
        t_l = DMA("sync", gi128[0:TSL // 16, :],
                  gidx_h[k:k + 1, :].rearrange("a (r f) -> (a r) f", f=16),
                  after=[t_gist] + t_gi[-1:])
        t_t = OP("tensor", lambda h, k=k: h.transpose(
            pb[2][0:16, 32:32 + TSL // 16], gi128[0:TSL // 16, :], idnf[0:TSL // 16, 0:TSL // 16]),
            after=[t_l, t_pb5u])
        t_e = OP("vector", lambda h, k=k: h.tensor_copy(
            gidx_p[0:16, k * NKIc:(k + 1) * NKIc], pb[2][0:16, 32:32 + TSL // 16]),
            after=[t_t])
        t_gi.append(t_e)
    t_gpsv = DMA("sync", gip_h[:, :], gidx_p[0:16, 0:2 * NKIc], after=t_gi)
    t_gprl = DMA("sync", gidx_p[16:32, 0:2 * NKIc], gip_h[:, :], after=[t_gpsv])
    t_gi.append(t_gprl)

    if c.get("upto") == "p3r":
        _maybe_dump(locals())
        t_end = m.join_all(); m.fence(t_end); m.finalize(); return nc, m, locals()
    t_g2end = m.join_all()
    m.fence(t_g2end)
    _group_reset()
    NCand = 8 * TSL
    C2 = NCand // 128
    rtq = sb("rtq", [C2, 128], I16)
    rtq32 = sb("rtq32", [C2, 128], F32)
    rtT = sb("rtT", [128, C2], F32)
    mvx = sb("mvx", [128, C2], F32)
    lsm16 = sb("lsm16", [128, C2], I16)
    NCH2 = NCand // 2
    idxt = sb("idxt", [16, NCH2], I16)
    ids_sb = sb("ids_sb", [16, NCH2], I16)
    tokt = sb("tokt", [16, 2 * 2 * CAP], I16)
    tokf = sb("tokf", [1, CAP], F32)
    toki = sb("toki", [1, CAP], I16)
    CB = CAP // 16
    tk40 = sb("tk40", [CB, 16], I16)
    tk40f = sb("tk40f", [CB, 16], F32)
    gi2 = sb("gi2", [128, 2 * CB], I16)
    xin_g = sb("xin_g", [128, 2 * (CAP // 128) * DM], BF16)
    xin32b = sb("xin32b", [128, 128], F32)
    xin_fm = sb("xin_fm", [128, 2 * 8 * CAP], BF16)
    wi_buf = sb("wi_buf", [128, 2 * 8 * 128], BF16)
    wo_sb = sb("wo_sb", [128, (F // 128) * 512], BF16)
    h1_sb = sb("h1_sb", [128, (F // 128) * CAP], BF16)
    eo_sb = sb("eo_sb", [128, (CAP // 128) * DM], BF16)
    SC5 = CAP // 128
    FK = F // 128

    t_mset = OP("vector", lambda h: h.memset(idxt[:, :], -1))
    t_mset2 = OP("vector", lambda h: h.memset(ids_sb[:, :], 0))
    t_mset3 = OP("vector", lambda h: h.memset(gi2[:, :], 0))
    t_idld = []
    t_lsm_dump = []
    lsv = sb("lsv", [128, C2], F32)
    for gg in range(2):
        t_l = DMA("sync", rtq[:, :],
                  rt_all[4 * gg:4 * gg + 4, :, :].rearrange("a b (r f) -> (a b r) f", f=128),
                  after=[t_agrt] + t_lsm_dump)
        t_c32 = OP("vector", lambda h: h.tensor_copy(rtq32[0:C2, 0:128], rtq[0:C2, 0:128]),
                   after=[t_l])
        t_tr = OP("tensor", lambda h: h.transpose(pb[4][0:128, 0:C2], rtq32[0:C2, 0:128],
                  idnf[0:C2, 0:C2]), after=[t_c32, t_oev])
        t_ev = OP("vector", lambda h: h.tensor_copy(rtT[:, :], pb[4][0:128, 0:C2]),
                  after=[t_tr])
        FIL()
        t_ls = OP("vector", lambda h: h.tensor_scalar(lsv[:, :], rtT[:, :],
                  metab[:, 3:4], None, ALU.subtract), after=[t_ev, t_mb])
        FIL()
        t_m1 = OP("vector", lambda h: h.tensor_single_scalar(mvx[:, :], lsv[:, :], 0.0, ALU.is_ge))
        t_m2 = OP("vector", lambda h: h.tensor_single_scalar(rtT[:, :], lsv[:, :],
                  float(CAP - 1), ALU.is_le))
        FIL()
        t_m3 = OP("vector", lambda h: h.tensor_tensor(mvx[:, :], mvx[:, :], rtT[:, :], ALU.mult))
        t_a1 = OP("vector", lambda h: h.tensor_scalar(lsv[:, :], lsv[:, :], 1.0, None, ALU.add))
        FIL()
        t_a2 = OP("vector", lambda h: h.tensor_tensor(lsv[:, :], lsv[:, :], mvx[:, :], ALU.mult))
        FIL()
        t_a3 = OP("vector", lambda h: h.tensor_scalar(lsv[:, :], lsv[:, :], -1.0, None, ALU.add))
        FIL()
        t_ci = OP("vector", lambda h: h.tensor_copy(lsm16[:, :], lsv[:, :]))
        t_lsm_dump.append(DMA("sync", lsm_h[gg, :, :], lsm16[:, :], after=[t_ci]))
    t_libls = OP("gpsimd", lambda h: h.load_library(library_config.local_scatter),
                 after=t_lsm_dump + t_gi + [t_agx2])
    t_scats = []
    t_scprev = None
    for gg in range(2):
        halves = []
        for hv in range(2):
            t_li = DMA("gpsimd", idxt[0:1, 0:NCH2],
                       lsm_h[gg, :, :].rearrange("a b -> (a b)")
                       .rearrange("(a b) -> a b", a=2)[hv:hv + 1, :],
                       after=[t_libls, t_mset] + ([t_scprev] if t_scprev else []))
            t_ld = DMA("gpsimd", ids_sb[0:1, 0:NCH2], ids[gg][:, hv * NCH2:(hv + 1) * NCH2],
                       after=[t_libls, t_mset2] + ([t_scprev] if t_scprev else []))
            t_sc = OP("gpsimd", lambda h, gg=gg, hv=hv: h.local_scatter(
                tokt[:, (2 * gg + hv) * CAP:(2 * gg + hv + 1) * CAP],
                ids_sb[:, :], idxt[:, :], 16, CAP, NCH2),
                after=[t_li, t_ld])
            t_scprev = t_sc
            halves.append(t_sc)
        t_mrg = OP("vector", lambda h, gg=gg: h.tensor_tensor(
            tokt[0:1, 2 * gg * CAP:(2 * gg) * CAP + CAP],
            tokt[0:1, 2 * gg * CAP:(2 * gg) * CAP + CAP],
            tokt[0:1, (2 * gg + 1) * CAP:(2 * gg + 1) * CAP + CAP], ALU.max),
            after=halves)
        t_scats.append(t_mrg)
    t_libm = OP("gpsimd", lambda h: h.load_library(library_config.mlp), after=t_scats)
    t_eo_sts = []
    t_h1ev = [None, None]
    t_eoev = [None, None]
    t_xinu = None
    t_gi2e = None
    t_prev_eost = None
    t_wolast = [None]
    for gg in range(2):
        t_tf = OP("vector", lambda h, gg=gg: h.tensor_copy(
            tokf[:, :], tokt[0:1, 2 * gg * CAP:2 * gg * CAP + CAP]), after=[t_scats[gg], t_gi2e])
        t_tf2 = OP("vector", lambda h: h.tensor_scalar(tokf[:, :], tokf[:, :], -1.0, None, ALU.add))
        t_tf3 = OP("vector", lambda h: h.tensor_scalar(tokf[:, :], tokf[:, :], 0.0, None, ALU.max))
        t_ti16 = OP("vector", lambda h: h.tensor_copy(toki[:, :], tokf[:, :]))
        t_ts = DMA("sync", tok_h[gg:gg + 1, :], toki[:, :], after=[t_ti16])
        t_tl = DMA("sync", tk40[:, :],
                   tok_h[gg:gg + 1, :].rearrange("a (r f) -> (a r) f", f=16), after=[t_ts])
        t_t4f = OP("vector", lambda h: h.tensor_copy(tk40f[:, :], tk40[:, :]), after=[t_tl])
        t_t4t = OP("tensor", lambda h, gg=gg: h.transpose(pb[5][0:16, 0:CB], tk40f[:, :],
                   idnf[0:CB, 0:CB]), after=[t_t4f])
        t_gi2e = OP("vector", lambda h, gg=gg: h.tensor_copy(gi2[0:16, gg * CB:(gg + 1) * CB],
                    pb[5][0:16, 0:CB]), after=[t_t4t, t_mset3])
        t_g2sv = DMA("sync", git_h[:, gg * CB:(gg + 1) * CB],
                     gi2[0:16, gg * CB:(gg + 1) * CB], after=[t_gi2e])
        t_gi2e = DMA("sync", gi2[16:32, gg * CB:(gg + 1) * CB],
                     git_h[:, gg * CB:(gg + 1) * CB], after=[t_g2sv])
        xo_g = (gg % 2) * SC5 * DM
        xo_f = (gg % 2) * 8 * CAP
        t_g = m.dmaop("gpsimd", lambda h, gg=gg, xo_g=xo_g: h.dma_gather(
            out_ap=xin_g[:, xo_g:xo_g + SC5 * DM].rearrange("p (a d) -> p a d", a=SC5),
            in_ap=x2f, idxs_ap=gi2[:, gg * CB:(gg + 1) * CB],
            num_idxs=CAP, num_idxs_reg=CAP, elem_size=DM),
            after=[t_gi2e, t_libm, t_agx2])
        t_fmev = []
        t_trp = None
        t_cb = None
        for sb5 in range(SC5):
            for dmb in range(8):
                t_cb = OP("vector", lambda h, sb5=sb5, dmb=dmb, xo_g=xo_g: h.tensor_copy(
                    xin32b[:, :], xin_g[:, xo_g + sb5 * DM + dmb * 128:xo_g + sb5 * DM + (dmb + 1) * 128]),
                    after=[t_g] + ([t_tp] if t_fmev else []))
                t_tp = OP("tensor", lambda h: h.transpose(
                    pb[4][0:128, 128:256], xin32b[:, :], idnf[:, :]),
                    after=[t_cb] + ([t_trp] if t_trp else []))
                t_trp = OP("vector", lambda h, sb5=sb5, dmb=dmb, xo_f=xo_f: h.tensor_copy(
                    xin_fm[:, xo_f + dmb * CAP + sb5 * 128:xo_f + dmb * CAP + (sb5 + 1) * 128],
                    pb[4][0:128, 128:256]), after=[t_tp])
                t_fmev.append(t_trp)
        t_wi = [None, None]
        t_h1 = None
        slot_chunks = []
        off = 0
        while off < CAP:
            w = min(512, CAP - off)
            slot_chunks.append((off, w))
            off += w
        for fM in range(FK * F // F if False else (F // 128)):
            buf = fM % 2
            deps = [t_fmev[-1]] if fM == 0 else []
            if t_wi[buf]:
                deps.append(t_wi[buf])
            t_wis = [DMA("sync",
                         wi_buf[:, (buf * 8) * 128:(buf + 1) * 8 * 128]
                         .rearrange("p (a c) -> p a c", a=8),
                         wi_b.rearrange("(a p) c -> p a c", p=128)[:, :, fM * 128:(fM + 1) * 128],
                         after=deps + ([t_h1] if t_h1 else []) + t_wib)]
            t_wi[buf] = t_wis[-1]
            for (so, sw) in slot_chunks:
                bank = pb[0] if (so == 0) else pb[1]

                def h1mm(h, fM=fM, buf=buf, so=so, sw=sw, bank=bank, xo_f=xo_f):
                    r2 = None
                    for kk in range(8):
                        r2 = h.matmul(bank[:, 0:sw],
                                      wi_buf[:, (buf * 8 + kk) * 128:(buf * 8 + kk + 1) * 128],
                                      xin_fm[:, xo_f + kk * CAP + so:xo_f + kk * CAP + so + sw],
                                      start=(kk == 0), stop=(kk == 7))
                    return r2
                dep2 = t_wis + [t_fmev[-1]]
                if t_prev_eost:
                    dep2.append(t_prev_eost)
                if t_h1ev[0 if so == 0 else 1]:
                    dep2.append(t_h1ev[0 if so == 0 else 1])
                t_mm = OP("tensor", h1mm, after=dep2)
                t_h1ev[0 if so == 0 else 1] = OP("scalar",
                    lambda h, fM=fM, so=so, sw=sw, bank=bank: h.activation(
                        h1_sb[:, fM * CAP + so:fM * CAP + so + sw], bank[:, 0:sw], AF.Relu),
                    after=[t_mm])
                t_h1 = t_h1ev[0 if so == 0 else 1]
        for dmc in range(2):
            t_woh = []
            woh_dep = [t_h1ev[0], t_h1ev[1]]
            if t_wolast[0]:
                woh_dep.append(t_wolast[0])
            t_woh.append(DMA("sync",
                             wo_sb[:, :].rearrange("p (a c) -> p a c", a=FK),
                             wo_b.rearrange("(a p) c -> p a c", p=128)[:, :, dmc * 512:(dmc + 1) * 512],
                             after=woh_dep + t_wob))
            for sM in range(SC5):
                bank = pb[2] if sM % 2 == 0 else pb[3]

                def eomm(h, sM=sM, dmc=dmc, bank=bank):
                    r2 = None
                    for fK in range(FK):
                        r2 = h.matmul(bank[:, 0:512],
                                      h1_sb[:, fK * CAP + sM * 128:fK * CAP + (sM + 1) * 128],
                                      wo_sb[:, fK * 512:(fK + 1) * 512],
                                      start=(fK == 0), stop=(fK == FK - 1))
                    return r2
                dep = [t_h1ev[0], t_h1ev[1]] + t_woh
                if t_prev_eost:
                    dep.append(t_prev_eost)
                if t_eoev[sM % 2]:
                    dep.append(t_eoev[sM % 2])
                t_mm = OP("tensor", eomm, after=dep)
                t_wolast[0] = t_mm
                t_eoev[sM % 2] = OP("vector", lambda h, sM=sM, dmc=dmc, bank=bank: h.tensor_copy(
                    eo_sb[:, sM * DM + dmc * 512:sM * DM + (dmc + 1) * 512], bank[:, 0:512]),
                    after=[t_mm])
        t_eost = DMA("sync",
                     eo_loc.ap().rearrange("c g d -> c (g d)")[:, gg * DM:(gg + 1) * DM]
                     .rearrange("(a b) d -> b a d", b=128),
                     eo_sb[:, :].rearrange("p (a d) -> p a d", a=SC5),
                     after=[t_eoev[0], t_eoev[1]])
        t_eo_sts.append(t_eost)
        t_xinu = t_h1
        t_prev_eost = t_eost
    t_ageo = CC(lambda h: h.collective_compute(
        "AllGather", ALU.bypass, replica_groups=rg,
        ins=[eo_loc.ap().opt()], outs=[eo_all.ap().opt()]), after=t_eo_sts)

    if c.get("upto") == "p4":
        _maybe_dump(locals())
        t_end = m.join_all(); m.fence(t_end); m.finalize(); return nc, m, locals()
    g5 = sb("g5", [128, 2 * DM], BF16)
    x2t5 = sb("x2t5", [128, DM], F32)
    yt5 = sb("yt5", [128, DM], F32)
    x3t = sb("x3t", [128, DM], F32)
    sq5 = sb("sq5", [128, DM], F32)
    st5 = sb("st5", [128, 6], F32)
    ot5 = sb("ot5", [128, DM], F32)
    p5 = dict(s=None, x2=None)
    t_c2p = None
    for ti in range(TT):
        t_g5 = []
        for k in range(2):
            t_g5.append(m.dmaop("gpsimd", lambda h, k=k, ti=ti: h.dma_gather(
                out_ap=g5[:, k * DM:(k + 1) * DM].rearrange("p (a d) -> p a d", a=1),
                in_ap=eof, idxs_ap=gidx_p[:, k * NKIc + 8 * ti:k * NKIc + 8 * ti + 8],
                num_idxs=128, num_idxs_reg=128, elem_size=DM),
                after=[t_ageo] + t_gi + ([t_c2p] if t_c2p else [])))
        t_xl = DMA("sync", x2t5[:, :], x2_h[ti * 128:(ti + 1) * 128, :],
                   after=([p5["x2"]] if p5["x2"] else []) + t_x2_sts)
        t_c1 = OP("vector", lambda h, ti=ti: h.tensor_scalar(
            yt5[:, :], g5[:, 0:DM], gates_p[:, 2 * ti:2 * ti + 1],
            None, ALU.mult), after=t_g5)
        t_c2 = OP("vector", lambda h, ti=ti: h.scalar_tensor_tensor(
            yt5[:, :], g5[:, DM:2 * DM],
            gates_p[:, 2 * ti + 1:2 * ti + 2], yt5[:, :], ALU.mult, ALU.add))
        t_c2p = t_c2
        t_x3 = OP("vector", lambda h: h.tensor_tensor(x3t[:, :], yt5[:, :], x2t5[:, :],
                  ALU.add), after=[t_xl])
        p5["x2"] = t_x3
        t_sq = OP("scalar", lambda h: h.activation(sq5[:, :], x3t[:, :], AF.Square,
                  accum_out=st5[:, 0:1]), after=[t_x3])
        t_mu = OP("vector", lambda h: h.reduce_sum(st5[:, 1:2], x3t[:, :], AX.X))
        t_a = OP("vector", lambda h: h.tensor_scalar(st5[:, 2:3], st5[:, 0:1],
                 1.0 / DM, 1e-5, ALU.mult, ALU.add), after=[t_sq])
        t_mu2 = OP("vector", lambda h: h.tensor_scalar(st5[:, 1:2], st5[:, 1:2],
                   1.0 / DM, None, ALU.mult))
        FIL()
        t_b2 = OP("vector", lambda h: h.tensor_tensor(st5[:, 3:4], st5[:, 1:2],
                  st5[:, 1:2], ALU.mult))
        FIL()
        t_va = OP("vector", lambda h: h.tensor_tensor(st5[:, 2:3], st5[:, 2:3],
                  st5[:, 3:4], ALU.subtract))
        t_sd = OP("scalar", lambda h: h.activation(st5[:, 4:5], st5[:, 2:3], AF.Sqrt),
                  after=[t_va])
        t_rs = OP("vector", lambda h: h.reciprocal(st5[:, 5:6], st5[:, 4:5]),
                  after=[t_sd])
        FIL()
        t_o = OP("vector", lambda h: h.tensor_scalar(ot5[:, :], x3t[:, :],
                 st5[:, 1:2], st5[:, 5:6], ALU.subtract, ALU.mult),
                 after=[p5["s"]])
        p5["s"] = DMA("sync", out_t[ti * 128:(ti + 1) * 128, :], ot5[:, :], after=[t_o])
    _maybe_dump(locals())
    t_end = m.join_all()
    m.fence(t_end)

    m.finalize()
    return nc, m, locals()


def shard_inputs(c, inputs):
    DM, DI, NT = c["DM"], c["DI"], c["NT"]
    TSL, DSL = c["TSL"], c["DSL"]
    x = np.asarray(inputs["L0_attn_x"], np.float32).reshape(c["T"], DM)
    ipw = np.asarray(inputs["L0_attn_in_proj_w"], np.float32)
    in_maps = []
    for cc in range(c["NC"]):
        d0, d1 = DSL * cc, DSL * (cc + 1)
        al = np.asarray(inputs["L0_attn_A_log"], np.float32)[d0:d1].reshape(-1)
        al = np.ascontiguousarray(al.reshape(NT, 128).T)
        mt = np.zeros((1, 8), np.float32)
        mt[0, 0] = cc; mt[0, 1] = cc // 4; mt[0, 2] = (cc // 4) * (c["E"] * c["CAP"])
        mt[0, 3] = c["CAP"] * cc
        mt = np.ascontiguousarray(np.broadcast_to(mt, (128, 8)))
        m_ = {
            "x_sl": np.ascontiguousarray(x[TSL * cc:TSL * (cc + 1)]),
            "normw": np.ascontiguousarray(np.broadcast_to(
                np.asarray(inputs["L0_attn_norm_w"], np.float32).reshape(1, DM), (128, DM))),
            "winp": np.ascontiguousarray(np.concatenate(
                [ipw[:, d0:d1], ipw[:, DI + d0:DI + d1]], axis=1)),
            "convw": np.ascontiguousarray(np.asarray(inputs["L0_attn_conv_w"], np.float32)[d0:d1]),
            "convb": np.ascontiguousarray(np.asarray(inputs["L0_attn_conv_b"], np.float32)[d0:d1].reshape(-1, 1)),
            "xpw": np.ascontiguousarray(np.asarray(inputs["L0_attn_x_proj_w"], np.float32)[d0:d1]),
            "dtw": np.ascontiguousarray(np.asarray(inputs["L0_attn_dt_proj_w"], np.float32)[:, d0:d1]),
            "dtb": np.ascontiguousarray(np.asarray(inputs["L0_attn_dt_proj_b"], np.float32)[d0:d1].reshape(-1, 1)),
            "alog": np.ascontiguousarray(al),
            "dvec": np.ascontiguousarray(np.asarray(inputs["L0_attn_D"], np.float32)[d0:d1].reshape(-1, 1)),
            "outw": np.ascontiguousarray(np.asarray(inputs["L0_attn_out_proj_w"], np.float32)),
            "wr": np.ascontiguousarray(np.asarray(inputs["L0_ffn_w_router"], np.float32)),
            "wi_e": np.ascontiguousarray(np.asarray(inputs["L0_ffn_wi"], np.float32)[cc]),
            "wo_e": np.ascontiguousarray(np.asarray(inputs["L0_ffn_wo"], np.float32)[cc]),
            "meta": mt,
        }
        in_maps.append(m_)
    return in_maps


def assemble(c, outs):
    return np.concatenate(outs, axis=0).reshape(c["B"], c["L"], c["DM"])


from concourse.bass_utils import run_bass_kernel_spmd


def kernel(**inputs):
    c = make_cfg()
    nc, m, env = build_nc(c)
    in_maps = shard_inputs(c, inputs)
    res = run_bass_kernel_spmd(nc, in_maps, core_ids=list(range(c["NC"])))
    outs = [np.asarray(r["out"], np.float32) for r in res.results]
    return assemble(c, outs).astype(np.float32)


# version 10 (final)
# speedup vs baseline: 1.0284x; 1.0284x over previous
import numpy as np
import concourse.bass as bass
import concourse.mybir as mybir
from concourse import library_config
from concourse.library_overlay import lower_extended_insts

F32 = mybir.dt.float32
BF16 = mybir.dt.bfloat16
I16 = mybir.dt.int16
U32 = mybir.dt.uint32
ALU = mybir.AluOpType
AF = mybir.ActivationFunctionType
AX = mybir.AxisListType
ENGS = ("sync", "tensor", "scalar", "vector", "gpsimd")


def make_cfg(L=8192, TC=512, moe=True, cap=640):
    c = {}
    c["moe"] = moe
    c["B"] = 2; c["L"] = L; c["DM"] = 1024; c["DI"] = 2048; c["N"] = 16
    c["R"] = 64; c["E"] = 8; c["F"] = 4096; c["CAP"] = cap; c["NC"] = 8
    c["T"] = c["B"] * L
    c["TSL"] = c["T"] // c["NC"]
    c["DSL"] = c["DI"] // c["NC"]
    c["NT"] = c["DSL"] * c["N"] // 128
    c["TC"] = TC
    c["NCH"] = c["T"] // TC
    c["CPB"] = L // TC
    c["TT"] = c["TSL"] // 128
    assert c["TSL"] % 128 == 0 and L % TC == 0 and c["TSL"] % TC == 0
    return c


class MK:
    def __init__(self, nc):
        self.nc = nc
        self.ops = []
        self.used = set()
        self.esem = {e: nc.alloc_semaphore("es_" + e) for e in ENGS}
        self.dsem = {e: nc.alloc_semaphore("ds_" + e) for e in ("sync", "scalar", "gpsimd")}

    def _rec(self, eng, fn, after, kind):
        tid = len(self.ops)
        after = [a for a in after if a is not None]
        if kind in ("d", "x"):
            self.used.add(tid)
        import traceback as _tb
        st = _tb.extract_stack()
        lines = [f.lineno for f in st if f.filename.endswith("mk.py") or "kernel" in f.filename]
        self.ops.append(dict(id=tid, eng=eng, fn=fn, after=list(after), kind=kind,
                             src=str(lines[-3:-1])))
        for a in after:
            ao = self.ops[a]
            if not (ao["kind"] == "c" and ao["eng"] == eng):
                self.used.add(a)
        return tid

    def op(self, eng, fn, after=()):
        return self._rec(eng, fn, after, "c")

    def dma(self, eng, dst, src, after=(), transpose=False):
        def fn(h, d=dst, s=src, t=transpose):
            if t:
                return h.dma_start(out=d, in_=s, transpose=True)
            return h.dma_start(out=d, in_=s)
        return self._rec(eng, fn, after, "d")

    def dmaop(self, eng, fn, after=()):
        return self._rec(eng, fn, after, "d")

    def cc(self, fn, after=()):
        return self._rec("gpsimd", fn, after, "x")

    def join_all(self, eng="gpsimd"):
        last = {}
        for o in self.ops:
            last[o["eng"]] = o["id"]
        afters = [v for k, v in last.items()]
        return self.op(eng, lambda h: h.nop(), after=afters)

    def fence(self, tok):
        for e in ENGS:
            self.op(e, lambda h: h.nop(), after=[tok])

    def finalize(self, name="prog"):
        nc = self.nc
        cnt = {}
        val = {}
        for o in self.ops:
            if o["id"] in self.used:
                if o["kind"] == "c":
                    sem, inc = self.esem[o["eng"]], 1
                elif o["kind"] == "d":
                    sem, inc = self.dsem[o["eng"]], 16
                else:
                    sem, inc = self.esem[o["eng"]], 1
                key = sem.name
                cnt[key] = cnt.get(key, 0) + inc
                val[o["id"]] = (sem, cnt[key], inc)
        per_eng = {e: [o for o in self.ops if o["eng"] == e] for e in ENGS}
        used = self.used

        def make_stream(eng):
            ops = per_eng[eng]
            def stream(h):
                lastw = {}
                for o in ops:
                    for a in o["after"]:
                        ao = self.ops[a]
                        if ao["kind"] == "c" and ao["eng"] == eng:
                            continue
                        sem, v, _ = val[a]
                        if lastw.get(sem.name, -1) >= v:
                            continue
                        h.wait_ge(sem, v)
                        lastw[sem.name] = v
                    try:
                        inst = o["fn"](h)
                    except Exception as ex:
                        raise RuntimeError(f"emit failed op={o['id']} eng={eng} kind={o['kind']} src={o.get('src')}") from ex
                    if o["id"] in used:
                        sem, v, inc = val[o["id"]]
                        assert inst is not None, f"op {o['id']} on {eng} returned None but is awaited"
                        inst.then_inc(sem, inc)
            return stream

        with nc.Block(name) as blk:
            for e in ENGS:
                if per_eng[e]:
                    getattr(blk, e)(make_stream(e))
        lower_extended_insts(nc)


def build_nc(c):
    nc = bass.Bass("TRN2", target_bir_lowering=False, detect_race_conditions=False)
    DM, DI, N, R, E, F, CAP = c["DM"], c["DI"], c["N"], c["R"], c["E"], c["F"], c["CAP"]
    T, TSL, DSL, NT, TC, NCH, TT = c["T"], c["TSL"], c["DSL"], c["NT"], c["TC"], c["NCH"], c["TT"]
    NC, CPB = c["NC"], c["CPB"]
    DT2 = DSL // 128
    FCH = TSL // TC
    TPC = TC // 128
    NKI = TSL // 16
    rg = [list(range(NC))]

    def inp(name, shape, dtype=F32):
        return nc.dram_tensor(name, list(shape), dtype, kind="ExternalInput")
    x_sl = inp("x_sl", [TSL, DM]); normw = inp("normw", [128, DM])
    winp = inp("winp", [DM, 2 * DSL]); convw = inp("convw", [DSL, 4])
    convb = inp("convb", [DSL, 1]); xpw = inp("xpw", [DSL, 96])
    dtw = inp("dtw", [R, DSL]); dtb = inp("dtb", [DSL, 1])
    alog = inp("alog", [128, NT]); dvec = inp("dvec", [DSL, 1])
    outw = inp("outw", [DI, DM]); wr = inp("wr", [DM, E])
    wi_e = inp("wi_e", [DM, F]); wo_e = inp("wo_e", [F, DM])
    meta = inp("meta", [128, 8])
    out_t = nc.dram_tensor("out", [TSL, DM], F32, kind="ExternalOutput")

    xn_loc = nc.dram_tensor("xn_loc", [TSL, DM], BF16)
    xn_all = nc.dram_tensor("xn_all", [NC, TSL, DM], BF16)
    ucb_h = nc.dram_tensor("ucb_h", [DSL, T], BF16)
    zs_h = nc.dram_tensor("zs_h", [DSL, T], BF16)
    xdp = nc.dram_tensor("xdp", [96, T], F32)
    xdr = nc.dram_tensor("xdr", [96, T], F32)
    y3_loc = nc.dram_tensor("y3_loc", [NC, DSL, TSL], BF16)
    y3_all = nc.dram_tensor("y3_all", [NC, DSL, TSL], BF16)
    x2_h = nc.dram_tensor("x2_h", [TSL, DM], F32)
    x2_loc = nc.dram_tensor("x2_loc", [TSL, DM], BF16)
    x2_all = nc.dram_tensor("x2_all", [NC, TSL, DM], BF16)
    rt_loc = nc.dram_tensor("rt_loc", [2, TSL], I16)
    rt_all = nc.dram_tensor("rt_all", [NC, 2, TSL], I16)
    cnt_loc = nc.dram_tensor("cnt_loc", [16, 1], F32)
    cnt_all = nc.dram_tensor("cnt_all", [NC, 16, 1], F32)
    oall_h = nc.dram_tensor("oall_h", [128, 1], F32)
    gidx_h = nc.dram_tensor("gidx_h", [2, TSL], F32)
    lsm_h = nc.dram_tensor("lsm_h", [2, 128, 8 * TSL // 128], I16)
    tok_h = nc.dram_tensor("tok_h", [2, CAP], I16)
    git_h = nc.dram_tensor("git_h", [16, 2 * (CAP // 16)], I16)
    wi_b = nc.dram_tensor("wi_b", [DM, F], BF16)
    wo_b = nc.dram_tensor("wo_b", [F, DM], BF16)
    gip_h = nc.dram_tensor("gip_h", [16, 2 * (TSL // 16)], I16)
    eo_loc = nc.dram_tensor("eo_loc", [CAP, 2, DM], BF16)
    if c.get("dbg"):
        dbg_lg = nc.dram_tensor("dbg_lg", [8, TSL], F32)
        dbg_pr = nc.dram_tensor("dbg_pr", [TSL, 8], F32)
        dbg_i8 = nc.dram_tensor("dbg_i8", [TSL, 8], F32)
    eo_all = nc.dram_tensor("eo_all", [NC, CAP, 2, DM], BF16)
    xnf = xn_all.ap().rearrange("a b c -> (a b) c")
    x2f = x2_all.ap().rearrange("a b c -> (a b) c")
    eof = eo_all.ap().rearrange("a b g d -> (a b g) d")
    rtf = rt_all.ap().rearrange("a b c -> (a b c)")

    P = np.zeros((128, 128), np.float32)
    for c2 in range(NC):
        for k2 in range(2):
            for cc_ in range(NC):
                for k in range(2):
                    if (c2 // 4) == (cc_ // 4) and (k2 < k or (k2 == k and c2 < cc_)):
                        for e2 in range(E):
                            P[16 * c2 + 8 * k2 + e2, 16 * cc_ + 8 * k + e2] = 1.0
    pref_t = nc.inline_tensor(P, name="prefix")
    e640_t = nc.inline_tensor(
        np.array([[float(CAP) * (r % 8)] for r in range(16)], np.float32), name="e640")
    NJ = TT
    ids = []
    for gg in range(2):
        a = np.zeros((1, 128 * 4 * 2 * NJ), np.int16)
        ix = 0
        for p in range(128):
            for cc_ in range(4):
                for k in range(2):
                    for j in range(NJ):
                        a[0, ix] = TSL * (4 * gg + cc_) + 128 * j + p + 1
                        ix += 1
        ids.append(nc.inline_tensor(a, name=f"ids{gg}"))

    SB_ALIGN = 64
    _base0 = nc.lookup_mloc(nc.alloc_sbuf_tensor("arena_mark", [128, 16], F32)).addr + 64
    _base0 = ((_base0 + SB_ALIGN - 1) // SB_ALIGN) * SB_ALIGN
    _cur = {"perm": _base0, "g": 0}
    _gbase = [None]

    def _mk_alloc(group):
        def a(name, shape, dtype):
            import math as _math
            nel = 1
            for s_ in shape[1:]:
                nel *= s_
            width = nel * mybir.dt.size(dtype)
            width = ((width + SB_ALIGN - 1) // SB_ALIGN) * SB_ALIGN
            if group == "perm":
                off = _cur["perm"]; _cur["perm"] += width
                assert _gbase[0] is None, "perm alloc after group start"
            else:
                if _gbase[0] is None:
                    _gbase[0] = _cur["perm"]
                off = _gbase[0] + _cur["g"]; _cur["g"] += width
            t = nc.alloc_sbuf_tensor_at(name, list(shape), dtype, offset=off)
            lim = 229000
            assert off + width <= lim, f"sbuf overflow {name}: {off+width} > {lim}"
            return t
        return a

    def _group_reset():
        _cur["g"] = 0

    sbP = _mk_alloc("perm")
    sb = _mk_alloc("g")
    consts = sbP("consts", [128, 4], F32)
    fil = sbP("fil", [128, 2], F32)
    A_col = sbP("A_col", [128, NT], F32)
    dtb_c = sbP("dtb_c", [128, DT2], F32)
    dvec_c = sbP("dvec_c", [128, DT2], F32)
    cb_c = sbP("cb_c", [128, DT2], F32)
    nwb = sbP("nwb", [128, DM], F32)
    erowb = sbP("erowb", [128, 8], F32)
    metab = sbP("metab", [128, 8], F32)
    idnf = sbP("idnf", [128, 128], F32)
    pref_s = sbP("pref_s", [128, 128], F32)
    e640_c = sbP("e640_c", [16, 1], F32)
    onesb = sbP("onesb", [128, 128], BF16)
    onesf = sbP("onesf", [128, max(TC, TSL // 8) if False else TC], F32)
    carry = sbP("carry", [128, NT], F32)
    gates_p = sbP("gates_p", [128, 2 * TT], F32)
    gidx_p = sbP("gidx_p", [128, 2 * NKI], I16)
    wr_st = sbP("wr_st", [128, 8 * 8], BF16)
    convw_s = sb("convw_s", [128, DT2 * 4], F32)
    wrep = sb("wrep", [128, DT2 * 4 * 128], BF16)
    win_st = sb("win_st", [128, 8 * 4 * 128], BF16)
    xpw_st = sb("xpw_st", [128, DT2 * 96], BF16)
    dtw_st = sb("dtw_st", [64, DT2 * 128], F32)
    dgw = sb("dgw", [128, DT2 * 4 * 128], BF16)
    rep_st = sb("rep_st", [128, NT * 128], BF16)
    nsm_st = sb("nsm_st", [128, NT * 128], BF16)
    bsel = sb("bsel", [96, 2 * 128], F32)

    pb = [nc.alloc_psum_tensor(f"pb{i}", [128, 512], F32) for i in range(6)]
    pbw = nc.alloc_psum_tensor("pbw", [128, 1024], F32)

    m = MK(nc)
    OP, DMA, CC = m.op, m.dma, m.cc

    def FIL():
        return OP("vector", lambda h: h.tensor_copy(fil[:, 0:1], fil[:, 1:2]))


    def _maybe_dump(lcl):
        fn = c.get("dump")
        if fn is None:
            return
        ap = fn(lcl)
        m.dma("gpsimd", out_t.ap().rearrange("a b -> (a b)")[0:ap.nelem()].rearrange(
            "(a b) -> a b", b=DM) if False else out_t[0:ap.shape[0], 0:ap.shape[1]], ap,
            after=[m.ops[-1]["id"]])
    tA = DMA("sync", A_col[:, :], alog[:, :])
    t_small = []
    for mm in range(DT2):
        t_small.append(DMA("sync", dtb_c[:, mm:mm + 1], dtb[mm * 128:(mm + 1) * 128, :]))
        t_small.append(DMA("sync", dvec_c[:, mm:mm + 1], dvec[mm * 128:(mm + 1) * 128, :]))
        t_small.append(DMA("sync", cb_c[:, mm:mm + 1], convb[mm * 128:(mm + 1) * 128, :]))
        t_small.append(DMA("sync", convw_s[:, mm * 4:(mm + 1) * 4], convw[mm * 128:(mm + 1) * 128, :]))
    t_small.append(DMA("sync", pref_s[:, :], pref_t[:, :]))
    t_small.append(DMA("sync", e640_c[:, :], e640_t[:, :]))
    t_nw = DMA("sync", nwb[:, :], normw[:, :])
    t_mt = DMA("sync", metab[:, :], meta[:, :])
    t_win = []
    for kk in range(8):
        for mm in range(4):
            t_win.append(DMA("gpsimd", win_st[:, (kk * 4 + mm) * 128:(kk * 4 + mm + 1) * 128],
                             winp[kk * 128:(kk + 1) * 128, mm * 128:(mm + 1) * 128]))
    t_xpw = [DMA("gpsimd", xpw_st[:, kk * 96:(kk + 1) * 96], xpw[kk * 128:(kk + 1) * 128, :])
             for kk in range(DT2)]
    t_dtw = [DMA("sync", dtw_st[:, mm * 128:(mm + 1) * 128], dtw[:, mm * 128:(mm + 1) * 128])
             for mm in range(DT2)]
    t_wr = [DMA("gpsimd", wr_st[:, kk * 8:(kk + 1) * 8], wr[kk * 128:(kk + 1) * 128, :])
            for kk in range(8)]
    t_wib = [DMA("gpsimd", wi_b[q * (DM // 4):(q + 1) * (DM // 4), :],
                 wi_e[q * (DM // 4):(q + 1) * (DM // 4), :], after=t_wr[-1:])
             for q in range(4)]
    t_wob = [DMA("gpsimd", wo_b[q * (F // 4):(q + 1) * (F // 4), :],
                 wo_e[q * (F // 4):(q + 1) * (F // 4), :], after=[t_wib[-1]])
             for q in range(4)]

    OP("vector", lambda h: h.memset(fil[:, :], 0.0))
    OP("vector", lambda h: h.memset(consts[:, 0:1], 1.0))
    OP("vector", lambda h: h.memset(consts[:, 1:2], 1e-5))
    OP("vector", lambda h: h.memset(consts[:, 2:3], -1.0))
    OP("vector", lambda h: h.memset(consts[:, 3:4], 0.0))
    t_ones = OP("vector", lambda h: h.memset(onesb[:, :], 1.0))
    t_onesf = OP("vector", lambda h: h.memset(onesf[:, :], 1.0))
    t_car0 = OP("vector", lambda h: h.memset(carry[:, :], 0.0))
    t_gip0 = OP("vector", lambda h: h.memset(gidx_p[:, :], 0))
    t_wrep = []
    for mm in range(DT2):
        for j in range(4):
            t_wrep.append(OP("vector", lambda h, mm=mm, j=j: h.tensor_copy(
                wrep[:, (mm * 4 + j) * 128:(mm * 4 + j + 1) * 128],
                convw_s[:, mm * 4 + j:mm * 4 + j + 1].broadcast_to([128, 128])),
                after=t_small))

    t_io = OP("gpsimd", lambda h: h.iota(erowb[:, :], [[1, 8]], base=0, channel_multiplier=0, allow_small_or_imprecise_dtypes=True))
    t_lib = OP("gpsimd", lambda h: h.load_library(library_config.mlp), after=[t_io])
    t_nwb = t_nw
    t_mb = t_mt
    t_idnf = OP("gpsimd", lambda h: h.affine_select(idnf[:, :], onesf[:, 0:128], [[-1, 128]],
                ALU.is_equal, 0.0, base=0, channel_multiplier=1), after=[t_onesf])
    t_idnb = t_idnf
    t_dgw = []
    for mm in range(DT2):
        for j in range(4):
            t_dgw.append(OP("gpsimd", lambda h, mm=mm, j=j: h.affine_select(
                dgw[:, (mm * 4 + j) * 128:(mm * 4 + j + 1) * 128],
                wrep[:, (mm * 4 + j) * 128:(mm * 4 + j + 1) * 128],
                [[-1, 128]], ALU.is_equal, 0.0, base=0, channel_multiplier=1),
                after=[t_wrep[mm * 4 + j]]))
    t_rep = []
    for i in range(NT):
        t_rep.append(OP("gpsimd", lambda h, i=i: h.affine_select(
            rep_st[:, i * 128:(i + 1) * 128].rearrange("p (a b) -> p a b", a=8),
            onesb[:, :].rearrange("p (a b) -> p a b", a=8),
            [[-1, 8], [0, 16]], ALU.is_equal, 0.0,
            base=-8 * (i % (NT // DT2)), channel_multiplier=1), after=[t_ones]))
    t_bsel = []
    for w_, bs in ((0, -64), (1, -80)):
        t_bsel.append(OP("gpsimd", lambda h, w=w_, b=bs: h.affine_select(
            bsel[:, w * 128:(w + 1) * 128].rearrange("p (a b) -> p a b", a=8),
            onesf[0:96, 0:128].rearrange("p (a b) -> p a b", a=8),
            [[0, 8], [-1, 16]], ALU.is_equal, 0.0, base=b, channel_multiplier=1),
            after=[t_onesf]))

    t_A1 = OP("scalar", lambda h: h.activation(A_col[:, :], A_col[:, :], AF.Exp), after=[tA])
    t_A = OP("scalar", lambda h: h.mul(A_col[:, :], A_col[:, :], -1.0))

    NTD0 = NT // DT2
    nsmtmp = sb("nsmtmp", [128, 128], BF16)
    t_nsm = []
    for i in range(NT):
        b0 = 16 * 8 * (i % NTD0)
        OP("gpsimd", lambda h, i=i, b0=b0: h.affine_select(
            nsmtmp[:, :], onesb[:, :], [[-16, 128]], ALU.is_ge, 0.0,
            base=b0, channel_multiplier=1), after=[t_ones])
        t_nsm.append(OP("gpsimd", lambda h, i=i, b0=b0: h.affine_select(
            nsm_st[:, i * 128:(i + 1) * 128], nsmtmp[:, :], [[16, 128]], ALU.is_ge, 0.0,
            base=15 - b0, channel_multiplier=-1)))

    t_initdone = m.join_all()
    m.fence(t_initdone)

    if c.get("upto") == "init":
        _maybe_dump(locals())
        t_end = m.join_all(); m.fence(t_end); m.finalize(); return nc, m, locals()
    xt0 = sb("xt0", [128, DM], F32)
    sqd = sb("sqd", [128, DM], F32)
    st0 = sb("st0", [128, 4], F32)
    xnb0 = sb("xnb0", [128, DM], BF16)
    t_st = None
    t_use = None
    t_ms_prev = None
    t_sts = []
    for i in range(TT):
        t_ld = DMA("sync", xt0[:, :], x_sl[i * 128:(i + 1) * 128, :],
                   after=[t_use] if t_use else [])
        t_sq = OP("scalar", lambda h: h.activation(sqd[:, :], xt0[:, :], AF.Square,
                  accum_out=st0[:, 0:1]), after=[t_ld, t_ms_prev])
        t_ms = OP("vector", lambda h: h.tensor_scalar(st0[:, 1:2], st0[:, 0:1],
                  1.0 / DM, 1e-5, ALU.mult, ALU.add), after=[t_sq])
        t_ms_prev = t_ms
        OP("vector", lambda h: h.tensor_copy(st0[:, 2:3], st0[:, 0:1]))
        t_iv = OP("vector", lambda h: h.reciprocal(st0[:, 2:3], st0[:, 1:2]))
        t_rt = OP("scalar", lambda h: h.activation(st0[:, 3:4], st0[:, 2:3], AF.Sqrt),
                  after=[t_iv])
        t_ap = OP("vector", lambda h: h.tensor_scalar(sqd[:, :], xt0[:, :], st0[:, 3:4],
                  None, ALU.mult), after=[t_rt])
        t_use = OP("vector", lambda h: h.tensor_tensor(xnb0[:, :], sqd[:, :], nwb[:, :],
                   ALU.mult), after=[t_nwb, t_st])
        t_st = DMA("sync", xn_loc[i * 128:(i + 1) * 128, :], xnb0[:, :], after=[t_use])
        t_sts.append(t_st)
    t_agxn = CC(lambda h: h.collective_compute(
        "AllGather", ALU.bypass, replica_groups=rg,
        ins=[xn_loc.ap().opt()], outs=[xn_all.ap().opt()]), after=t_sts)

    if c.get("upto") == "p0":
        _maybe_dump(locals())
        t_end = m.join_all(); m.fence(t_end); m.finalize(); return nc, m, locals()
    xnf_t = sb("xnf_t", [128, 2 * 8 * TC], BF16)
    u_sb = sb("u_sb", [128, DT2 * (TC + 4)], BF16)
    u_sbS = sb("u_sbS", [128, DT2 * (TC + 4)], BF16)
    tailb = sb("tailb", [128, DT2 * 3], BF16)
    ucf = sb("ucf", [128, DT2 * TC], F32)
    ucbt = sb("ucbt", [128, DT2 * TC], BF16)
    zsbt = sb("zsbt", [128, DT2 * TC], BF16)
    xps = sb("xps", [96, TC], F32)
    tk = dict(xn=[None] * 8, uev=[None, None], zev=[None, None], cvev=[None, None],
              ucb=[None, None], tails=[None, None], xpev=None, halo=[None, None],
              ipmm=[None, None, None, None], cvmm=[None, None], xpmm=None,
              zst=[None, None], ucst=[None, None], xpst=None)
    t_xp_sts = []
    ipmm_hist = {0: [], 1: []}
    P1SUB = c.get("p1sub")
    for k in range(NCH):
        xb = (k % 2) * 8 * TC
        for j in range(8):
            dep = [t_agxn] + ipmm_hist[k % 2]
            tk["xn"][j] = DMA("sync" if j % 2 == 0 else "scalar",
                              xnf_t[:, xb + j * TC:xb + (j + 1) * TC],
                              xnf[k * TC:(k + 1) * TC, j * 128:(j + 1) * 128],
                              after=dep, transpose=True)
        if P1SUB == "ld":
            continue
        for mm in range(DT2):
            off = mm * (TC + 4)
            dep = [t for t in tk["cvmm"] if t]
            if k % CPB == 0:
                tk["halo"][mm] = OP("vector", lambda h, off=off: h.memset(
                    u_sb[:, off + 1:off + 4], 0.0), after=dep)
            else:
                tk["halo"][mm] = OP("vector", lambda h, mm=mm, off=off: h.tensor_copy(
                    u_sb[:, off + 1:off + 4], tailb[:, mm * 3:(mm + 1) * 3]),
                    after=dep + [tk["tails"][mm]])
        for mm in range(4):
            bank = pb[mm % 2]
            dep = list(tk["xn"])
            prev_ev = tk["uev"][mm % 2] if mm < 2 else tk["zev"][mm % 2]
            if prev_ev:
                dep.append(prev_ev)
            if mm < 2:
                dep.append(tk["halo"][mm])

            def ipmm(h, mm=mm, bank=bank, xb=xb):
                r = None
                for kk in range(8):
                    r = h.matmul(bank[:, :TC], win_st[:, (kk * 4 + mm) * 128:(kk * 4 + mm + 1) * 128],
                                 xnf_t[:, xb + kk * TC:xb + (kk + 1) * TC], start=(kk == 0), stop=(kk == 7))
                return r
            tk["ipmm"][mm] = OP("tensor", ipmm, after=dep)
            if mm < 2:
                off = mm * (TC + 4)
                tk["uev"][mm] = OP("scalar", lambda h, off=off, bank=bank: h.copy(
                    u_sb[:, off + 4:off + 4 + TC], bank[:, :TC]), after=[tk["ipmm"][mm]])
            else:
                tk["zev"][mm - 2] = OP("scalar", lambda h, mm=mm, bank=bank: h.activation(
                    zsbt[:, (mm - 2) * TC:(mm - 1) * TC], bank[:, :TC], AF.Silu),
                    after=[tk["ipmm"][mm], tk["zst"][mm - 2]])
        ipmm_hist[k % 2] = [t for t in tk["ipmm"] if t]
        if P1SUB == "ip":
            continue
        for mm in range(DT2):
            off = mm * (TC + 4)
            t_shf = OP("vector", lambda h, mm=mm, off=off: h.tensor_copy(
                u_sbS[:, off:off + TC + 3], u_sb[:, off + 1:off + 4 + TC]),
                after=[tk["uev"][mm], tk["halo"][mm]] + ([tk["cvmm"][mm]] if tk["cvmm"][mm] else []))
            dep = [tk["uev"][mm], tk["halo"][mm], t_shf]
            if tk["cvev"][mm]:
                dep.append(tk["cvev"][mm])

            def cvmm(h, mm=mm, off=off):
                r = None
                for j in range(4):
                    if (1 + j) % 2 == 0:
                        mv = u_sb[:, off + 1 + j:off + 1 + j + TC]
                    else:
                        mv = u_sbS[:, off + j:off + j + TC]
                    r = h.matmul(pb[2][:, :TC], dgw[:, (mm * 4 + j) * 128:(mm * 4 + j + 1) * 128],
                                 mv, start=(j == 0), stop=(j == 3))
                return r
            tk["cvmm"][mm] = OP("tensor", cvmm, after=dep + t_dgw)
            tk["cvev"][mm] = OP("scalar", lambda h, mm=mm: h.activation(
                ucf[:, mm * TC:(mm + 1) * TC], pb[2][:, :TC], AF.Silu,
                bias=cb_c[:, mm:mm + 1]), after=[tk["cvmm"][mm]])
            tk["ucb"][mm] = OP("vector", lambda h, mm=mm: h.tensor_copy(
                ucbt[:, mm * TC:(mm + 1) * TC], ucf[:, mm * TC:(mm + 1) * TC]),
                after=[tk["cvev"][mm], tk["ucst"][mm], tk["xpmm"]])
            tk["tails"][mm] = OP("vector", lambda h, mm=mm, off=off: h.tensor_copy(
                tailb[:, mm * 3:(mm + 1) * 3], u_sb[:, off + TC + 1:off + TC + 4]),
                after=[tk["uev"][mm]])
        if P1SUB == "cv":
            continue
        dep = [tk["ucb"][0], tk["ucb"][1]]
        if tk["xpev"]:
            dep.append(tk["xpev"])

        def xpmm(h):
            r = None
            for kk in range(DT2):
                r = h.matmul(pb[3][0:96, :TC], xpw_st[:, kk * 96:(kk + 1) * 96],
                             ucbt[:, kk * TC:(kk + 1) * TC], start=(kk == 0), stop=(kk == DT2 - 1))
            return r
        tk["xpmm"] = OP("tensor", xpmm, after=dep)
        tk["xpev"] = OP("vector", lambda h: h.tensor_copy(xps[:, :], pb[3][0:96, :TC]),
                        after=[tk["xpmm"], tk["xpst"]])
        for mm in range(DT2):
            tk["zst"][mm] = DMA("sync", zs_h[mm * 128:(mm + 1) * 128, k * TC:(k + 1) * TC],
                zsbt[:, mm * TC:(mm + 1) * TC], after=[tk["zev"][mm]])
            tk["ucst"][mm] = DMA("sync", ucb_h[mm * 128:(mm + 1) * 128, k * TC:(k + 1) * TC],
                ucbt[:, mm * TC:(mm + 1) * TC], after=[tk["ucb"][mm]])
        tk["xpst"] = DMA("sync", xdp[:, k * TC:(k + 1) * TC], xps[:, :],
                         after=[tk["xpev"]])
        t_xp_sts.append(tk["xpst"])
    if P1SUB:
        _maybe_dump(locals())
        t_end = m.join_all(); m.fence(t_end); m.finalize(); return nc, m, locals()
    t_ar = CC(lambda h: h.collective_compute(
        "AllReduce", ALU.add, replica_groups=rg,
        ins=[xdp.ap().opt()], outs=[xdr.ap().opt()]), after=t_xp_sts)

    if c.get("upto") == "p1":
        _maybe_dump(locals())
        t_end = m.join_all(); m.fence(t_end); m.finalize(); return nc, m, locals()
    xdl = sb("xdl", [96, TC], F32)
    brep = sb("brep", [128, TC], F32)
    crep = sb("crep", [128, TC], F32)
    dt_s = sb("dt_s", [128, DT2 * TC], F32)
    v_s = sb("v_s", [128, DT2 * TC], BF16)
    ucb2 = sb("ucb2", [128, DT2 * TC], BF16)
    zsl2 = sb("zsl2", [128, DT2 * TC], BF16)
    tmp1 = sb("tmp1", [128, DT2 * TC], F32)
    tmp2 = sb("tmp2", [128, DT2 * TC], F32)
    dth_s = sb("dth_s", [128, DT2 * TC], BF16)
    dtl_s = sb("dtl_s", [128, DT2 * TC], BF16)
    dtf_t = sb("dtf_t", [128, TC], F32)
    dA_s = sb("dA_s", [128, 4 * TC], F32)
    dBu_s = sb("dBu_s", [128, 4 * TC], F32)
    h_s = sb("h_s", [128, 4 * TC], F32)
    hC_s = sb("hC_s", [128, 4 * TC], BF16)
    y2_s = sb("y2_s", [128, DT2 * TC], F32)
    y3_s = sb("y3_s", [128, DT2 * TC], BF16)
    NTD = NT // DT2
    p2 = dict(ld=[None] * 5, bc=None, dtp=[None, None], xb=[None, None], ab=[None, None],
              dtv=[None, None], dtl=[None, None], vv=[None, None],
              r1=[None] * NT, r2=[None] * NT,
              lastr1=[None, None], lastr2=[None, None],
              exp=[None] * NT, dbu=[None] * NT, scan=[None] * NT,
              hc=[None] * NT, nsumt=[None] * NT, y2=[None, None], y3=[None, None],
              y3st=[None, None], bcev=None)
    t_y3_sts = []
    P2L = c.get("p2sub", 9)
    t_libstd = OP("gpsimd", lambda h: h.load_library(library_config.standard),
                  after=[t_lib])
    for k in range(NCH):
        dep_b = [p2["dtp"][1]] if p2["dtp"][1] else []
        p2["ld"][0] = DMA("sync", xdl[:, :], xdr[:, k * TC:(k + 1) * TC],
                          after=[t_ar] + dep_b)
        for mm in range(DT2):
            p2["ld"][1 + mm] = DMA("sync", ucb2[:, mm * TC:(mm + 1) * TC],
                                   ucb_h[mm * 128:(mm + 1) * 128, k * TC:(k + 1) * TC],
                                   after=[p2["y2"][mm]] if p2["y2"][mm] else [])
            p2["ld"][3 + mm] = DMA("sync", zsl2[:, mm * TC:(mm + 1) * TC],
                                   zs_h[mm * 128:(mm + 1) * 128, k * TC:(k + 1) * TC],
                                   after=[p2["y3"][mm]] if p2["y3"][mm] else [])
        if P2L < 1:
            continue
        dep = [p2["ld"][0]] + ([p2["bcev"]] if p2["bcev"] else [])
        t_b = OP("tensor", lambda h: h.matmul(pbw[:, 0:TC], bsel[:, 0:128], xdl[:, :],
                 start=True, stop=True), after=dep + t_bsel)
        t_c = OP("tensor", lambda h: h.matmul(pbw[:, 512:512 + TC], bsel[:, 128:256], xdl[:, :],
                 start=True, stop=True), after=dep)
        t_bev = OP("scalar", lambda h: h.copy(brep[:, :], pbw[:, 0:TC]), after=[t_b])
        t_cev = OP("scalar", lambda h: h.copy(crep[:, :], pbw[:, 512:512 + TC]), after=[t_c])
        p2["bc"] = t_cev
        if P2L < 2:
            continue
        for mm in range(DT2):
            dep = [p2["ld"][0]]
            if mm == 0 and p2["y2"][0]:
                dep.append(p2["y2"][0])
            if mm == 1:
                dep.append(p2["xb"][0])
            p2["dtp"][mm] = OP("tensor", lambda h, mm=mm: h.matmul(
                pb[4][:, :TC], dtw_st[:, mm * 128:(mm + 1) * 128], xdl[0:64, :],
                start=True, stop=True), after=dep)
            sl_m = mm * TC
            p2["xb"][mm] = OP("vector", lambda h, sl_m=sl_m: h.tensor_scalar(
                tmp1[:, sl_m:sl_m + TC], pb[4][:, :TC], dtb_c[:, mm:mm + 1], None, ALU.add),
                after=[p2["dtp"][mm], p2["ab"][mm]])
            t_ab = OP("scalar", lambda h, sl_m=sl_m: h.activation(
                tmp2[:, sl_m:sl_m + TC], tmp1[:, sl_m:sl_m + TC], AF.Abs),
                after=[p2["xb"][mm], p2["dtv"][mm]])
            p2["ab"][mm] = t_ab
            t_ea = OP("scalar", lambda h, sl_m=sl_m: h.activation(
                tmp2[:, sl_m:sl_m + TC], tmp2[:, sl_m:sl_m + TC], AF.Exp,
                scale=consts[:, 2:3]))
            t_sp = OP("scalar", lambda h, sl_m=sl_m: h.activation(
                tmp2[:, sl_m:sl_m + TC], tmp2[:, sl_m:sl_m + TC], AF.Ln,
                bias=consts[:, 0:1]))
            t_rl = OP("vector", lambda h, sl_m=sl_m: h.tensor_scalar(
                tmp1[:, sl_m:sl_m + TC], tmp1[:, sl_m:sl_m + TC], 0.0,
                None, ALU.max), after=[t_ab])
            dep = [t_sp, t_rl]
            p2["dtv"][mm] = OP("vector", lambda h, mm=mm, sl_m=sl_m: h.tensor_tensor(
                dt_s[:, mm * TC:(mm + 1) * TC], tmp2[:, sl_m:sl_m + TC],
                tmp1[:, sl_m:sl_m + TC], ALU.add), after=dep)
            dep = [p2["dtv"][mm]]
            if p2["lastr1"][mm]:
                dep.append(p2["lastr1"][mm])
            t_dh = OP("vector", lambda h, mm=mm: h.tensor_copy(
                dth_s[:, mm * TC:(mm + 1) * TC], dt_s[:, mm * TC:(mm + 1) * TC]),
                after=dep)
            OP("vector", lambda h, mm=mm: h.tensor_copy(
                dtf_t[:, :], dth_s[:, mm * TC:(mm + 1) * TC]))
            OP("vector", lambda h, mm=mm: h.tensor_tensor(
                dtf_t[:, :], dt_s[:, mm * TC:(mm + 1) * TC], dtf_t[:, :], ALU.subtract))
            p2["dtl"][mm] = OP("vector", lambda h, mm=mm: h.tensor_copy(
                dtl_s[:, mm * TC:(mm + 1) * TC], dtf_t[:, :]))
            dep = [p2["ld"][1 + mm]]
            if p2["lastr2"][mm]:
                dep.append(p2["lastr2"][mm])
            p2["vv"][mm] = OP("vector", lambda h, mm=mm: h.tensor_tensor(
                v_s[:, mm * TC:(mm + 1) * TC], dt_s[:, mm * TC:(mm + 1) * TC],
                ucb2[:, mm * TC:(mm + 1) * TC], ALU.mult), after=dep)
        if P2L < 3:
            continue
        if k % CPB == 0:
            t_car = OP("vector", lambda h: h.memset(carry[:, :], 0.0))
        else:
            t_car = None
        def emit_rep(i):
            mm = i // NTD
            bnk_d = pb[0] if i % 2 == 0 else pb[1]
            bnk_v = pb[2] if i % 2 == 0 else pb[3]
            dep = [p2["dtl"][mm]]
            if i >= 2 and p2["exp"][i - 2]:
                dep.append(p2["exp"][i - 2])
            elif i < 2 and p2["exp"][NT - 2 + i]:
                dep.append(p2["exp"][NT - 2 + i])
            OP("tensor", lambda h, i=i, mm=mm, b=bnk_d: h.matmul(
                b[:, :TC], rep_st[:, i * 128:(i + 1) * 128],
                dth_s[:, mm * TC:(mm + 1) * TC], start=True, stop=False), after=dep)
            t_r1 = OP("tensor", lambda h, i=i, mm=mm, b=bnk_d: h.matmul(
                b[:, :TC], rep_st[:, i * 128:(i + 1) * 128],
                dtl_s[:, mm * TC:(mm + 1) * TC], start=False, stop=True))
            p2["r1"][i] = t_r1
            if i % NTD == NTD - 1:
                p2["lastr1"][mm] = t_r1
            dep = [p2["vv"][mm]]
            if i >= 2 and p2["dbu"][i - 2]:
                dep.append(p2["dbu"][i - 2])
            elif i < 2 and p2["dbu"][NT - 2 + i]:
                dep.append(p2["dbu"][NT - 2 + i])
            t_r2 = OP("tensor", lambda h, i=i, mm=mm, b=bnk_v: h.matmul(
                b[:, :TC], rep_st[:, i * 128:(i + 1) * 128],
                v_s[:, mm * TC:(mm + 1) * TC], start=True, stop=True), after=dep)
            p2["r2"][i] = t_r2
            if i % NTD == NTD - 1:
                p2["lastr2"][mm] = t_r2

        def emit_expdbu(i):
            sl = (i % 4) * TC
            bnk_d = pb[0] if i % 2 == 0 else pb[1]
            bnk_v = pb[2] if i % 2 == 0 else pb[3]
            dep = [p2["r1"][i], t_A]
            if i >= 4 and p2["scan"][i - 4]:
                dep.append(p2["scan"][i - 4])
            elif i < 4 and p2["scan"][NT - 4 + i]:
                dep.append(p2["scan"][NT - 4 + i])
            p2["exp"][i] = OP("scalar", lambda h, i=i, sl=sl, b=bnk_d: h.activation(
                dA_s[:, sl:sl + TC], b[:, :TC], AF.Exp, scale=A_col[:, i:i + 1]),
                after=dep)
            if P2L < 5:
                return
            dep = [p2["r2"][i], t_bev]
            if i >= 4 and p2["scan"][i - 4]:
                dep.append(p2["scan"][i - 4])
            elif i < 4 and p2["scan"][NT - 4 + i]:
                dep.append(p2["scan"][NT - 4 + i])
            p2["dbu"][i] = OP("vector", lambda h, i=i, sl=sl, b=bnk_v: h.tensor_tensor(
                dBu_s[:, sl:sl + TC], b[:, :TC], brep[:, :], ALU.mult), after=dep)

        def emit_scan(i):
            sl = (i % 4) * TC
            dep = [p2["exp"][i], p2["dbu"][i]]
            if t_car is not None:
                dep.append(t_car)
            if i >= 4 and p2["hc"][i - 4]:
                dep.append(p2["hc"][i - 4])
            elif i < 4 and p2["hc"][NT - 4 + i]:
                dep.append(p2["hc"][NT - 4 + i])
            p2["scan"][i] = OP("vector", lambda h, i=i, sl=sl: h.tensor_tensor_scan(
                h_s[:, sl:sl + TC], dA_s[:, sl:sl + TC], dBu_s[:, sl:sl + TC],
                carry[:, i:i + 1], ALU.mult, ALU.add), after=dep)

        def emit_hc(i):
            sl = (i % 4) * TC
            dep = [t_cev, p2["scan"][i]]
            if i >= 4 and p2["nsumt"][i - 4]:
                dep.append(p2["nsumt"][i - 4])
            elif i < 4 and p2["nsumt"][NT - 4 + i]:
                dep.append(p2["nsumt"][NT - 4 + i])
            p2["hc"][i] = OP("vector", lambda h, i=i, sl=sl: h.tensor_tensor(
                hC_s[:, sl:sl + TC], h_s[:, sl:sl + TC], crep[:, :], ALU.mult), after=dep)
            OP("vector", lambda h, i=i, sl=sl: h.tensor_copy(
                carry[:, i:i + 1], h_s[:, sl + TC - 1:sl + TC]))

        def emit_nsumt(i):
            mm = i // NTD
            first = (i % NTD == 0)
            dep = [p2["hc"][i]]
            if first:
                dep.append(t_nsm[-1])
                if p2["y2"][mm]:
                    dep.append(p2["y2"][mm])
                if mm == 0:
                    dep.append(p2["xb"][1])
            p2["nsumt"][i] = OP("tensor", lambda h, i=i, mm=mm, first=first: h.matmul(
                pb[4 + mm][:, :TC], nsm_st[:, i * 128:(i + 1) * 128],
                hC_s[:, (i % 4) * TC:(i % 4) * TC + TC],
                start=first, stop=(i % NTD == NTD - 1)), after=dep)

        for t in range(NT + 4):
            if t < NT:
                emit_rep(t)
            if P2L >= 4 and 1 <= t <= NT:
                emit_expdbu(t - 1)
            if P2L >= 6 and 2 <= t <= NT + 1:
                emit_scan(t - 2)
            if P2L >= 7 and 3 <= t <= NT + 2:
                emit_hc(t - 3)
            if P2L >= 8 and 4 <= t <= NT + 3:
                emit_nsumt(t - 4)
        if P2L < 9:
            continue
        for mm in range(DT2):
            p2["y2"][mm] = OP("vector", lambda h, mm=mm: h.scalar_tensor_tensor(
                y2_s[:, mm * TC:(mm + 1) * TC], ucb2[:, mm * TC:(mm + 1) * TC],
                dvec_c[:, mm:mm + 1], pb[4 + mm][:, :TC], ALU.mult, ALU.add),
                after=[p2["nsumt"][(mm + 1) * NTD - 1], p2["ld"][1 + mm]])
            p2["y3"][mm] = OP("vector", lambda h, mm=mm: h.tensor_tensor(
                y3_s[:, mm * TC:(mm + 1) * TC], y2_s[:, mm * TC:(mm + 1) * TC],
                zsl2[:, mm * TC:(mm + 1) * TC], ALU.mult),
                after=[p2["ld"][3 + mm], p2["y3st"][mm]])
            _s = (k * TC) // TSL
            _c0 = (k * TC) % TSL
            p2["y3st"][mm] = DMA("sync", y3_loc[_s, mm * 128:(mm + 1) * 128, _c0:_c0 + TC],
                                 y3_s[:, mm * TC:(mm + 1) * TC], after=[p2["y3"][mm]])
            t_y3_sts.append(p2["y3st"][mm])
        p2["bcev"] = p2["hc"][NT - 1]
    from concourse.aot_env import allow_ici_all_to_all

    def _a2a_y3(h):
        with allow_ici_all_to_all():
            return h.collective_compute(
                "AllToAll", ALU.bypass, replica_groups=rg,
                ins=[y3_loc.ap().opt()], outs=[y3_all.ap().opt()])
    if P2L < 9:
        _maybe_dump(locals())
        t_end = m.join_all(); m.fence(t_end); m.finalize(); return nc, m, locals()
    t_agy3 = CC(_a2a_y3, after=t_y3_sts)

    if c.get("upto") == "p2":
        _maybe_dump(locals())
        t_end = m.join_all(); m.fence(t_end); m.finalize(); return nc, m, locals()
    t_g1end = m.join_all()
    m.fence(t_g1end)
    _group_reset()
    y3t = sb("y3t", [128, 16 * TC], BF16)
    outw_st = sb("outw_st", [128, 16 * 8 * 128], BF16)
    xmf = sb("xmf", [128, 8 * TC], F32)
    xr_sb = sb("xr_sb", [128, DM], F32)
    xt3 = sb("xt3", [128, DM], F32)
    st3 = sb("st3", [128, 6], F32)
    sq3 = sb("sq3", [128, DM], F32)
    x2t = sb("x2t", [128, DM], F32)
    x2b = sb("x2b", [128, DM], BF16)
    t_outw = []
    for kk in range(16):
        for mm2 in range(8):
            t_outw.append(DMA("gpsimd", outw_st[:, (kk * 8 + mm2) * 128:(kk * 8 + mm2 + 1) * 128],
                              outw[kk * 128:(kk + 1) * 128, mm2 * 128:(mm2 + 1) * 128],
                              after=[t_agy3] if kk == 0 and mm2 == 0 else []))
    pid_g = nc.partition_id()
    p3 = dict(mv=[None] * 16, op=[None] * 8, ev=[None] * 8, tr=[None] * 8,
              xr=None, s1=None, s2=None)
    t_x2_sts = []
    for j in range(FCH):
        for kk in range(16):
            dep = [t_agy3]
            if p3["op"][7]:
                dep.append(p3["op"][7])
            p3["mv"][kk] = DMA("sync", y3t[:, kk * TC:(kk + 1) * TC],
                               y3_all[kk // 2, (kk % 2) * 128:(kk % 2) * 128 + 128,
                                      j * TC:(j + 1) * TC],
                               after=dep)
        for mm2 in range(8):
            bank = pb[mm2 % 2]
            dep = list(p3["mv"]) + t_outw
            if p3["ev"][mm2 % 2 + 6] if False else False:
                pass
            if mm2 >= 2:
                dep.append(p3["ev"][mm2 - 2])
            elif p3["ev"][6 + mm2]:
                dep.append(p3["ev"][6 + mm2])

            def opmm(h, mm2=mm2, bank=bank):
                r2 = None
                for kk in range(16):
                    r2 = h.matmul(bank[:, :TC], outw_st[:, (kk * 8 + mm2) * 128:(kk * 8 + mm2 + 1) * 128],
                                  y3t[:, kk * TC:(kk + 1) * TC], start=(kk == 0), stop=(kk == 15))
                return r2
            p3["op"][mm2] = OP("tensor", opmm, after=dep)
            p3["ev"][mm2] = OP("scalar", lambda h, mm2=mm2, bank=bank: h.copy(
                xmf[:, mm2 * TC:(mm2 + 1) * TC], bank[:, :TC]), after=[p3["op"][mm2]])
        for tt in range(TPC):
            ti = j * TPC + tt
            xl_dep = [p3["xr"]] if p3["xr"] else []
            if c.get("p3pre") and p3["s1"]:
                xl_dep.append(p3["s1"])
            t_xl = DMA("sync", xt3[:, :], x_sl[ti * 128:(ti + 1) * 128, :],
                       after=xl_dep)
            trs = []
            for mm2 in range(8):
                dep = [p3["ev"][mm2]]
                if p3["xr"]:
                    dep.append(p3["xr"])
                trs.append(OP("tensor", lambda h, mm2=mm2, tt=tt: h.transpose(
                    pbw[:, mm2 * 128:(mm2 + 1) * 128],
                    xmf[:, mm2 * TC + tt * 128:mm2 * TC + tt * 128 + 128], idnb[:, :])
                    if False else h.transpose(
                    pbw[:, mm2 * 128:(mm2 + 1) * 128],
                    xmf[:, mm2 * TC + tt * 128:mm2 * TC + tt * 128 + 128], idnf[:, :]),
                    after=dep + [t_idnf]))
            p3["xr"] = OP("vector", lambda h: h.tensor_tensor(
                xr_sb[:, :], pbw[:, 0:DM], xt3[:, :], ALU.add), after=trs + [t_xl])
            t_sq = OP("scalar", lambda h: h.activation(sq3[:, :], xr_sb[:, :], AF.Square,
                      accum_out=st3[:, 0:1]), after=[p3["xr"]])
            t_mu = OP("vector", lambda h: h.reduce_sum(st3[:, 1:2], xr_sb[:, :], AX.X))
            t_a = OP("vector", lambda h: h.tensor_scalar(st3[:, 2:3], st3[:, 0:1],
                     1.0 / DM, 1e-5, ALU.mult, ALU.add), after=[t_sq])
            t_mu2 = OP("vector", lambda h: h.tensor_scalar(st3[:, 1:2], st3[:, 1:2],
                       1.0 / DM, None, ALU.mult))
            FIL()
            t_b2 = OP("vector", lambda h: h.tensor_tensor(st3[:, 3:4], st3[:, 1:2],
                      st3[:, 1:2], ALU.mult))
            FIL()
            t_va = OP("vector", lambda h: h.tensor_tensor(st3[:, 2:3], st3[:, 2:3],
                      st3[:, 3:4], ALU.subtract))
            t_sd = OP("scalar", lambda h: h.activation(st3[:, 4:5], st3[:, 2:3], AF.Sqrt),
                      after=[t_va])
            t_rs = OP("vector", lambda h: h.reciprocal(st3[:, 5:6], st3[:, 4:5]),
                      after=[t_sd])
            FIL()
            t_x2 = OP("vector", lambda h: h.tensor_scalar(x2t[:, :], xr_sb[:, :],
                      st3[:, 1:2], st3[:, 5:6], ALU.subtract, ALU.mult),
                      after=[p3["s1"]])
            t_x2b = OP("vector", lambda h: h.tensor_copy(x2b[:, :], x2t[:, :]),
                       after=[p3["s2"]])
            if c.get("p3pre"):
                t_s1 = DMA("sync", x2_h[ti * 128:(ti + 1) * 128, :], xr_sb[:, :],
                           after=[p3["xr"], t_x2])
            else:
                t_s1 = DMA("sync", x2_h[ti * 128:(ti + 1) * 128, :], x2t[:, :], after=[t_x2])
            t_s2 = DMA("sync", x2_loc[ti * 128:(ti + 1) * 128, :], x2b[:, :], after=[t_x2b])
            p3["s1"], p3["s2"] = t_s1, t_s2
            t_x2_sts += [t_s1, t_s2]
            if not c.get("moe", True):
                t_s3 = DMA("sync", out_t[ti * 128:(ti + 1) * 128, :], x2t[:, :],
                           after=[t_x2])
                p3["s1"] = t_s3
                t_x2_sts.append(t_s3)
    if not c.get("moe", True):
        t_end = m.join_all()
        m.fence(t_end)
        m.finalize()
        return nc, m, locals()
    t_agx2 = CC(lambda h: h.collective_compute(
        "AllGather", ALU.bypass, replica_groups=rg,
        ins=[x2_loc.ap().opt()], outs=[x2_all.ap().opt()]), after=t_x2_sts)

    if c.get("upto") == "p3":
        _maybe_dump(locals())
        t_end = m.join_all(); m.fence(t_end); m.finalize(); return nc, m, locals()
    x2fm = sb("x2fm", [128, 8 * TSL], BF16)
    lgf = sb("lgf", [8, TSL], F32)
    lt_sb = sb("lt_sb", [128, 8], F32)
    st4 = sb("st4", [128, 4], F32)
    pr_sb = sb("pr_sb", [128, 8], F32)
    m8_sb = sb("m8_sb", [128, 8], F32)
    i8_sb = sb("i8_sb", [128, 8], U32)
    if8 = sb("if8", [128, 8], F32)
    mtm = sb("mtm", [128, 16], F32)
    Mg = sb("Mg", [16, TSL], F32)
    ones16 = sb("ones16", [16, TSL], F32)
    posi = sb("posi", [16, TSL], F32)
    kp = sb("kp", [16, TSL], F32)
    slotM = sb("slotM", [16, TSL], F32)
    cnt_sb = sb("cnt_sb", [16, 1], F32)
    cnts128 = sb("cnts128", [128, 1], F32)
    of32 = sb("of32", [128, 1], F32)
    offs = sb("offs", [16, 1], F32)
    sltk = sb("sltk", [128, 2], F32)
    keepk = sb("keepk", [128, 2], F32)
    rowid = sb("rowid", [128, 2], F32)
    smk = sb("smk", [128, 2], F32)
    rt_sb = sb("rt_sb", [2, TSL], I16)
    gid_sb = sb("gid_sb", [2, TSL], F32)
    gi128 = sb("gi128", [128, 16], F32)
    NKIc = TSL // 16

    t_o16 = OP("vector", lambda h: h.memset(ones16[:, :], 1.0))
    x2t4 = sb("x2t4", [128, DM], F32)
    p3r = dict(tr=[None, None], ld=None, ev=[None] * 8, soft=None, mg=[None] * 100)
    t_lg_dep = []
    for ti in range(TT):
        t_l = DMA("sync", x2t4[:, :], x2_h[ti * 128:(ti + 1) * 128, :],
                  after=[p3r["tr"][0], p3r["tr"][1]] + t_x2_sts)
        for q in range(8):
            bnk = pb[2 + q % 2]
            dep = [t_l, t_idnf]
            prev_user = p3r["ev"][q - 2] if q >= 2 else p3r["ev"][q % 2 + 6]
            if prev_user:
                dep.append(prev_user)
            tq = OP("tensor", lambda h, q=q, bnk=bnk: h.transpose(
                bnk[:, 0:128], x2t4[:, q * 128:(q + 1) * 128], idnf[:, :]), after=dep)
            p3r["tr"][q % 2] = tq
            p3r["ev"][q] = OP("vector", lambda h, q=q, ti=ti, bnk=bnk: h.tensor_copy(
                x2fm[:, q * TSL + ti * 128:q * TSL + ti * 128 + 128], bnk[:, 0:128]),
                after=[tq])
    t_lg_dep = [t for t in p3r["ev"] if t]
    NFC = TSL // 512 if TSL >= 512 else 1
    FCW = min(512, TSL)
    t_lgev = []
    for fc in range(NFC):
        def lgmm(h, fc=fc):
            r2 = None
            for kk in range(8):
                r2 = h.matmul(pb[4][0:8, 0:FCW], wr_st[:, kk * 8:(kk + 1) * 8],
                              x2fm[:, kk * TSL + fc * FCW:kk * TSL + (fc + 1) * FCW],
                              start=(kk == 0), stop=(kk == 7))
            return r2
        dep = t_lg_dep + t_wr + (t_lgev[-1:] if t_lgev else [])
        t_mm = OP("tensor", lgmm, after=dep)
        t_lgev.append(OP("vector", lambda h, fc=fc: h.tensor_copy(
            lgf[:, fc * FCW:(fc + 1) * FCW], pb[4][0:8, 0:FCW]), after=[t_mm]))
    if c.get("dbg"):
        DMA("sync", dbg_lg[:, :], lgf[:, :], after=[t_lgev[-1]])
    t_mtr = None
    t_gsave = [None] * TT
    for ti in range(TT):
        dep = [t_lgev[-1]]
        if t_mtr:
            dep.append(t_mtr)
        t_tr = OP("tensor", lambda h, ti=ti: h.transpose(
            pb[5][0:128, 0:8], lgf[:, ti * 128:(ti + 1) * 128], idnf[0:8, 0:8]), after=dep)
        t_le = OP("vector", lambda h: h.tensor_copy(lt_sb[:, :], pb[5][0:128, 0:8]),
                  after=[t_tr])
        t_mtr = t_le
        FIL()
        t_rm = OP("vector", lambda h: h.reduce_max(st4[:, 0:1], lt_sb[:, :], AX.X))
        FIL()
        t_nr = OP("vector", lambda h: h.tensor_scalar(st4[:, 1:2], st4[:, 0:1], -1.0,
                  None, ALU.mult))
        t_ex = OP("scalar", lambda h: h.activation(pr_sb[:, :], lt_sb[:, :], AF.Exp,
                  bias=st4[:, 1:2]), after=[t_nr])
        t_sm = OP("vector", lambda h: h.reduce_sum(st4[:, 2:3], pr_sb[:, :], AX.X),
                  after=[t_ex])
        FIL()
        t_rc = OP("vector", lambda h: h.reciprocal(st4[:, 3:4], st4[:, 2:3]))
        FIL()
        t_pr = OP("vector", lambda h: h.tensor_scalar(pr_sb[:, :], pr_sb[:, :],
                  st4[:, 3:4], None, ALU.mult))
        FIL()
        t_m8 = OP("vector", lambda h: h.max(out=m8_sb[:, :], in_=pr_sb[:, :]))
        FIL()
        t_i8 = OP("vector", lambda h: h.max_index(i8_sb[:, :], m8_sb[:, :], pr_sb[:, :]))
        FIL()
        t_if = OP("vector", lambda h: h.tensor_copy(if8[:, :], i8_sb[:, :]))
        t_gs = OP("vector", lambda h, ti=ti: h.tensor_copy(
            gates_p[:, 2 * ti:2 * ti + 2], m8_sb[:, 0:2]))
        t_gsave[ti] = t_gs
        if c.get("dbg"):
            DMA("sync", dbg_pr[ti * 128:(ti + 1) * 128, :], pr_sb[:, :], after=[t_m8])
            DMA("sync", dbg_i8[ti * 128:(ti + 1) * 128, :], if8[:, :], after=[t_if])
        t_m0 = OP("vector", lambda h: h.tensor_tensor(
            mtm[:, 0:8], if8[:, 0:1].broadcast_to([128, 8]), erowb[:, :], ALU.is_equal),
            after=[t_mb])
        t_m1 = OP("vector", lambda h: h.tensor_tensor(
            mtm[:, 8:16], if8[:, 1:2].broadcast_to([128, 8]), erowb[:, :], ALU.is_equal))
        t_trm = OP("tensor", lambda h, ti=ti: h.transpose(
            pb[5][0:16, 0:128], mtm[:, :], idnf[:, :]), after=[t_m1, t_m0])
        t_mgev = OP("vector", lambda h, ti=ti: h.tensor_copy(
            Mg[:, ti * 128:(ti + 1) * 128], pb[5][0:16, 0:128]), after=[t_trm])
        p3r["mg"][ti] = t_mgev
        t_mtr = t_mgev
    t_cs = OP("vector", lambda h: h.tensor_tensor_scan(
        posi[:, :], ones16[:, :], Mg[:, :], 0.0, ALU.mult, ALU.add),
        after=[t_o16] + [p3r["mg"][t] for t in range(TT)])
    FIL()
    t_ct = OP("vector", lambda h: h.tensor_copy(cnt_sb[:, :], posi[:, TSL - 1:TSL]))
    t_cst = DMA("sync", cnt_loc[:, :], cnt_sb[:, :], after=[t_ct])
    t_agc = CC(lambda h: h.collective_compute(
        "AllGather", ALU.bypass, replica_groups=rg,
        ins=[cnt_loc.ap().opt()], outs=[cnt_all.ap().opt()]), after=[t_cst])
    t_lc = DMA("gpsimd", cnts128[:, :], cnt_all.ap().rearrange("a b c -> (a b) c"),
               after=[t_agc])
    t_off = OP("tensor", lambda h: h.matmul(pb[4][0:128, 0:1], pref_s[:, :],
               cnts128[:, :], start=True, stop=True), after=[t_lc, t_lgev[-1]])
    t_oev = OP("vector", lambda h: h.tensor_copy(of32[:, :], pb[4][0:128, 0:1]),
               after=[t_off])
    t_ost = DMA("sync", oall_h[:, :], of32[:, :], after=[t_oev])
    t_lof = DMA("gpsimd", offs[:, :], oall_h.ap()[bass.ts(pid_g, 16), :], after=[t_ost])
    t_om = OP("vector", lambda h: h.tensor_scalar(offs[:, :], offs[:, :], -1.0,
              None, ALU.add), after=[t_lof])
    FIL()
    t_pg = OP("vector", lambda h: h.tensor_scalar(posi[:, :], posi[:, :], offs[:, 0:1],
              None, ALU.add), after=[t_om, t_cs])
    t_kl = OP("vector", lambda h: h.tensor_single_scalar(kp[:, :], posi[:, :], float(CAP - 1), ALU.is_le))
    t_kp = OP("vector", lambda h: h.tensor_tensor(kp[:, :], kp[:, :], Mg[:, :], ALU.mult))
    t_sc = OP("vector", lambda h: h.tensor_scalar(slotM[:, :], posi[:, :], float(CAP - 1),
              e640_c[:, 0:1], ALU.min, ALU.add))
    t_sM = OP("vector", lambda h: h.tensor_tensor(slotM[:, :], slotM[:, :], Mg[:, :], ALU.mult))
    t_pb3u = None
    t_pb5u = None
    t_rtst = []
    for ti in range(TT):
        t_t1 = OP("tensor", lambda h, ti=ti: h.transpose(
            pb[3][0:128, 0:16], slotM[:, ti * 128:(ti + 1) * 128], idnf[0:16, 0:16]),
            after=[t_sM] + ([t_pb3u] if t_pb3u else []))
        t_t2 = OP("tensor", lambda h, ti=ti: h.transpose(
            pb[2][0:128, 0:16], kp[:, ti * 128:(ti + 1) * 128], idnf[0:16, 0:16]),
            after=[t_kp] + ([t_pb5u] if t_pb5u else []))
        t_r1 = OP("vector", lambda h: h.tensor_reduce(
            sltk[:, :], pb[3][0:128, 0:16].rearrange("p (a b) -> p a b", a=2),
            AX.X, ALU.add), after=[t_t1])
        t_pb3u = t_r1
        t_r2 = OP("vector", lambda h: h.tensor_reduce(
            keepk[:, :], pb[2][0:128, 0:16].rearrange("p (a b) -> p a b", a=2),
            AX.X, ALU.add), after=[t_t2])
        t_pb5u = t_r2
        FIL()
        t_gk = OP("vector", lambda h, ti=ti: h.tensor_tensor(
            gates_p[:, 2 * ti:2 * ti + 2], gates_p[:, 2 * ti:2 * ti + 2], keepk[:, :],
            ALU.mult), after=[t_r2, t_gsave[ti]])
        t_ri = OP("vector", lambda h: h.tensor_scalar(rowid[:, :], sltk[:, :], 2.0,
                  metab[:, 1:2], ALU.mult, ALU.add), after=[t_r1, t_mb])
        t_s1 = OP("vector", lambda h: h.tensor_scalar(smk[:, :], sltk[:, :], 1.0,
                  None, ALU.add))
        FIL()
        t_s2 = OP("vector", lambda h: h.tensor_tensor(smk[:, :], smk[:, :], keepk[:, :], ALU.mult))
        FIL()
        t_s3 = OP("vector", lambda h: h.tensor_scalar(smk[:, :], smk[:, :], -1.0,
                  None, ALU.add))
        t_t3 = OP("tensor", lambda h, ti=ti: h.transpose(
            pb[3][0:2, 256:384], smk[:, :], idnf[:, :]), after=[t_s3, t_r1])
        t_e3 = OP("vector", lambda h, ti=ti: h.tensor_copy(
            rt_sb[:, ti * 128:(ti + 1) * 128], pb[3][0:2, 256:384]), after=[t_t3])
        t_pb3u = t_e3
        t_t4 = OP("tensor", lambda h, ti=ti: h.transpose(
            pb[2][0:2, 256:384], rowid[:, :], idnf[:, :]), after=[t_ri, t_r2])
        t_e4 = OP("vector", lambda h, ti=ti: h.tensor_copy(
            gid_sb[:, ti * 128:(ti + 1) * 128], pb[2][0:2, 256:384]), after=[t_t4])
        t_pb5u = t_e4
    t_rtst = [DMA("sync", rt_loc[:, :], rt_sb[:, :], after=[t_pb3u])]
    t_gist = DMA("sync", gidx_h[:, :], gid_sb[:, :], after=[t_pb5u])
    t_agrt = CC(lambda h: h.collective_compute(
        "AllGather", ALU.bypass, replica_groups=rg,
        ins=[rt_loc.ap().opt()], outs=[rt_all.ap().opt()]), after=t_rtst)
    t_gi = []
    for k in range(2):
        t_l = DMA("sync", gi128[0:TSL // 16, :],
                  gidx_h[k:k + 1, :].rearrange("a (r f) -> (a r) f", f=16),
                  after=[t_gist] + t_gi[-1:])
        t_t = OP("tensor", lambda h, k=k: h.transpose(
            pb[2][0:16, 32:32 + TSL // 16], gi128[0:TSL // 16, :], idnf[0:TSL // 16, 0:TSL // 16]),
            after=[t_l, t_pb5u])
        t_e = OP("vector", lambda h, k=k: h.tensor_copy(
            gidx_p[0:16, k * NKIc:(k + 1) * NKIc], pb[2][0:16, 32:32 + TSL // 16]),
            after=[t_t])
        t_gi.append(t_e)
    t_gpsv = DMA("sync", gip_h[:, :], gidx_p[0:16, 0:2 * NKIc], after=t_gi)
    t_gprl = DMA("sync", gidx_p[16:32, 0:2 * NKIc], gip_h[:, :], after=[t_gpsv])
    t_gi.append(t_gprl)

    if c.get("upto") == "p3r":
        _maybe_dump(locals())
        t_end = m.join_all(); m.fence(t_end); m.finalize(); return nc, m, locals()
    t_g2end = m.join_all()
    m.fence(t_g2end)
    _group_reset()
    NCand = 8 * TSL
    C2 = NCand // 128
    rtq = sb("rtq", [C2, 128], I16)
    rtq32 = sb("rtq32", [C2, 128], F32)
    rtT = sb("rtT", [128, C2], F32)
    mvx = sb("mvx", [128, C2], F32)
    lsm16 = sb("lsm16", [128, C2], I16)
    NCH2 = NCand // 2
    idxt = sb("idxt", [16, NCH2], I16)
    ids_sb = sb("ids_sb", [16, NCH2], I16)
    tokt = sb("tokt", [16, 2 * 2 * CAP], I16)
    tokf = sb("tokf", [1, CAP], F32)
    toki = sb("toki", [1, CAP], I16)
    CB = CAP // 16
    tk40 = sb("tk40", [CB, 16], I16)
    tk40f = sb("tk40f", [CB, 16], F32)
    gi2 = sb("gi2", [128, 2 * CB], I16)
    xin_g = sb("xin_g", [128, 2 * (CAP // 128) * DM], BF16)
    xin32b = sb("xin32b", [128, 128], F32)
    xin_fm = sb("xin_fm", [128, 2 * 8 * CAP], BF16)
    wi_buf = sb("wi_buf", [128, 2 * 8 * 128], BF16)
    wo_sb = sb("wo_sb", [128, (F // 128) * 512], BF16)
    h1_sb = sb("h1_sb", [128, (F // 128) * CAP], BF16)
    eo_sb = sb("eo_sb", [128, (CAP // 128) * DM], BF16)
    SC5 = CAP // 128
    FK = F // 128

    t_mset = OP("vector", lambda h: h.memset(idxt[:, :], -1))
    t_mset2 = OP("vector", lambda h: h.memset(ids_sb[:, :], 0))
    t_mset3 = OP("vector", lambda h: h.memset(gi2[:, :], 0))
    t_idld = []
    t_lsm_dump = []
    lsv = sb("lsv", [128, C2], F32)
    for gg in range(2):
        t_l = DMA("sync", rtq[:, :],
                  rt_all[4 * gg:4 * gg + 4, :, :].rearrange("a b (r f) -> (a b r) f", f=128),
                  after=[t_agrt] + t_lsm_dump)
        t_c32 = OP("vector", lambda h: h.tensor_copy(rtq32[0:C2, 0:128], rtq[0:C2, 0:128]),
                   after=[t_l])
        t_tr = OP("tensor", lambda h: h.transpose(pb[4][0:128, 0:C2], rtq32[0:C2, 0:128],
                  idnf[0:C2, 0:C2]), after=[t_c32, t_oev])
        t_ev = OP("vector", lambda h: h.tensor_copy(rtT[:, :], pb[4][0:128, 0:C2]),
                  after=[t_tr])
        FIL()
        t_ls = OP("vector", lambda h: h.tensor_scalar(lsv[:, :], rtT[:, :],
                  metab[:, 3:4], None, ALU.subtract), after=[t_ev, t_mb])
        FIL()
        t_m1 = OP("vector", lambda h: h.tensor_single_scalar(mvx[:, :], lsv[:, :], 0.0, ALU.is_ge))
        t_m2 = OP("vector", lambda h: h.tensor_single_scalar(rtT[:, :], lsv[:, :],
                  float(CAP - 1), ALU.is_le))
        FIL()
        t_m3 = OP("vector", lambda h: h.tensor_tensor(mvx[:, :], mvx[:, :], rtT[:, :], ALU.mult))
        t_a1 = OP("vector", lambda h: h.tensor_scalar(lsv[:, :], lsv[:, :], 1.0, None, ALU.add))
        FIL()
        t_a2 = OP("vector", lambda h: h.tensor_tensor(lsv[:, :], lsv[:, :], mvx[:, :], ALU.mult))
        FIL()
        t_a3 = OP("vector", lambda h: h.tensor_scalar(lsv[:, :], lsv[:, :], -1.0, None, ALU.add))
        FIL()
        t_ci = OP("vector", lambda h: h.tensor_copy(lsm16[:, :], lsv[:, :]))
        t_lsm_dump.append(DMA("sync", lsm_h[gg, :, :], lsm16[:, :], after=[t_ci]))
    t_libls = OP("gpsimd", lambda h: h.load_library(library_config.local_scatter),
                 after=t_lsm_dump + t_gi + [t_agx2])
    t_scats = []
    t_scprev = None
    for gg in range(2):
        halves = []
        for hv in range(2):
            t_li = DMA("gpsimd", idxt[0:1, 0:NCH2],
                       lsm_h[gg, :, :].rearrange("a b -> (a b)")
                       .rearrange("(a b) -> a b", a=2)[hv:hv + 1, :],
                       after=[t_libls, t_mset] + ([t_scprev] if t_scprev else []))
            t_ld = DMA("gpsimd", ids_sb[0:1, 0:NCH2], ids[gg][:, hv * NCH2:(hv + 1) * NCH2],
                       after=[t_libls, t_mset2] + ([t_scprev] if t_scprev else []))
            t_sc = OP("gpsimd", lambda h, gg=gg, hv=hv: h.local_scatter(
                tokt[:, (2 * gg + hv) * CAP:(2 * gg + hv + 1) * CAP],
                ids_sb[:, :], idxt[:, :], 16, CAP, NCH2),
                after=[t_li, t_ld])
            t_scprev = t_sc
            halves.append(t_sc)
        t_mrg = OP("vector", lambda h, gg=gg: h.tensor_tensor(
            tokt[0:1, 2 * gg * CAP:(2 * gg) * CAP + CAP],
            tokt[0:1, 2 * gg * CAP:(2 * gg) * CAP + CAP],
            tokt[0:1, (2 * gg + 1) * CAP:(2 * gg + 1) * CAP + CAP], ALU.max),
            after=halves)
        t_scats.append(t_mrg)
    t_libm = OP("gpsimd", lambda h: h.load_library(library_config.mlp), after=t_scats)
    t_eo_sts = []
    t_h1ev = [None, None]
    t_eoev = [None, None]
    t_xinu = None
    t_gi2e = None
    t_prev_eost = None
    t_wolast = [None]
    for gg in range(2):
        t_tf = OP("vector", lambda h, gg=gg: h.tensor_copy(
            tokf[:, :], tokt[0:1, 2 * gg * CAP:2 * gg * CAP + CAP]), after=[t_scats[gg], t_gi2e])
        t_tf2 = OP("vector", lambda h: h.tensor_scalar(tokf[:, :], tokf[:, :], -1.0, None, ALU.add))
        t_tf3 = OP("vector", lambda h: h.tensor_scalar(tokf[:, :], tokf[:, :], 0.0, None, ALU.max))
        t_ti16 = OP("vector", lambda h: h.tensor_copy(toki[:, :], tokf[:, :]))
        t_ts = DMA("sync", tok_h[gg:gg + 1, :], toki[:, :], after=[t_ti16])
        t_tl = DMA("sync", tk40[:, :],
                   tok_h[gg:gg + 1, :].rearrange("a (r f) -> (a r) f", f=16), after=[t_ts])
        t_t4f = OP("vector", lambda h: h.tensor_copy(tk40f[:, :], tk40[:, :]), after=[t_tl])
        t_t4t = OP("tensor", lambda h, gg=gg: h.transpose(pb[5][0:16, 0:CB], tk40f[:, :],
                   idnf[0:CB, 0:CB]), after=[t_t4f])
        t_gi2e = OP("vector", lambda h, gg=gg: h.tensor_copy(gi2[0:16, gg * CB:(gg + 1) * CB],
                    pb[5][0:16, 0:CB]), after=[t_t4t, t_mset3])
        t_g2sv = DMA("sync", git_h[:, gg * CB:(gg + 1) * CB],
                     gi2[0:16, gg * CB:(gg + 1) * CB], after=[t_gi2e])
        t_gi2e = DMA("sync", gi2[16:32, gg * CB:(gg + 1) * CB],
                     git_h[:, gg * CB:(gg + 1) * CB], after=[t_g2sv])
        xo_g = (gg % 2) * SC5 * DM
        xo_f = (gg % 2) * 8 * CAP
        t_g = m.dmaop("gpsimd", lambda h, gg=gg, xo_g=xo_g: h.dma_gather(
            out_ap=xin_g[:, xo_g:xo_g + SC5 * DM].rearrange("p (a d) -> p a d", a=SC5),
            in_ap=x2f, idxs_ap=gi2[:, gg * CB:(gg + 1) * CB],
            num_idxs=CAP, num_idxs_reg=CAP, elem_size=DM),
            after=[t_gi2e, t_libm, t_agx2])
        t_fmev = []
        t_trp = None
        t_cb = None
        for sb5 in range(SC5):
            for dmb in range(8):
                t_cb = OP("vector", lambda h, sb5=sb5, dmb=dmb, xo_g=xo_g: h.tensor_copy(
                    xin32b[:, :], xin_g[:, xo_g + sb5 * DM + dmb * 128:xo_g + sb5 * DM + (dmb + 1) * 128]),
                    after=[t_g] + ([t_tp] if t_fmev else []))
                t_tp = OP("tensor", lambda h: h.transpose(
                    pb[4][0:128, 128:256], xin32b[:, :], idnf[:, :]),
                    after=[t_cb] + ([t_trp] if t_trp else []))
                t_trp = OP("vector", lambda h, sb5=sb5, dmb=dmb, xo_f=xo_f: h.tensor_copy(
                    xin_fm[:, xo_f + dmb * CAP + sb5 * 128:xo_f + dmb * CAP + (sb5 + 1) * 128],
                    pb[4][0:128, 128:256]), after=[t_tp])
                t_fmev.append(t_trp)
        t_wi = [None, None]
        t_h1 = None
        slot_chunks = []
        off = 0
        while off < CAP:
            w = min(512, CAP - off)
            slot_chunks.append((off, w))
            off += w
        for fM in range(FK * F // F if False else (F // 128)):
            buf = fM % 2
            deps = [t_fmev[-1]] if fM == 0 else []
            if t_wi[buf]:
                deps.append(t_wi[buf])
            t_wis = [DMA("sync",
                         wi_buf[:, (buf * 8) * 128:(buf + 1) * 8 * 128]
                         .rearrange("p (a c) -> p a c", a=8),
                         wi_b.rearrange("(a p) c -> p a c", p=128)[:, :, fM * 128:(fM + 1) * 128],
                         after=deps + ([t_h1] if t_h1 else []) + t_wib)]
            t_wi[buf] = t_wis[-1]
            for (so, sw) in slot_chunks:
                bank = pb[0] if (so == 0) else pb[1]

                def h1mm(h, fM=fM, buf=buf, so=so, sw=sw, bank=bank, xo_f=xo_f):
                    r2 = None
                    for kk in range(8):
                        r2 = h.matmul(bank[:, 0:sw],
                                      wi_buf[:, (buf * 8 + kk) * 128:(buf * 8 + kk + 1) * 128],
                                      xin_fm[:, xo_f + kk * CAP + so:xo_f + kk * CAP + so + sw],
                                      start=(kk == 0), stop=(kk == 7))
                    return r2
                dep2 = t_wis + [t_fmev[-1]]
                if t_prev_eost:
                    dep2.append(t_prev_eost)
                if t_h1ev[0 if so == 0 else 1]:
                    dep2.append(t_h1ev[0 if so == 0 else 1])
                t_mm = OP("tensor", h1mm, after=dep2)
                t_h1ev[0 if so == 0 else 1] = OP("scalar",
                    lambda h, fM=fM, so=so, sw=sw, bank=bank: h.activation(
                        h1_sb[:, fM * CAP + so:fM * CAP + so + sw], bank[:, 0:sw], AF.Relu),
                    after=[t_mm])
                t_h1 = t_h1ev[0 if so == 0 else 1]
        for dmc in range(2):
            t_woh = []
            woh_dep = [t_h1ev[0], t_h1ev[1]]
            if t_wolast[0]:
                woh_dep.append(t_wolast[0])
            t_woh.append(DMA("sync",
                             wo_sb[:, :].rearrange("p (a c) -> p a c", a=FK),
                             wo_b.rearrange("(a p) c -> p a c", p=128)[:, :, dmc * 512:(dmc + 1) * 512],
                             after=woh_dep + t_wob))
            for sM in range(SC5):
                bank = pb[2] if sM % 2 == 0 else pb[3]

                def eomm(h, sM=sM, dmc=dmc, bank=bank):
                    r2 = None
                    for fK in range(FK):
                        r2 = h.matmul(bank[:, 0:512],
                                      h1_sb[:, fK * CAP + sM * 128:fK * CAP + (sM + 1) * 128],
                                      wo_sb[:, fK * 512:(fK + 1) * 512],
                                      start=(fK == 0), stop=(fK == FK - 1))
                    return r2
                dep = [t_h1ev[0], t_h1ev[1]] + t_woh
                if t_prev_eost:
                    dep.append(t_prev_eost)
                if t_eoev[sM % 2]:
                    dep.append(t_eoev[sM % 2])
                t_mm = OP("tensor", eomm, after=dep)
                t_wolast[0] = t_mm
                t_eoev[sM % 2] = OP("vector", lambda h, sM=sM, dmc=dmc, bank=bank: h.tensor_copy(
                    eo_sb[:, sM * DM + dmc * 512:sM * DM + (dmc + 1) * 512], bank[:, 0:512]),
                    after=[t_mm])
        t_eost = DMA("sync",
                     eo_loc.ap().rearrange("c g d -> c (g d)")[:, gg * DM:(gg + 1) * DM]
                     .rearrange("(a b) d -> b a d", b=128),
                     eo_sb[:, :].rearrange("p (a d) -> p a d", a=SC5),
                     after=[t_eoev[0], t_eoev[1]])
        t_eo_sts.append(t_eost)
        t_xinu = t_h1
        t_prev_eost = t_eost
    t_ageo = CC(lambda h: h.collective_compute(
        "AllGather", ALU.bypass, replica_groups=rg,
        ins=[eo_loc.ap().opt()], outs=[eo_all.ap().opt()]), after=t_eo_sts)

    if c.get("upto") == "p4":
        _maybe_dump(locals())
        t_end = m.join_all(); m.fence(t_end); m.finalize(); return nc, m, locals()
    g5 = sb("g5", [128, 2 * DM], BF16)
    x2t5 = sb("x2t5", [128, DM], F32)
    yt5 = sb("yt5", [128, DM], F32)
    x3t = sb("x3t", [128, DM], F32)
    sq5 = sb("sq5", [128, DM], F32)
    st5 = sb("st5", [128, 6], F32)
    ot5 = sb("ot5", [128, DM], F32)
    p5 = dict(s=None, x2=None)
    t_c2p = None
    for ti in range(TT):
        t_g5 = []
        for k in range(2):
            t_g5.append(m.dmaop("gpsimd", lambda h, k=k, ti=ti: h.dma_gather(
                out_ap=g5[:, k * DM:(k + 1) * DM].rearrange("p (a d) -> p a d", a=1),
                in_ap=eof, idxs_ap=gidx_p[:, k * NKIc + 8 * ti:k * NKIc + 8 * ti + 8],
                num_idxs=128, num_idxs_reg=128, elem_size=DM),
                after=[t_ageo] + t_gi + ([t_c2p] if t_c2p else [])))
        t_xl = DMA("sync", x2t5[:, :], x2_h[ti * 128:(ti + 1) * 128, :],
                   after=([p5["x2"]] if p5["x2"] else []) + t_x2_sts)
        t_c1 = OP("vector", lambda h, ti=ti: h.tensor_scalar(
            yt5[:, :], g5[:, 0:DM], gates_p[:, 2 * ti:2 * ti + 1],
            None, ALU.mult), after=t_g5)
        t_c2 = OP("vector", lambda h, ti=ti: h.scalar_tensor_tensor(
            yt5[:, :], g5[:, DM:2 * DM],
            gates_p[:, 2 * ti + 1:2 * ti + 2], yt5[:, :], ALU.mult, ALU.add))
        t_c2p = t_c2
        t_x3 = OP("vector", lambda h: h.tensor_tensor(x3t[:, :], yt5[:, :], x2t5[:, :],
                  ALU.add), after=[t_xl])
        p5["x2"] = t_x3
        t_sq = OP("scalar", lambda h: h.activation(sq5[:, :], x3t[:, :], AF.Square,
                  accum_out=st5[:, 0:1]), after=[t_x3])
        t_mu = OP("vector", lambda h: h.reduce_sum(st5[:, 1:2], x3t[:, :], AX.X))
        t_a = OP("vector", lambda h: h.tensor_scalar(st5[:, 2:3], st5[:, 0:1],
                 1.0 / DM, 1e-5, ALU.mult, ALU.add), after=[t_sq])
        t_mu2 = OP("vector", lambda h: h.tensor_scalar(st5[:, 1:2], st5[:, 1:2],
                   1.0 / DM, None, ALU.mult))
        FIL()
        t_b2 = OP("vector", lambda h: h.tensor_tensor(st5[:, 3:4], st5[:, 1:2],
                  st5[:, 1:2], ALU.mult))
        FIL()
        t_va = OP("vector", lambda h: h.tensor_tensor(st5[:, 2:3], st5[:, 2:3],
                  st5[:, 3:4], ALU.subtract))
        t_sd = OP("scalar", lambda h: h.activation(st5[:, 4:5], st5[:, 2:3], AF.Sqrt),
                  after=[t_va])
        t_rs = OP("vector", lambda h: h.reciprocal(st5[:, 5:6], st5[:, 4:5]),
                  after=[t_sd])
        FIL()
        t_o = OP("vector", lambda h: h.tensor_scalar(ot5[:, :], x3t[:, :],
                 st5[:, 1:2], st5[:, 5:6], ALU.subtract, ALU.mult),
                 after=[p5["s"]])
        p5["s"] = DMA("sync", out_t[ti * 128:(ti + 1) * 128, :], ot5[:, :], after=[t_o])
    _maybe_dump(locals())
    t_end = m.join_all()
    m.fence(t_end)

    m.finalize()
    return nc, m, locals()


def shard_inputs(c, inputs):
    DM, DI, NT = c["DM"], c["DI"], c["NT"]
    TSL, DSL = c["TSL"], c["DSL"]
    x = np.asarray(inputs["L0_attn_x"], np.float32).reshape(c["T"], DM)
    ipw = np.asarray(inputs["L0_attn_in_proj_w"], np.float32)
    in_maps = []
    for cc in range(c["NC"]):
        d0, d1 = DSL * cc, DSL * (cc + 1)
        al = np.asarray(inputs["L0_attn_A_log"], np.float32)[d0:d1].reshape(-1)
        al = np.ascontiguousarray(al.reshape(NT, 128).T)
        mt = np.zeros((1, 8), np.float32)
        mt[0, 0] = cc; mt[0, 1] = cc // 4; mt[0, 2] = (cc // 4) * (c["E"] * c["CAP"])
        mt[0, 3] = c["CAP"] * cc
        mt = np.ascontiguousarray(np.broadcast_to(mt, (128, 8)))
        m_ = {
            "x_sl": np.ascontiguousarray(x[TSL * cc:TSL * (cc + 1)]),
            "normw": np.ascontiguousarray(np.broadcast_to(
                np.asarray(inputs["L0_attn_norm_w"], np.float32).reshape(1, DM), (128, DM))),
            "winp": np.ascontiguousarray(np.concatenate(
                [ipw[:, d0:d1], ipw[:, DI + d0:DI + d1]], axis=1)),
            "convw": np.ascontiguousarray(np.asarray(inputs["L0_attn_conv_w"], np.float32)[d0:d1]),
            "convb": np.ascontiguousarray(np.asarray(inputs["L0_attn_conv_b"], np.float32)[d0:d1].reshape(-1, 1)),
            "xpw": np.ascontiguousarray(np.asarray(inputs["L0_attn_x_proj_w"], np.float32)[d0:d1]),
            "dtw": np.ascontiguousarray(np.asarray(inputs["L0_attn_dt_proj_w"], np.float32)[:, d0:d1]),
            "dtb": np.ascontiguousarray(np.asarray(inputs["L0_attn_dt_proj_b"], np.float32)[d0:d1].reshape(-1, 1)),
            "alog": np.ascontiguousarray(al),
            "dvec": np.ascontiguousarray(np.asarray(inputs["L0_attn_D"], np.float32)[d0:d1].reshape(-1, 1)),
            "outw": np.ascontiguousarray(np.asarray(inputs["L0_attn_out_proj_w"], np.float32)),
            "wr": np.ascontiguousarray(np.asarray(inputs["L0_ffn_w_router"], np.float32)),
            "wi_e": np.ascontiguousarray(np.asarray(inputs["L0_ffn_wi"], np.float32)[cc]),
            "wo_e": np.ascontiguousarray(np.asarray(inputs["L0_ffn_wo"], np.float32)[cc]),
            "meta": mt,
        }
        in_maps.append(m_)
    return in_maps


def assemble(c, outs):
    return np.concatenate(outs, axis=0).reshape(c["B"], c["L"], c["DM"])


from concourse.bass_utils import run_bass_kernel_spmd


def kernel(**inputs):
    c = make_cfg()
    nc, m, env = build_nc(c)
    in_maps = shard_inputs(c, inputs)
    res = run_bass_kernel_spmd(nc, in_maps, core_ids=list(range(c["NC"])))
    outs = [np.asarray(r["out"], np.float32) for r in res.results]
    return assemble(c, outs).astype(np.float32)


# version 11
# speedup vs baseline: 1.0459x; 1.0170x over previous
import numpy as np
import concourse.bass as bass
import concourse.mybir as mybir
from concourse import library_config
from concourse.library_overlay import lower_extended_insts

F32 = mybir.dt.float32
BF16 = mybir.dt.bfloat16
I16 = mybir.dt.int16
U32 = mybir.dt.uint32
ALU = mybir.AluOpType
AF = mybir.ActivationFunctionType
AX = mybir.AxisListType
ENGS = ("sync", "tensor", "scalar", "vector", "gpsimd")


def make_cfg(L=8192, TC=512, moe=True, cap=640):
    c = {}
    c["moe"] = moe
    c["B"] = 2; c["L"] = L; c["DM"] = 1024; c["DI"] = 2048; c["N"] = 16
    c["R"] = 64; c["E"] = 8; c["F"] = 4096; c["CAP"] = cap; c["NC"] = 8
    c["T"] = c["B"] * L
    c["TSL"] = c["T"] // c["NC"]
    c["DSL"] = c["DI"] // c["NC"]
    c["NT"] = c["DSL"] * c["N"] // 128
    c["TC"] = TC
    c["NCH"] = c["T"] // TC
    c["CPB"] = L // TC
    c["TT"] = c["TSL"] // 128
    assert c["TSL"] % 128 == 0 and L % TC == 0 and c["TSL"] % TC == 0
    return c


class MK:
    def __init__(self, nc):
        self.nc = nc
        self.ops = []
        self.used = set()
        self.esem = {e: nc.alloc_semaphore("es_" + e) for e in ENGS}
        self.dsem = {e: nc.alloc_semaphore("ds_" + e) for e in ("sync", "scalar", "gpsimd")}

    def _rec(self, eng, fn, after, kind):
        tid = len(self.ops)
        after = [a for a in after if a is not None]
        if kind in ("d", "x"):
            self.used.add(tid)
        import traceback as _tb
        st = _tb.extract_stack()
        lines = [f.lineno for f in st if f.filename.endswith("mk.py") or "kernel" in f.filename]
        self.ops.append(dict(id=tid, eng=eng, fn=fn, after=list(after), kind=kind,
                             src=str(lines[-3:-1])))
        for a in after:
            ao = self.ops[a]
            if not (ao["kind"] == "c" and ao["eng"] == eng):
                self.used.add(a)
        return tid

    def op(self, eng, fn, after=()):
        return self._rec(eng, fn, after, "c")

    def dma(self, eng, dst, src, after=(), transpose=False):
        def fn(h, d=dst, s=src, t=transpose):
            if t:
                return h.dma_start(out=d, in_=s, transpose=True)
            return h.dma_start(out=d, in_=s)
        return self._rec(eng, fn, after, "d")

    def dmaop(self, eng, fn, after=()):
        return self._rec(eng, fn, after, "d")

    def cc(self, fn, after=()):
        return self._rec("gpsimd", fn, after, "x")

    def join_all(self, eng="gpsimd"):
        last = {}
        for o in self.ops:
            last[o["eng"]] = o["id"]
        afters = [v for k, v in last.items()]
        return self.op(eng, lambda h: h.nop(), after=afters)

    def fence(self, tok):
        for e in ENGS:
            self.op(e, lambda h: h.nop(), after=[tok])

    def finalize(self, name="prog"):
        nc = self.nc
        cnt = {}
        val = {}
        for o in self.ops:
            if o["id"] in self.used:
                if o["kind"] == "c":
                    sem, inc = self.esem[o["eng"]], 1
                elif o["kind"] == "d":
                    sem, inc = self.dsem[o["eng"]], 16
                else:
                    sem, inc = self.esem[o["eng"]], 1
                key = sem.name
                cnt[key] = cnt.get(key, 0) + inc
                val[o["id"]] = (sem, cnt[key], inc)
        per_eng = {e: [o for o in self.ops if o["eng"] == e] for e in ENGS}
        used = self.used

        def make_stream(eng):
            ops = per_eng[eng]
            def stream(h):
                lastw = {}
                for o in ops:
                    for a in o["after"]:
                        ao = self.ops[a]
                        if ao["kind"] == "c" and ao["eng"] == eng:
                            continue
                        sem, v, _ = val[a]
                        if lastw.get(sem.name, -1) >= v:
                            continue
                        h.wait_ge(sem, v)
                        lastw[sem.name] = v
                    try:
                        inst = o["fn"](h)
                    except Exception as ex:
                        raise RuntimeError(f"emit failed op={o['id']} eng={eng} kind={o['kind']} src={o.get('src')}") from ex
                    if o["id"] in used:
                        sem, v, inc = val[o["id"]]
                        assert inst is not None, f"op {o['id']} on {eng} returned None but is awaited"
                        inst.then_inc(sem, inc)
            return stream

        with nc.Block(name) as blk:
            for e in ENGS:
                if per_eng[e]:
                    getattr(blk, e)(make_stream(e))
        lower_extended_insts(nc)


def build_nc(c):
    nc = bass.Bass("TRN2", target_bir_lowering=False, detect_race_conditions=False)
    DM, DI, N, R, E, F, CAP = c["DM"], c["DI"], c["N"], c["R"], c["E"], c["F"], c["CAP"]
    T, TSL, DSL, NT, TC, NCH, TT = c["T"], c["TSL"], c["DSL"], c["NT"], c["TC"], c["NCH"], c["TT"]
    NC, CPB = c["NC"], c["CPB"]
    DT2 = DSL // 128
    FCH = TSL // TC
    TPC = TC // 128
    NKI = TSL // 16
    rg = [list(range(NC))]

    def inp(name, shape, dtype=F32):
        return nc.dram_tensor(name, list(shape), dtype, kind="ExternalInput")
    x_sl = inp("x_sl", [TSL, DM]); normw = inp("normw", [128, DM])
    winp = inp("winp", [DM, 2 * DSL]); convw = inp("convw", [DSL, 4])
    convb = inp("convb", [DSL, 1]); xpw = inp("xpw", [DSL, 96])
    dtw = inp("dtw", [R, DSL]); dtb = inp("dtb", [DSL, 1])
    alog = inp("alog", [128, NT]); dvec = inp("dvec", [DSL, 1])
    outw = inp("outw", [DI, DM]); wr = inp("wr", [DM, E])
    wi_e = inp("wi_e", [DM, F]); wo_e = inp("wo_e", [F, DM])
    meta = inp("meta", [128, 8])
    out_t = nc.dram_tensor("out", [TSL, DM], F32, kind="ExternalOutput")

    xn_loc = nc.dram_tensor("xn_loc", [TSL, DM], BF16)
    xn_all = nc.dram_tensor("xn_all", [NC, TSL, DM], BF16)
    ucb_h = nc.dram_tensor("ucb_h", [DSL, T], BF16)
    zs_h = nc.dram_tensor("zs_h", [DSL, T], BF16)
    xdp = nc.dram_tensor("xdp", [96, T], F32)
    xdr = nc.dram_tensor("xdr", [96, T], F32)
    y3_loc = nc.dram_tensor("y3_loc", [NC, DSL, TSL], BF16)
    y3_all = nc.dram_tensor("y3_all", [NC, DSL, TSL], BF16)
    x2_h = nc.dram_tensor("x2_h", [TSL, DM], F32)
    x2_loc = nc.dram_tensor("x2_loc", [TSL, DM], BF16)
    x2_all = nc.dram_tensor("x2_all", [NC, TSL, DM], BF16)
    rt_loc = nc.dram_tensor("rt_loc", [2, TSL], I16)
    rt_all = nc.dram_tensor("rt_all", [NC, 2, TSL], I16)
    cnt_loc = nc.dram_tensor("cnt_loc", [16, 1], F32)
    cnt_all = nc.dram_tensor("cnt_all", [NC, 16, 1], F32)
    oall_h = nc.dram_tensor("oall_h", [128, 1], F32)
    gidx_h = nc.dram_tensor("gidx_h", [2, TSL], F32)
    lsm_h = nc.dram_tensor("lsm_h", [2, 128, 8 * TSL // 128], I16)
    tok_h = nc.dram_tensor("tok_h", [2, CAP], I16)
    git_h = nc.dram_tensor("git_h", [16, 2 * (CAP // 16)], I16)
    wi_b = nc.dram_tensor("wi_b", [DM, F], BF16)
    wo_b = nc.dram_tensor("wo_b", [F, DM], BF16)
    gip_h = nc.dram_tensor("gip_h", [16, 2 * (TSL // 16)], I16)
    eo_loc = nc.dram_tensor("eo_loc", [CAP, 2, DM], BF16)
    if c.get("dbg"):
        dbg_lg = nc.dram_tensor("dbg_lg", [8, TSL], F32)
        dbg_pr = nc.dram_tensor("dbg_pr", [TSL, 8], F32)
        dbg_i8 = nc.dram_tensor("dbg_i8", [TSL, 8], F32)
    eo_all = nc.dram_tensor("eo_all", [NC, CAP, 2, DM], BF16)
    xnf = xn_all.ap().rearrange("a b c -> (a b) c")
    x2f = x2_all.ap().rearrange("a b c -> (a b) c")
    eof = eo_all.ap().rearrange("a b g d -> (a b g) d")
    rtf = rt_all.ap().rearrange("a b c -> (a b c)")

    P = np.zeros((128, 128), np.float32)
    for c2 in range(NC):
        for k2 in range(2):
            for cc_ in range(NC):
                for k in range(2):
                    if (c2 // 4) == (cc_ // 4) and (k2 < k or (k2 == k and c2 < cc_)):
                        for e2 in range(E):
                            P[16 * c2 + 8 * k2 + e2, 16 * cc_ + 8 * k + e2] = 1.0
    pref_t = nc.inline_tensor(P, name="prefix")
    e640_t = nc.inline_tensor(
        np.array([[float(CAP) * (r % 8)] for r in range(16)], np.float32), name="e640")
    NJ = TT
    ids = []
    for gg in range(2):
        a = np.zeros((1, 128 * 4 * 2 * NJ), np.int16)
        ix = 0
        for p in range(128):
            for cc_ in range(4):
                for k in range(2):
                    for j in range(NJ):
                        a[0, ix] = TSL * (4 * gg + cc_) + 128 * j + p + 1
                        ix += 1
        ids.append(nc.inline_tensor(a, name=f"ids{gg}"))

    SB_ALIGN = 64
    _base0 = nc.lookup_mloc(nc.alloc_sbuf_tensor("arena_mark", [128, 16], F32)).addr + 64
    _base0 = ((_base0 + SB_ALIGN - 1) // SB_ALIGN) * SB_ALIGN
    _cur = {"perm": _base0, "g": 0}
    _gbase = [None]

    def _mk_alloc(group):
        def a(name, shape, dtype):
            import math as _math
            nel = 1
            for s_ in shape[1:]:
                nel *= s_
            width = nel * mybir.dt.size(dtype)
            width = ((width + SB_ALIGN - 1) // SB_ALIGN) * SB_ALIGN
            if group == "perm":
                off = _cur["perm"]; _cur["perm"] += width
                assert _gbase[0] is None, "perm alloc after group start"
            else:
                if _gbase[0] is None:
                    _gbase[0] = _cur["perm"]
                off = _gbase[0] + _cur["g"]; _cur["g"] += width
            t = nc.alloc_sbuf_tensor_at(name, list(shape), dtype, offset=off)
            lim = 229000
            assert off + width <= lim, f"sbuf overflow {name}: {off+width} > {lim}"
            return t
        return a

    def _group_reset():
        _cur["g"] = 0

    sbP = _mk_alloc("perm")
    sb = _mk_alloc("g")
    consts = sbP("consts", [128, 4], F32)
    fil = sbP("fil", [128, 2], F32)
    A_col = sbP("A_col", [128, NT], F32)
    dtb_c = sbP("dtb_c", [128, DT2], F32)
    dvec_c = sbP("dvec_c", [128, DT2], F32)
    cb_c = sbP("cb_c", [128, DT2], F32)
    nwb = sbP("nwb", [128, DM], F32)
    erowb = sbP("erowb", [128, 8], F32)
    metab = sbP("metab", [128, 8], F32)
    idnf = sbP("idnf", [128, 128], F32)
    pref_s = sbP("pref_s", [128, 128], F32)
    e640_c = sbP("e640_c", [16, 1], F32)
    onesb = sbP("onesb", [128, 128], BF16)
    onesf = sbP("onesf", [128, max(TC, TSL // 8) if False else TC], F32)
    carry = sbP("carry", [128, NT], F32)
    gates_p = sbP("gates_p", [128, 2 * TT], F32)
    gidx_p = sbP("gidx_p", [128, 2 * NKI], I16)
    wr_st = sbP("wr_st", [128, 8 * 8], BF16)
    convw_s = sb("convw_s", [128, DT2 * 4], F32)
    wrep = sb("wrep", [128, DT2 * 4 * 128], BF16)
    win_st = sb("win_st", [128, 8 * 4 * 128], BF16)
    xpw_st = sb("xpw_st", [128, DT2 * 96], BF16)
    dtw_st = sb("dtw_st", [64, DT2 * 128], F32)
    dgw = sb("dgw", [128, DT2 * 4 * 128], BF16)
    rep_st = sb("rep_st", [128, NT * 128], BF16)
    nsm_st = sb("nsm_st", [128, NT * 128], BF16)
    bsel = sb("bsel", [96, 2 * 128], F32)

    pb = [nc.alloc_psum_tensor(f"pb{i}", [128, 512], F32) for i in range(6)]
    pbw = nc.alloc_psum_tensor("pbw", [128, 1024], F32)

    m = MK(nc)
    OP, DMA, CC = m.op, m.dma, m.cc

    def FIL():
        return OP("vector", lambda h: h.tensor_copy(fil[:, 0:1], fil[:, 1:2]))


    def _maybe_dump(lcl):
        fn = c.get("dump")
        if fn is None:
            return
        ap = fn(lcl)
        m.dma("gpsimd", out_t.ap().rearrange("a b -> (a b)")[0:ap.nelem()].rearrange(
            "(a b) -> a b", b=DM) if False else out_t[0:ap.shape[0], 0:ap.shape[1]], ap,
            after=[m.ops[-1]["id"]])
    tA = DMA("sync", A_col[:, :], alog[:, :])
    t_small = []
    for mm in range(DT2):
        t_small.append(DMA("sync", dtb_c[:, mm:mm + 1], dtb[mm * 128:(mm + 1) * 128, :]))
        t_small.append(DMA("sync", dvec_c[:, mm:mm + 1], dvec[mm * 128:(mm + 1) * 128, :]))
        t_small.append(DMA("sync", cb_c[:, mm:mm + 1], convb[mm * 128:(mm + 1) * 128, :]))
        t_small.append(DMA("sync", convw_s[:, mm * 4:(mm + 1) * 4], convw[mm * 128:(mm + 1) * 128, :]))
    t_small.append(DMA("sync", pref_s[:, :], pref_t[:, :]))
    t_small.append(DMA("sync", e640_c[:, :], e640_t[:, :]))
    t_nw = DMA("sync", nwb[:, :], normw[:, :])
    t_mt = DMA("sync", metab[:, :], meta[:, :])
    t_win = []
    for kk in range(8):
        for mm in range(4):
            t_win.append(DMA("gpsimd", win_st[:, (kk * 4 + mm) * 128:(kk * 4 + mm + 1) * 128],
                             winp[kk * 128:(kk + 1) * 128, mm * 128:(mm + 1) * 128]))
    t_xpw = [DMA("gpsimd", xpw_st[:, kk * 96:(kk + 1) * 96], xpw[kk * 128:(kk + 1) * 128, :])
             for kk in range(DT2)]
    t_dtw = [DMA("sync", dtw_st[:, mm * 128:(mm + 1) * 128], dtw[:, mm * 128:(mm + 1) * 128])
             for mm in range(DT2)]
    t_wr = [DMA("gpsimd", wr_st[:, kk * 8:(kk + 1) * 8], wr[kk * 128:(kk + 1) * 128, :])
            for kk in range(8)]
    t_wib = [DMA("gpsimd", wi_b[q * (DM // 4):(q + 1) * (DM // 4), :],
                 wi_e[q * (DM // 4):(q + 1) * (DM // 4), :], after=t_wr[-1:])
             for q in range(4)]
    t_wob = [DMA("gpsimd", wo_b[q * (F // 4):(q + 1) * (F // 4), :],
                 wo_e[q * (F // 4):(q + 1) * (F // 4), :], after=[t_wib[-1]])
             for q in range(4)]

    OP("vector", lambda h: h.memset(fil[:, :], 0.0))
    OP("vector", lambda h: h.memset(consts[:, 0:1], 1.0))
    OP("vector", lambda h: h.memset(consts[:, 1:2], 1e-5))
    OP("vector", lambda h: h.memset(consts[:, 2:3], -1.0))
    OP("vector", lambda h: h.memset(consts[:, 3:4], 0.0))
    t_ones = OP("vector", lambda h: h.memset(onesb[:, :], 1.0))
    t_onesf = OP("vector", lambda h: h.memset(onesf[:, :], 1.0))
    t_car0 = OP("vector", lambda h: h.memset(carry[:, :], 0.0))
    t_gip0 = OP("vector", lambda h: h.memset(gidx_p[:, :], 0))
    t_wrep = []
    for mm in range(DT2):
        for j in range(4):
            t_wrep.append(OP("vector", lambda h, mm=mm, j=j: h.tensor_copy(
                wrep[:, (mm * 4 + j) * 128:(mm * 4 + j + 1) * 128],
                convw_s[:, mm * 4 + j:mm * 4 + j + 1].broadcast_to([128, 128])),
                after=t_small))

    t_io = OP("gpsimd", lambda h: h.iota(erowb[:, :], [[1, 8]], base=0, channel_multiplier=0, allow_small_or_imprecise_dtypes=True))
    t_lib = OP("gpsimd", lambda h: h.load_library(library_config.mlp), after=[t_io])
    t_nwb = t_nw
    t_mb = t_mt
    t_idnf = OP("gpsimd", lambda h: h.affine_select(idnf[:, :], onesf[:, 0:128], [[-1, 128]],
                ALU.is_equal, 0.0, base=0, channel_multiplier=1), after=[t_onesf])
    t_idnb = t_idnf
    t_dgw = []
    for mm in range(DT2):
        for j in range(4):
            t_dgw.append(OP("gpsimd", lambda h, mm=mm, j=j: h.affine_select(
                dgw[:, (mm * 4 + j) * 128:(mm * 4 + j + 1) * 128],
                wrep[:, (mm * 4 + j) * 128:(mm * 4 + j + 1) * 128],
                [[-1, 128]], ALU.is_equal, 0.0, base=0, channel_multiplier=1),
                after=[t_wrep[mm * 4 + j]]))
    t_rep = []
    for i in range(NT):
        t_rep.append(OP("gpsimd", lambda h, i=i: h.affine_select(
            rep_st[:, i * 128:(i + 1) * 128].rearrange("p (a b) -> p a b", a=8),
            onesb[:, :].rearrange("p (a b) -> p a b", a=8),
            [[-1, 8], [0, 16]], ALU.is_equal, 0.0,
            base=-8 * (i % (NT // DT2)), channel_multiplier=1), after=[t_ones]))
    t_bsel = []
    for w_, bs in ((0, -64), (1, -80)):
        t_bsel.append(OP("gpsimd", lambda h, w=w_, b=bs: h.affine_select(
            bsel[:, w * 128:(w + 1) * 128].rearrange("p (a b) -> p a b", a=8),
            onesf[0:96, 0:128].rearrange("p (a b) -> p a b", a=8),
            [[0, 8], [-1, 16]], ALU.is_equal, 0.0, base=b, channel_multiplier=1),
            after=[t_onesf]))

    t_A1 = OP("scalar", lambda h: h.activation(A_col[:, :], A_col[:, :], AF.Exp), after=[tA])
    t_A = OP("scalar", lambda h: h.mul(A_col[:, :], A_col[:, :], -1.0))

    NTD0 = NT // DT2
    nsmtmp = sb("nsmtmp", [128, 128], BF16)
    t_nsm = []
    for i in range(NT):
        b0 = 16 * 8 * (i % NTD0)
        OP("gpsimd", lambda h, i=i, b0=b0: h.affine_select(
            nsmtmp[:, :], onesb[:, :], [[-16, 128]], ALU.is_ge, 0.0,
            base=b0, channel_multiplier=1), after=[t_ones])
        t_nsm.append(OP("gpsimd", lambda h, i=i, b0=b0: h.affine_select(
            nsm_st[:, i * 128:(i + 1) * 128], nsmtmp[:, :], [[16, 128]], ALU.is_ge, 0.0,
            base=15 - b0, channel_multiplier=-1)))

    t_initdone = m.join_all()
    m.fence(t_initdone)

    if c.get("upto") == "init":
        _maybe_dump(locals())
        t_end = m.join_all(); m.fence(t_end); m.finalize(); return nc, m, locals()
    xt0 = sb("xt0", [128, DM], F32)
    sqd = sb("sqd", [128, DM], F32)
    st0 = sb("st0", [128, 4], F32)
    xnb0 = sb("xnb0", [128, DM], BF16)
    t_st = None
    t_use = None
    t_ms_prev = None
    t_sts = []
    for i in range(TT):
        t_ld = DMA("sync", xt0[:, :], x_sl[i * 128:(i + 1) * 128, :],
                   after=[t_use] if t_use else [])
        t_sq = OP("scalar", lambda h: h.activation(sqd[:, :], xt0[:, :], AF.Square,
                  accum_out=st0[:, 0:1]), after=[t_ld, t_ms_prev])
        t_ms = OP("vector", lambda h: h.tensor_scalar(st0[:, 1:2], st0[:, 0:1],
                  1.0 / DM, 1e-5, ALU.mult, ALU.add), after=[t_sq])
        t_ms_prev = t_ms
        OP("vector", lambda h: h.tensor_copy(st0[:, 2:3], st0[:, 0:1]))
        t_iv = OP("vector", lambda h: h.reciprocal(st0[:, 2:3], st0[:, 1:2]))
        t_rt = OP("scalar", lambda h: h.activation(st0[:, 3:4], st0[:, 2:3], AF.Sqrt),
                  after=[t_iv])
        t_ap = OP("vector", lambda h: h.tensor_scalar(sqd[:, :], xt0[:, :], st0[:, 3:4],
                  None, ALU.mult), after=[t_rt])
        t_use = OP("vector", lambda h: h.tensor_tensor(xnb0[:, :], sqd[:, :], nwb[:, :],
                   ALU.mult), after=[t_nwb, t_st])
        t_st = DMA("sync", xn_loc[i * 128:(i + 1) * 128, :], xnb0[:, :], after=[t_use])
        t_sts.append(t_st)
    t_agxn = CC(lambda h: h.collective_compute(
        "AllGather", ALU.bypass, replica_groups=rg,
        ins=[xn_loc.ap().opt()], outs=[xn_all.ap().opt()]), after=t_sts)

    if c.get("upto") == "p0":
        _maybe_dump(locals())
        t_end = m.join_all(); m.fence(t_end); m.finalize(); return nc, m, locals()
    xnf_t = sb("xnf_t", [128, 2 * 8 * TC], BF16)
    u_sb = sb("u_sb", [128, DT2 * (TC + 4)], BF16)
    u_sbS = sb("u_sbS", [128, DT2 * (TC + 4)], BF16)
    tailb = sb("tailb", [128, DT2 * 3], BF16)
    ucf = sb("ucf", [128, DT2 * TC], F32)
    ucbt = sb("ucbt", [128, DT2 * TC], BF16)
    zsbt = sb("zsbt", [128, DT2 * TC], BF16)
    xps = sb("xps", [96, TC], F32)
    tk = dict(xn=[None] * 8, uev=[None, None], zev=[None, None], cvev=[None, None],
              ucb=[None, None], tails=[None, None], xpev=None, halo=[None, None],
              ipmm=[None, None, None, None], cvmm=[None, None], xpmm=None,
              zst=[None, None], ucst=[None, None], xpst=None)
    t_xp_sts = []
    ipmm_hist = {0: [], 1: []}
    P1SUB = c.get("p1sub")
    for k in range(NCH):
        xb = (k % 2) * 8 * TC
        for j in range(8):
            dep = [t_agxn] + ipmm_hist[k % 2]
            tk["xn"][j] = DMA("sync" if j % 2 == 0 else "scalar",
                              xnf_t[:, xb + j * TC:xb + (j + 1) * TC],
                              xnf[k * TC:(k + 1) * TC, j * 128:(j + 1) * 128],
                              after=dep, transpose=True)
        if P1SUB == "ld":
            continue
        for mm in range(DT2):
            off = mm * (TC + 4)
            dep = [t for t in tk["cvmm"] if t]
            if k % CPB == 0:
                tk["halo"][mm] = OP("vector", lambda h, off=off: h.memset(
                    u_sb[:, off + 1:off + 4], 0.0), after=dep)
            else:
                tk["halo"][mm] = OP("vector", lambda h, mm=mm, off=off: h.tensor_copy(
                    u_sb[:, off + 1:off + 4], tailb[:, mm * 3:(mm + 1) * 3]),
                    after=dep + [tk["tails"][mm]])
        for mm in range(4):
            bank = pb[mm % 2]
            dep = list(tk["xn"])
            prev_ev = tk["uev"][mm % 2] if mm < 2 else tk["zev"][mm % 2]
            if prev_ev:
                dep.append(prev_ev)
            if mm < 2:
                dep.append(tk["halo"][mm])

            def ipmm(h, mm=mm, bank=bank, xb=xb):
                r = None
                for kk in range(8):
                    r = h.matmul(bank[:, :TC], win_st[:, (kk * 4 + mm) * 128:(kk * 4 + mm + 1) * 128],
                                 xnf_t[:, xb + kk * TC:xb + (kk + 1) * TC], start=(kk == 0), stop=(kk == 7))
                return r
            tk["ipmm"][mm] = OP("tensor", ipmm, after=dep)
            if mm < 2:
                off = mm * (TC + 4)
                tk["uev"][mm] = OP("scalar", lambda h, off=off, bank=bank: h.copy(
                    u_sb[:, off + 4:off + 4 + TC], bank[:, :TC]), after=[tk["ipmm"][mm]])
            else:
                tk["zev"][mm - 2] = OP("scalar", lambda h, mm=mm, bank=bank: h.activation(
                    zsbt[:, (mm - 2) * TC:(mm - 1) * TC], bank[:, :TC], AF.Silu),
                    after=[tk["ipmm"][mm], tk["zst"][mm - 2]])
        ipmm_hist[k % 2] = [t for t in tk["ipmm"] if t]
        if P1SUB == "ip":
            continue
        for mm in range(DT2):
            off = mm * (TC + 4)
            t_shf = OP("vector", lambda h, mm=mm, off=off: h.tensor_copy(
                u_sbS[:, off:off + TC + 3], u_sb[:, off + 1:off + 4 + TC]),
                after=[tk["uev"][mm], tk["halo"][mm]] + ([tk["cvmm"][mm]] if tk["cvmm"][mm] else []))
            dep = [tk["uev"][mm], tk["halo"][mm], t_shf]
            if tk["cvev"][mm]:
                dep.append(tk["cvev"][mm])

            def cvmm(h, mm=mm, off=off):
                r = None
                for j in range(4):
                    if (1 + j) % 2 == 0:
                        mv = u_sb[:, off + 1 + j:off + 1 + j + TC]
                    else:
                        mv = u_sbS[:, off + j:off + j + TC]
                    r = h.matmul(pb[2][:, :TC], dgw[:, (mm * 4 + j) * 128:(mm * 4 + j + 1) * 128],
                                 mv, start=(j == 0), stop=(j == 3))
                return r
            tk["cvmm"][mm] = OP("tensor", cvmm, after=dep + t_dgw)
            tk["cvev"][mm] = OP("scalar", lambda h, mm=mm: h.activation(
                ucf[:, mm * TC:(mm + 1) * TC], pb[2][:, :TC], AF.Silu,
                bias=cb_c[:, mm:mm + 1]), after=[tk["cvmm"][mm]])
            tk["ucb"][mm] = OP("vector", lambda h, mm=mm: h.tensor_copy(
                ucbt[:, mm * TC:(mm + 1) * TC], ucf[:, mm * TC:(mm + 1) * TC]),
                after=[tk["cvev"][mm], tk["ucst"][mm], tk["xpmm"]])
            tk["tails"][mm] = OP("vector", lambda h, mm=mm, off=off: h.tensor_copy(
                tailb[:, mm * 3:(mm + 1) * 3], u_sb[:, off + TC + 1:off + TC + 4]),
                after=[tk["uev"][mm]])
        if P1SUB == "cv":
            continue
        dep = [tk["ucb"][0], tk["ucb"][1]]
        if tk["xpev"]:
            dep.append(tk["xpev"])

        def xpmm(h):
            r = None
            for kk in range(DT2):
                r = h.matmul(pb[3][0:96, :TC], xpw_st[:, kk * 96:(kk + 1) * 96],
                             ucbt[:, kk * TC:(kk + 1) * TC], start=(kk == 0), stop=(kk == DT2 - 1))
            return r
        tk["xpmm"] = OP("tensor", xpmm, after=dep)
        tk["xpev"] = OP("vector", lambda h: h.tensor_copy(xps[:, :], pb[3][0:96, :TC]),
                        after=[tk["xpmm"], tk["xpst"]])
        for mm in range(DT2):
            tk["zst"][mm] = DMA("gpsimd", zs_h[mm * 128:(mm + 1) * 128, k * TC:(k + 1) * TC],
                zsbt[:, mm * TC:(mm + 1) * TC], after=[tk["zev"][mm]])
            tk["ucst"][mm] = DMA("gpsimd", ucb_h[mm * 128:(mm + 1) * 128, k * TC:(k + 1) * TC],
                ucbt[:, mm * TC:(mm + 1) * TC], after=[tk["ucb"][mm]])
        tk["xpst"] = DMA("gpsimd", xdp[:, k * TC:(k + 1) * TC], xps[:, :],
                         after=[tk["xpev"]])
        t_xp_sts.append(tk["xpst"])
    if P1SUB:
        _maybe_dump(locals())
        t_end = m.join_all(); m.fence(t_end); m.finalize(); return nc, m, locals()
    t_ar = CC(lambda h: h.collective_compute(
        "AllReduce", ALU.add, replica_groups=rg,
        ins=[xdp.ap().opt()], outs=[xdr.ap().opt()]), after=t_xp_sts)

    if c.get("upto") == "p1":
        _maybe_dump(locals())
        t_end = m.join_all(); m.fence(t_end); m.finalize(); return nc, m, locals()
    xdl = sb("xdl", [96, TC], F32)
    brep = sb("brep", [128, TC], F32)
    crep = sb("crep", [128, TC], F32)
    dt_s = sb("dt_s", [128, DT2 * TC], F32)
    v_s = sb("v_s", [128, DT2 * TC], BF16)
    ucb2 = sb("ucb2", [128, DT2 * TC], BF16)
    zsl2 = sb("zsl2", [128, DT2 * TC], BF16)
    tmp1 = sb("tmp1", [128, DT2 * TC], F32)
    tmp2 = sb("tmp2", [128, DT2 * TC], F32)
    dth_s = sb("dth_s", [128, DT2 * TC], BF16)
    dtl_s = sb("dtl_s", [128, DT2 * TC], BF16)
    dtf_t = sb("dtf_t", [128, TC], F32)
    dA_s = sb("dA_s", [128, 4 * TC], F32)
    dBu_s = sb("dBu_s", [128, 4 * TC], F32)
    h_s = sb("h_s", [128, 4 * TC], F32)
    hC_s = sb("hC_s", [128, 4 * TC], BF16)
    y2_s = sb("y2_s", [128, DT2 * TC], F32)
    y3_s = sb("y3_s", [128, DT2 * TC], BF16)
    NTD = NT // DT2
    p2 = dict(ld=[None] * 5, bc=None, dtp=[None, None], xb=[None, None], ab=[None, None],
              dtv=[None, None], dtl=[None, None], vv=[None, None],
              r1=[None] * NT, r2=[None] * NT,
              lastr1=[None, None], lastr2=[None, None],
              exp=[None] * NT, dbu=[None] * NT, scan=[None] * NT,
              hc=[None] * NT, nsumt=[None] * NT, y2=[None, None], y3=[None, None],
              y3st=[None, None], bcev=None)
    t_y3_sts = []
    P2L = c.get("p2sub", 9)
    t_libstd = OP("gpsimd", lambda h: h.load_library(library_config.standard),
                  after=[t_lib])
    for k in range(NCH):
        dep_b = [p2["dtp"][1]] if p2["dtp"][1] else []
        p2["ld"][0] = DMA("sync", xdl[:, :], xdr[:, k * TC:(k + 1) * TC],
                          after=[t_ar] + dep_b)
        for mm in range(DT2):
            p2["ld"][1 + mm] = DMA("sync", ucb2[:, mm * TC:(mm + 1) * TC],
                                   ucb_h[mm * 128:(mm + 1) * 128, k * TC:(k + 1) * TC],
                                   after=[p2["y2"][mm]] if p2["y2"][mm] else [])
            p2["ld"][3 + mm] = DMA("sync", zsl2[:, mm * TC:(mm + 1) * TC],
                                   zs_h[mm * 128:(mm + 1) * 128, k * TC:(k + 1) * TC],
                                   after=[p2["y3"][mm]] if p2["y3"][mm] else [])
        if P2L < 1:
            continue
        dep = [p2["ld"][0]] + ([p2["bcev"]] if p2["bcev"] else [])
        t_b = OP("tensor", lambda h: h.matmul(pbw[:, 0:TC], bsel[:, 0:128], xdl[:, :],
                 start=True, stop=True), after=dep + t_bsel)
        t_c = OP("tensor", lambda h: h.matmul(pbw[:, 512:512 + TC], bsel[:, 128:256], xdl[:, :],
                 start=True, stop=True), after=dep)
        t_bev = OP("scalar", lambda h: h.copy(brep[:, :], pbw[:, 0:TC]), after=[t_b])
        t_cev = OP("scalar", lambda h: h.copy(crep[:, :], pbw[:, 512:512 + TC]), after=[t_c])
        p2["bc"] = t_cev
        if P2L < 2:
            continue
        for mm in range(DT2):
            dep = [p2["ld"][0]]
            if mm == 0 and p2["y2"][0]:
                dep.append(p2["y2"][0])
            if mm == 1:
                dep.append(p2["xb"][0])
            p2["dtp"][mm] = OP("tensor", lambda h, mm=mm: h.matmul(
                pb[4][:, :TC], dtw_st[:, mm * 128:(mm + 1) * 128], xdl[0:64, :],
                start=True, stop=True), after=dep)
            sl_m = mm * TC
            p2["xb"][mm] = OP("vector", lambda h, sl_m=sl_m: h.tensor_scalar(
                tmp1[:, sl_m:sl_m + TC], pb[4][:, :TC], dtb_c[:, mm:mm + 1], None, ALU.add),
                after=[p2["dtp"][mm], p2["ab"][mm]])
            t_ab = OP("scalar", lambda h, sl_m=sl_m: h.activation(
                tmp2[:, sl_m:sl_m + TC], tmp1[:, sl_m:sl_m + TC], AF.Abs),
                after=[p2["xb"][mm], p2["dtv"][mm]])
            p2["ab"][mm] = t_ab
            t_ea = OP("scalar", lambda h, sl_m=sl_m: h.activation(
                tmp2[:, sl_m:sl_m + TC], tmp2[:, sl_m:sl_m + TC], AF.Exp,
                scale=consts[:, 2:3]))
            t_sp = OP("scalar", lambda h, sl_m=sl_m: h.activation(
                tmp2[:, sl_m:sl_m + TC], tmp2[:, sl_m:sl_m + TC], AF.Ln,
                bias=consts[:, 0:1]))
            t_rl = OP("vector", lambda h, sl_m=sl_m: h.tensor_scalar(
                tmp1[:, sl_m:sl_m + TC], tmp1[:, sl_m:sl_m + TC], 0.0,
                None, ALU.max), after=[t_ab])
            dep = [t_sp, t_rl]
            p2["dtv"][mm] = OP("vector", lambda h, mm=mm, sl_m=sl_m: h.tensor_tensor(
                dt_s[:, mm * TC:(mm + 1) * TC], tmp2[:, sl_m:sl_m + TC],
                tmp1[:, sl_m:sl_m + TC], ALU.add), after=dep)
            dep = [p2["dtv"][mm]]
            if p2["lastr1"][mm]:
                dep.append(p2["lastr1"][mm])
            t_dh = OP("vector", lambda h, mm=mm: h.tensor_copy(
                dth_s[:, mm * TC:(mm + 1) * TC], dt_s[:, mm * TC:(mm + 1) * TC]),
                after=dep)
            OP("vector", lambda h, mm=mm: h.tensor_copy(
                dtf_t[:, :], dth_s[:, mm * TC:(mm + 1) * TC]))
            OP("vector", lambda h, mm=mm: h.tensor_tensor(
                dtf_t[:, :], dt_s[:, mm * TC:(mm + 1) * TC], dtf_t[:, :], ALU.subtract))
            p2["dtl"][mm] = OP("vector", lambda h, mm=mm: h.tensor_copy(
                dtl_s[:, mm * TC:(mm + 1) * TC], dtf_t[:, :]))
            dep = [p2["ld"][1 + mm]]
            if p2["lastr2"][mm]:
                dep.append(p2["lastr2"][mm])
            p2["vv"][mm] = OP("vector", lambda h, mm=mm: h.tensor_tensor(
                v_s[:, mm * TC:(mm + 1) * TC], dt_s[:, mm * TC:(mm + 1) * TC],
                ucb2[:, mm * TC:(mm + 1) * TC], ALU.mult), after=dep)
        if P2L < 3:
            continue
        if k % CPB == 0:
            t_car = OP("vector", lambda h: h.memset(carry[:, :], 0.0))
        else:
            t_car = None
        def emit_rep(i):
            mm = i // NTD
            bnk_d = pb[0] if i % 2 == 0 else pb[1]
            bnk_v = pb[2] if i % 2 == 0 else pb[3]
            dep = [p2["dtl"][mm]]
            if i >= 2 and p2["exp"][i - 2]:
                dep.append(p2["exp"][i - 2])
            elif i < 2 and p2["exp"][NT - 2 + i]:
                dep.append(p2["exp"][NT - 2 + i])
            OP("tensor", lambda h, i=i, mm=mm, b=bnk_d: h.matmul(
                b[:, :TC], rep_st[:, i * 128:(i + 1) * 128],
                dth_s[:, mm * TC:(mm + 1) * TC], start=True, stop=False), after=dep)
            t_r1 = OP("tensor", lambda h, i=i, mm=mm, b=bnk_d: h.matmul(
                b[:, :TC], rep_st[:, i * 128:(i + 1) * 128],
                dtl_s[:, mm * TC:(mm + 1) * TC], start=False, stop=True))
            p2["r1"][i] = t_r1
            if i % NTD == NTD - 1:
                p2["lastr1"][mm] = t_r1
            dep = [p2["vv"][mm]]
            if i >= 2 and p2["dbu"][i - 2]:
                dep.append(p2["dbu"][i - 2])
            elif i < 2 and p2["dbu"][NT - 2 + i]:
                dep.append(p2["dbu"][NT - 2 + i])
            t_r2 = OP("tensor", lambda h, i=i, mm=mm, b=bnk_v: h.matmul(
                b[:, :TC], rep_st[:, i * 128:(i + 1) * 128],
                v_s[:, mm * TC:(mm + 1) * TC], start=True, stop=True), after=dep)
            p2["r2"][i] = t_r2
            if i % NTD == NTD - 1:
                p2["lastr2"][mm] = t_r2

        def emit_expdbu(i):
            sl = (i % 4) * TC
            bnk_d = pb[0] if i % 2 == 0 else pb[1]
            bnk_v = pb[2] if i % 2 == 0 else pb[3]
            dep = [p2["r1"][i], t_A]
            if i >= 4 and p2["scan"][i - 4]:
                dep.append(p2["scan"][i - 4])
            elif i < 4 and p2["scan"][NT - 4 + i]:
                dep.append(p2["scan"][NT - 4 + i])
            p2["exp"][i] = OP("scalar", lambda h, i=i, sl=sl, b=bnk_d: h.activation(
                dA_s[:, sl:sl + TC], b[:, :TC], AF.Exp, scale=A_col[:, i:i + 1]),
                after=dep)
            if P2L < 5:
                return
            dep = [p2["r2"][i], t_bev]
            if i >= 4 and p2["scan"][i - 4]:
                dep.append(p2["scan"][i - 4])
            elif i < 4 and p2["scan"][NT - 4 + i]:
                dep.append(p2["scan"][NT - 4 + i])
            p2["dbu"][i] = OP("vector", lambda h, i=i, sl=sl, b=bnk_v: h.tensor_tensor(
                dBu_s[:, sl:sl + TC], b[:, :TC], brep[:, :], ALU.mult), after=dep)

        def emit_scan(i):
            sl = (i % 4) * TC
            dep = [p2["exp"][i], p2["dbu"][i]]
            if t_car is not None:
                dep.append(t_car)
            if i >= 4 and p2["hc"][i - 4]:
                dep.append(p2["hc"][i - 4])
            elif i < 4 and p2["hc"][NT - 4 + i]:
                dep.append(p2["hc"][NT - 4 + i])
            p2["scan"][i] = OP("vector", lambda h, i=i, sl=sl: h.tensor_tensor_scan(
                h_s[:, sl:sl + TC], dA_s[:, sl:sl + TC], dBu_s[:, sl:sl + TC],
                carry[:, i:i + 1], ALU.mult, ALU.add), after=dep)

        def emit_hc(i):
            sl = (i % 4) * TC
            dep = [t_cev, p2["scan"][i]]
            if i >= 4 and p2["nsumt"][i - 4]:
                dep.append(p2["nsumt"][i - 4])
            elif i < 4 and p2["nsumt"][NT - 4 + i]:
                dep.append(p2["nsumt"][NT - 4 + i])
            p2["hc"][i] = OP("vector", lambda h, i=i, sl=sl: h.tensor_tensor(
                hC_s[:, sl:sl + TC], h_s[:, sl:sl + TC], crep[:, :], ALU.mult), after=dep)
            OP("vector", lambda h, i=i, sl=sl: h.tensor_copy(
                carry[:, i:i + 1], h_s[:, sl + TC - 1:sl + TC]))

        def emit_nsumt(i):
            mm = i // NTD
            first = (i % NTD == 0)
            dep = [p2["hc"][i]]
            if first:
                dep.append(t_nsm[-1])
                if p2["y2"][mm]:
                    dep.append(p2["y2"][mm])
                if mm == 0:
                    dep.append(p2["xb"][1])
            p2["nsumt"][i] = OP("tensor", lambda h, i=i, mm=mm, first=first: h.matmul(
                pb[4 + mm][:, :TC], nsm_st[:, i * 128:(i + 1) * 128],
                hC_s[:, (i % 4) * TC:(i % 4) * TC + TC],
                start=first, stop=(i % NTD == NTD - 1)), after=dep)

        for t in range(NT + 4):
            if t < NT:
                emit_rep(t)
            if P2L >= 4 and 1 <= t <= NT:
                emit_expdbu(t - 1)
            if P2L >= 6 and 2 <= t <= NT + 1:
                emit_scan(t - 2)
            if P2L >= 7 and 3 <= t <= NT + 2:
                emit_hc(t - 3)
            if P2L >= 8 and 4 <= t <= NT + 3:
                emit_nsumt(t - 4)
        if P2L < 9:
            continue
        for mm in range(DT2):
            p2["y2"][mm] = OP("vector", lambda h, mm=mm: h.scalar_tensor_tensor(
                y2_s[:, mm * TC:(mm + 1) * TC], ucb2[:, mm * TC:(mm + 1) * TC],
                dvec_c[:, mm:mm + 1], pb[4 + mm][:, :TC], ALU.mult, ALU.add),
                after=[p2["nsumt"][(mm + 1) * NTD - 1], p2["ld"][1 + mm]])
            p2["y3"][mm] = OP("vector", lambda h, mm=mm: h.tensor_tensor(
                y3_s[:, mm * TC:(mm + 1) * TC], y2_s[:, mm * TC:(mm + 1) * TC],
                zsl2[:, mm * TC:(mm + 1) * TC], ALU.mult),
                after=[p2["ld"][3 + mm], p2["y3st"][mm]])
            _s = (k * TC) // TSL
            _c0 = (k * TC) % TSL
            p2["y3st"][mm] = DMA("sync", y3_loc[_s, mm * 128:(mm + 1) * 128, _c0:_c0 + TC],
                                 y3_s[:, mm * TC:(mm + 1) * TC], after=[p2["y3"][mm]])
            t_y3_sts.append(p2["y3st"][mm])
        p2["bcev"] = p2["hc"][NT - 1]
    from concourse.aot_env import allow_ici_all_to_all

    def _a2a_y3(h):
        with allow_ici_all_to_all():
            return h.collective_compute(
                "AllToAll", ALU.bypass, replica_groups=rg,
                ins=[y3_loc.ap().opt()], outs=[y3_all.ap().opt()])
    if P2L < 9:
        _maybe_dump(locals())
        t_end = m.join_all(); m.fence(t_end); m.finalize(); return nc, m, locals()
    t_agy3 = CC(_a2a_y3, after=t_y3_sts)

    if c.get("upto") == "p2":
        _maybe_dump(locals())
        t_end = m.join_all(); m.fence(t_end); m.finalize(); return nc, m, locals()
    t_g1end = m.join_all()
    m.fence(t_g1end)
    _group_reset()
    y3t = sb("y3t", [128, 16 * TC], BF16)
    outw_st = sb("outw_st", [128, 16 * 8 * 128], BF16)
    xmf = sb("xmf", [128, 8 * TC], F32)
    xr_sb = sb("xr_sb", [128, DM], F32)
    xt3 = sb("xt3", [128, DM], F32)
    st3 = sb("st3", [128, 6], F32)
    sq3 = sb("sq3", [128, DM], F32)
    x2t = sb("x2t", [128, DM], F32)
    x2b = sb("x2b", [128, DM], BF16)
    t_outw = []
    for kk in range(16):
        for mm2 in range(8):
            t_outw.append(DMA("gpsimd", outw_st[:, (kk * 8 + mm2) * 128:(kk * 8 + mm2 + 1) * 128],
                              outw[kk * 128:(kk + 1) * 128, mm2 * 128:(mm2 + 1) * 128],
                              after=[t_agy3] if kk == 0 and mm2 == 0 else []))
    pid_g = nc.partition_id()
    p3 = dict(mv=[None] * 16, op=[None] * 8, ev=[None] * 8, tr=[None] * 8,
              xr=None, s1=None, s2=None)
    t_x2_sts = []
    for j in range(FCH):
        for kk in range(16):
            dep = [t_agy3]
            if p3["op"][7]:
                dep.append(p3["op"][7])
            p3["mv"][kk] = DMA("sync", y3t[:, kk * TC:(kk + 1) * TC],
                               y3_all[kk // 2, (kk % 2) * 128:(kk % 2) * 128 + 128,
                                      j * TC:(j + 1) * TC],
                               after=dep)
        for mm2 in range(8):
            bank = pb[mm2 % 2]
            dep = list(p3["mv"]) + t_outw
            if p3["ev"][mm2 % 2 + 6] if False else False:
                pass
            if mm2 >= 2:
                dep.append(p3["ev"][mm2 - 2])
            elif p3["ev"][6 + mm2]:
                dep.append(p3["ev"][6 + mm2])

            def opmm(h, mm2=mm2, bank=bank):
                r2 = None
                for kk in range(16):
                    r2 = h.matmul(bank[:, :TC], outw_st[:, (kk * 8 + mm2) * 128:(kk * 8 + mm2 + 1) * 128],
                                  y3t[:, kk * TC:(kk + 1) * TC], start=(kk == 0), stop=(kk == 15))
                return r2
            p3["op"][mm2] = OP("tensor", opmm, after=dep)
            p3["ev"][mm2] = OP("scalar", lambda h, mm2=mm2, bank=bank: h.copy(
                xmf[:, mm2 * TC:(mm2 + 1) * TC], bank[:, :TC]), after=[p3["op"][mm2]])
        for tt in range(TPC):
            ti = j * TPC + tt
            xl_dep = [p3["xr"]] if p3["xr"] else []
            if c.get("p3pre") and p3["s1"]:
                xl_dep.append(p3["s1"])
            t_xl = DMA("sync", xt3[:, :], x_sl[ti * 128:(ti + 1) * 128, :],
                       after=xl_dep)
            trs = []
            for mm2 in range(8):
                dep = [p3["ev"][mm2]]
                if p3["xr"]:
                    dep.append(p3["xr"])
                trs.append(OP("tensor", lambda h, mm2=mm2, tt=tt: h.transpose(
                    pbw[:, mm2 * 128:(mm2 + 1) * 128],
                    xmf[:, mm2 * TC + tt * 128:mm2 * TC + tt * 128 + 128], idnb[:, :])
                    if False else h.transpose(
                    pbw[:, mm2 * 128:(mm2 + 1) * 128],
                    xmf[:, mm2 * TC + tt * 128:mm2 * TC + tt * 128 + 128], idnf[:, :]),
                    after=dep + [t_idnf]))
            p3["xr"] = OP("vector", lambda h: h.tensor_tensor(
                xr_sb[:, :], pbw[:, 0:DM], xt3[:, :], ALU.add), after=trs + [t_xl])
            t_sq = OP("scalar", lambda h: h.activation(sq3[:, :], xr_sb[:, :], AF.Square,
                      accum_out=st3[:, 0:1]), after=[p3["xr"]])
            t_mu = OP("vector", lambda h: h.reduce_sum(st3[:, 1:2], xr_sb[:, :], AX.X))
            t_a = OP("vector", lambda h: h.tensor_scalar(st3[:, 2:3], st3[:, 0:1],
                     1.0 / DM, 1e-5, ALU.mult, ALU.add), after=[t_sq])
            t_mu2 = OP("vector", lambda h: h.tensor_scalar(st3[:, 1:2], st3[:, 1:2],
                       1.0 / DM, None, ALU.mult))
            FIL()
            t_b2 = OP("vector", lambda h: h.tensor_tensor(st3[:, 3:4], st3[:, 1:2],
                      st3[:, 1:2], ALU.mult))
            FIL()
            t_va = OP("vector", lambda h: h.tensor_tensor(st3[:, 2:3], st3[:, 2:3],
                      st3[:, 3:4], ALU.subtract))
            t_sd = OP("scalar", lambda h: h.activation(st3[:, 4:5], st3[:, 2:3], AF.Sqrt),
                      after=[t_va])
            t_rs = OP("vector", lambda h: h.reciprocal(st3[:, 5:6], st3[:, 4:5]),
                      after=[t_sd])
            FIL()
            t_x2 = OP("vector", lambda h: h.tensor_scalar(x2t[:, :], xr_sb[:, :],
                      st3[:, 1:2], st3[:, 5:6], ALU.subtract, ALU.mult),
                      after=[p3["s1"]])
            t_x2b = OP("vector", lambda h: h.tensor_copy(x2b[:, :], x2t[:, :]),
                       after=[p3["s2"]])
            if c.get("p3pre"):
                t_s1 = DMA("sync", x2_h[ti * 128:(ti + 1) * 128, :], xr_sb[:, :],
                           after=[p3["xr"], t_x2])
            else:
                t_s1 = DMA("sync", x2_h[ti * 128:(ti + 1) * 128, :], x2t[:, :], after=[t_x2])
            t_s2 = DMA("sync", x2_loc[ti * 128:(ti + 1) * 128, :], x2b[:, :], after=[t_x2b])
            p3["s1"], p3["s2"] = t_s1, t_s2
            t_x2_sts += [t_s1, t_s2]
            if not c.get("moe", True):
                t_s3 = DMA("sync", out_t[ti * 128:(ti + 1) * 128, :], x2t[:, :],
                           after=[t_x2])
                p3["s1"] = t_s3
                t_x2_sts.append(t_s3)
    if not c.get("moe", True):
        t_end = m.join_all()
        m.fence(t_end)
        m.finalize()
        return nc, m, locals()
    t_agx2 = CC(lambda h: h.collective_compute(
        "AllGather", ALU.bypass, replica_groups=rg,
        ins=[x2_loc.ap().opt()], outs=[x2_all.ap().opt()]), after=t_x2_sts)

    if c.get("upto") == "p3":
        _maybe_dump(locals())
        t_end = m.join_all(); m.fence(t_end); m.finalize(); return nc, m, locals()
    x2fm = sb("x2fm", [128, 8 * TSL], BF16)
    lgf = sb("lgf", [8, TSL], F32)
    lt_sb = sb("lt_sb", [128, 8], F32)
    st4 = sb("st4", [128, 4], F32)
    pr_sb = sb("pr_sb", [128, 8], F32)
    m8_sb = sb("m8_sb", [128, 8], F32)
    i8_sb = sb("i8_sb", [128, 8], U32)
    if8 = sb("if8", [128, 8], F32)
    mtm = sb("mtm", [128, 16], F32)
    Mg = sb("Mg", [16, TSL], F32)
    ones16 = sb("ones16", [16, TSL], F32)
    posi = sb("posi", [16, TSL], F32)
    kp = sb("kp", [16, TSL], F32)
    slotM = sb("slotM", [16, TSL], F32)
    cnt_sb = sb("cnt_sb", [16, 1], F32)
    cnts128 = sb("cnts128", [128, 1], F32)
    of32 = sb("of32", [128, 1], F32)
    offs = sb("offs", [16, 1], F32)
    sltk = sb("sltk", [128, 2], F32)
    keepk = sb("keepk", [128, 2], F32)
    rowid = sb("rowid", [128, 2], F32)
    smk = sb("smk", [128, 2], F32)
    rt_sb = sb("rt_sb", [2, TSL], I16)
    gid_sb = sb("gid_sb", [2, TSL], F32)
    gi128 = sb("gi128", [128, 16], F32)
    NKIc = TSL // 16

    t_o16 = OP("vector", lambda h: h.memset(ones16[:, :], 1.0))
    x2t4 = sb("x2t4", [128, DM], F32)
    p3r = dict(tr=[None, None], ld=None, ev=[None] * 8, soft=None, mg=[None] * 100)
    t_lg_dep = []
    for ti in range(TT):
        t_l = DMA("sync", x2t4[:, :], x2_h[ti * 128:(ti + 1) * 128, :],
                  after=[p3r["tr"][0], p3r["tr"][1]] + t_x2_sts)
        for q in range(8):
            bnk = pb[2 + q % 2]
            dep = [t_l, t_idnf]
            prev_user = p3r["ev"][q - 2] if q >= 2 else p3r["ev"][q % 2 + 6]
            if prev_user:
                dep.append(prev_user)
            tq = OP("tensor", lambda h, q=q, bnk=bnk: h.transpose(
                bnk[:, 0:128], x2t4[:, q * 128:(q + 1) * 128], idnf[:, :]), after=dep)
            p3r["tr"][q % 2] = tq
            p3r["ev"][q] = OP("vector", lambda h, q=q, ti=ti, bnk=bnk: h.tensor_copy(
                x2fm[:, q * TSL + ti * 128:q * TSL + ti * 128 + 128], bnk[:, 0:128]),
                after=[tq])
    t_lg_dep = [t for t in p3r["ev"] if t]
    NFC = TSL // 512 if TSL >= 512 else 1
    FCW = min(512, TSL)
    t_lgev = []
    for fc in range(NFC):
        def lgmm(h, fc=fc):
            r2 = None
            for kk in range(8):
                r2 = h.matmul(pb[4][0:8, 0:FCW], wr_st[:, kk * 8:(kk + 1) * 8],
                              x2fm[:, kk * TSL + fc * FCW:kk * TSL + (fc + 1) * FCW],
                              start=(kk == 0), stop=(kk == 7))
            return r2
        dep = t_lg_dep + t_wr + (t_lgev[-1:] if t_lgev else [])
        t_mm = OP("tensor", lgmm, after=dep)
        t_lgev.append(OP("vector", lambda h, fc=fc: h.tensor_copy(
            lgf[:, fc * FCW:(fc + 1) * FCW], pb[4][0:8, 0:FCW]), after=[t_mm]))
    if c.get("dbg"):
        DMA("sync", dbg_lg[:, :], lgf[:, :], after=[t_lgev[-1]])
    t_mtr = None
    t_gsave = [None] * TT
    for ti in range(TT):
        dep = [t_lgev[-1]]
        if t_mtr:
            dep.append(t_mtr)
        t_tr = OP("tensor", lambda h, ti=ti: h.transpose(
            pb[5][0:128, 0:8], lgf[:, ti * 128:(ti + 1) * 128], idnf[0:8, 0:8]), after=dep)
        t_le = OP("vector", lambda h: h.tensor_copy(lt_sb[:, :], pb[5][0:128, 0:8]),
                  after=[t_tr])
        t_mtr = t_le
        FIL()
        t_rm = OP("vector", lambda h: h.reduce_max(st4[:, 0:1], lt_sb[:, :], AX.X))
        FIL()
        t_nr = OP("vector", lambda h: h.tensor_scalar(st4[:, 1:2], st4[:, 0:1], -1.0,
                  None, ALU.mult))
        t_ex = OP("scalar", lambda h: h.activation(pr_sb[:, :], lt_sb[:, :], AF.Exp,
                  bias=st4[:, 1:2]), after=[t_nr])
        t_sm = OP("vector", lambda h: h.reduce_sum(st4[:, 2:3], pr_sb[:, :], AX.X),
                  after=[t_ex])
        FIL()
        t_rc = OP("vector", lambda h: h.reciprocal(st4[:, 3:4], st4[:, 2:3]))
        FIL()
        t_pr = OP("vector", lambda h: h.tensor_scalar(pr_sb[:, :], pr_sb[:, :],
                  st4[:, 3:4], None, ALU.mult))
        FIL()
        t_m8 = OP("vector", lambda h: h.max(out=m8_sb[:, :], in_=pr_sb[:, :]))
        FIL()
        t_i8 = OP("vector", lambda h: h.max_index(i8_sb[:, :], m8_sb[:, :], pr_sb[:, :]))
        FIL()
        t_if = OP("vector", lambda h: h.tensor_copy(if8[:, :], i8_sb[:, :]))
        t_gs = OP("vector", lambda h, ti=ti: h.tensor_copy(
            gates_p[:, 2 * ti:2 * ti + 2], m8_sb[:, 0:2]))
        t_gsave[ti] = t_gs
        if c.get("dbg"):
            DMA("sync", dbg_pr[ti * 128:(ti + 1) * 128, :], pr_sb[:, :], after=[t_m8])
            DMA("sync", dbg_i8[ti * 128:(ti + 1) * 128, :], if8[:, :], after=[t_if])
        t_m0 = OP("vector", lambda h: h.tensor_tensor(
            mtm[:, 0:8], if8[:, 0:1].broadcast_to([128, 8]), erowb[:, :], ALU.is_equal),
            after=[t_mb])
        t_m1 = OP("vector", lambda h: h.tensor_tensor(
            mtm[:, 8:16], if8[:, 1:2].broadcast_to([128, 8]), erowb[:, :], ALU.is_equal))
        t_trm = OP("tensor", lambda h, ti=ti: h.transpose(
            pb[5][0:16, 0:128], mtm[:, :], idnf[:, :]), after=[t_m1, t_m0])
        t_mgev = OP("vector", lambda h, ti=ti: h.tensor_copy(
            Mg[:, ti * 128:(ti + 1) * 128], pb[5][0:16, 0:128]), after=[t_trm])
        p3r["mg"][ti] = t_mgev
        t_mtr = t_mgev
    t_cs = OP("vector", lambda h: h.tensor_tensor_scan(
        posi[:, :], ones16[:, :], Mg[:, :], 0.0, ALU.mult, ALU.add),
        after=[t_o16] + [p3r["mg"][t] for t in range(TT)])
    FIL()
    t_ct = OP("vector", lambda h: h.tensor_copy(cnt_sb[:, :], posi[:, TSL - 1:TSL]))
    t_cst = DMA("sync", cnt_loc[:, :], cnt_sb[:, :], after=[t_ct])
    t_agc = CC(lambda h: h.collective_compute(
        "AllGather", ALU.bypass, replica_groups=rg,
        ins=[cnt_loc.ap().opt()], outs=[cnt_all.ap().opt()]), after=[t_cst])
    t_lc = DMA("gpsimd", cnts128[:, :], cnt_all.ap().rearrange("a b c -> (a b) c"),
               after=[t_agc])
    t_off = OP("tensor", lambda h: h.matmul(pb[4][0:128, 0:1], pref_s[:, :],
               cnts128[:, :], start=True, stop=True), after=[t_lc, t_lgev[-1]])
    t_oev = OP("vector", lambda h: h.tensor_copy(of32[:, :], pb[4][0:128, 0:1]),
               after=[t_off])
    t_ost = DMA("sync", oall_h[:, :], of32[:, :], after=[t_oev])
    t_lof = DMA("gpsimd", offs[:, :], oall_h.ap()[bass.ts(pid_g, 16), :], after=[t_ost])
    t_om = OP("vector", lambda h: h.tensor_scalar(offs[:, :], offs[:, :], -1.0,
              None, ALU.add), after=[t_lof])
    FIL()
    t_pg = OP("vector", lambda h: h.tensor_scalar(posi[:, :], posi[:, :], offs[:, 0:1],
              None, ALU.add), after=[t_om, t_cs])
    t_kl = OP("vector", lambda h: h.tensor_single_scalar(kp[:, :], posi[:, :], float(CAP - 1), ALU.is_le))
    t_kp = OP("vector", lambda h: h.tensor_tensor(kp[:, :], kp[:, :], Mg[:, :], ALU.mult))
    t_sc = OP("vector", lambda h: h.tensor_scalar(slotM[:, :], posi[:, :], float(CAP - 1),
              e640_c[:, 0:1], ALU.min, ALU.add))
    t_sM = OP("vector", lambda h: h.tensor_tensor(slotM[:, :], slotM[:, :], Mg[:, :], ALU.mult))
    t_pb3u = None
    t_pb5u = None
    t_rtst = []
    for ti in range(TT):
        t_t1 = OP("tensor", lambda h, ti=ti: h.transpose(
            pb[3][0:128, 0:16], slotM[:, ti * 128:(ti + 1) * 128], idnf[0:16, 0:16]),
            after=[t_sM] + ([t_pb3u] if t_pb3u else []))
        t_t2 = OP("tensor", lambda h, ti=ti: h.transpose(
            pb[2][0:128, 0:16], kp[:, ti * 128:(ti + 1) * 128], idnf[0:16, 0:16]),
            after=[t_kp] + ([t_pb5u] if t_pb5u else []))
        t_r1 = OP("vector", lambda h: h.tensor_reduce(
            sltk[:, :], pb[3][0:128, 0:16].rearrange("p (a b) -> p a b", a=2),
            AX.X, ALU.add), after=[t_t1])
        t_pb3u = t_r1
        t_r2 = OP("vector", lambda h: h.tensor_reduce(
            keepk[:, :], pb[2][0:128, 0:16].rearrange("p (a b) -> p a b", a=2),
            AX.X, ALU.add), after=[t_t2])
        t_pb5u = t_r2
        FIL()
        t_gk = OP("vector", lambda h, ti=ti: h.tensor_tensor(
            gates_p[:, 2 * ti:2 * ti + 2], gates_p[:, 2 * ti:2 * ti + 2], keepk[:, :],
            ALU.mult), after=[t_r2, t_gsave[ti]])
        t_ri = OP("vector", lambda h: h.tensor_scalar(rowid[:, :], sltk[:, :], 2.0,
                  metab[:, 1:2], ALU.mult, ALU.add), after=[t_r1, t_mb])
        t_s1 = OP("vector", lambda h: h.tensor_scalar(smk[:, :], sltk[:, :], 1.0,
                  None, ALU.add))
        FIL()
        t_s2 = OP("vector", lambda h: h.tensor_tensor(smk[:, :], smk[:, :], keepk[:, :], ALU.mult))
        FIL()
        t_s3 = OP("vector", lambda h: h.tensor_scalar(smk[:, :], smk[:, :], -1.0,
                  None, ALU.add))
        t_t3 = OP("tensor", lambda h, ti=ti: h.transpose(
            pb[3][0:2, 256:384], smk[:, :], idnf[:, :]), after=[t_s3, t_r1])
        t_e3 = OP("vector", lambda h, ti=ti: h.tensor_copy(
            rt_sb[:, ti * 128:(ti + 1) * 128], pb[3][0:2, 256:384]), after=[t_t3])
        t_pb3u = t_e3
        t_t4 = OP("tensor", lambda h, ti=ti: h.transpose(
            pb[2][0:2, 256:384], rowid[:, :], idnf[:, :]), after=[t_ri, t_r2])
        t_e4 = OP("vector", lambda h, ti=ti: h.tensor_copy(
            gid_sb[:, ti * 128:(ti + 1) * 128], pb[2][0:2, 256:384]), after=[t_t4])
        t_pb5u = t_e4
    t_rtst = [DMA("sync", rt_loc[:, :], rt_sb[:, :], after=[t_pb3u])]
    t_gist = DMA("sync", gidx_h[:, :], gid_sb[:, :], after=[t_pb5u])
    t_agrt = CC(lambda h: h.collective_compute(
        "AllGather", ALU.bypass, replica_groups=rg,
        ins=[rt_loc.ap().opt()], outs=[rt_all.ap().opt()]), after=t_rtst)
    t_gi = []
    for k in range(2):
        t_l = DMA("sync", gi128[0:TSL // 16, :],
                  gidx_h[k:k + 1, :].rearrange("a (r f) -> (a r) f", f=16),
                  after=[t_gist] + t_gi[-1:])
        t_t = OP("tensor", lambda h, k=k: h.transpose(
            pb[2][0:16, 32:32 + TSL // 16], gi128[0:TSL // 16, :], idnf[0:TSL // 16, 0:TSL // 16]),
            after=[t_l, t_pb5u])
        t_e = OP("vector", lambda h, k=k: h.tensor_copy(
            gidx_p[0:16, k * NKIc:(k + 1) * NKIc], pb[2][0:16, 32:32 + TSL // 16]),
            after=[t_t])
        t_gi.append(t_e)
    t_gpsv = DMA("sync", gip_h[:, :], gidx_p[0:16, 0:2 * NKIc], after=t_gi)
    t_gprl = DMA("sync", gidx_p[16:32, 0:2 * NKIc], gip_h[:, :], after=[t_gpsv])
    t_gi.append(t_gprl)

    if c.get("upto") == "p3r":
        _maybe_dump(locals())
        t_end = m.join_all(); m.fence(t_end); m.finalize(); return nc, m, locals()
    t_g2end = m.join_all()
    m.fence(t_g2end)
    _group_reset()
    NCand = 8 * TSL
    C2 = NCand // 128
    rtq = sb("rtq", [C2, 128], I16)
    rtq32 = sb("rtq32", [C2, 128], F32)
    rtT = sb("rtT", [128, C2], F32)
    mvx = sb("mvx", [128, C2], F32)
    lsm16 = sb("lsm16", [128, C2], I16)
    NCH2 = NCand // 2
    idxt = sb("idxt", [16, NCH2], I16)
    ids_sb = sb("ids_sb", [16, NCH2], I16)
    tokt = sb("tokt", [16, 2 * 2 * CAP], I16)
    tokf = sb("tokf", [1, CAP], F32)
    toki = sb("toki", [1, CAP], I16)
    CB = CAP // 16
    tk40 = sb("tk40", [CB, 16], I16)
    tk40f = sb("tk40f", [CB, 16], F32)
    gi2 = sb("gi2", [128, 2 * CB], I16)
    xin_g = sb("xin_g", [128, 2 * (CAP // 128) * DM], BF16)
    xin32b = sb("xin32b", [128, 128], F32)
    xin_fm = sb("xin_fm", [128, 2 * 8 * CAP], BF16)
    wi_buf = sb("wi_buf", [128, 2 * 8 * 128], BF16)
    wo_sb = sb("wo_sb", [128, (F // 128) * 512], BF16)
    h1_sb = sb("h1_sb", [128, (F // 128) * CAP], BF16)
    eo_sb = sb("eo_sb", [128, (CAP // 128) * DM], BF16)
    SC5 = CAP // 128
    FK = F // 128

    t_mset = OP("vector", lambda h: h.memset(idxt[:, :], -1))
    t_mset2 = OP("vector", lambda h: h.memset(ids_sb[:, :], 0))
    t_mset3 = OP("vector", lambda h: h.memset(gi2[:, :], 0))
    t_idld = []
    t_lsm_dump = []
    lsv = sb("lsv", [128, C2], F32)
    for gg in range(2):
        t_l = DMA("sync", rtq[:, :],
                  rt_all[4 * gg:4 * gg + 4, :, :].rearrange("a b (r f) -> (a b r) f", f=128),
                  after=[t_agrt] + t_lsm_dump)
        t_c32 = OP("vector", lambda h: h.tensor_copy(rtq32[0:C2, 0:128], rtq[0:C2, 0:128]),
                   after=[t_l])
        t_tr = OP("tensor", lambda h: h.transpose(pb[4][0:128, 0:C2], rtq32[0:C2, 0:128],
                  idnf[0:C2, 0:C2]), after=[t_c32, t_oev])
        t_ev = OP("vector", lambda h: h.tensor_copy(rtT[:, :], pb[4][0:128, 0:C2]),
                  after=[t_tr])
        FIL()
        t_ls = OP("vector", lambda h: h.tensor_scalar(lsv[:, :], rtT[:, :],
                  metab[:, 3:4], None, ALU.subtract), after=[t_ev, t_mb])
        FIL()
        t_m1 = OP("vector", lambda h: h.tensor_single_scalar(mvx[:, :], lsv[:, :], 0.0, ALU.is_ge))
        t_m2 = OP("vector", lambda h: h.tensor_single_scalar(rtT[:, :], lsv[:, :],
                  float(CAP - 1), ALU.is_le))
        FIL()
        t_m3 = OP("vector", lambda h: h.tensor_tensor(mvx[:, :], mvx[:, :], rtT[:, :], ALU.mult))
        t_a1 = OP("vector", lambda h: h.tensor_scalar(lsv[:, :], lsv[:, :], 1.0, None, ALU.add))
        FIL()
        t_a2 = OP("vector", lambda h: h.tensor_tensor(lsv[:, :], lsv[:, :], mvx[:, :], ALU.mult))
        FIL()
        t_a3 = OP("vector", lambda h: h.tensor_scalar(lsv[:, :], lsv[:, :], -1.0, None, ALU.add))
        FIL()
        t_ci = OP("vector", lambda h: h.tensor_copy(lsm16[:, :], lsv[:, :]))
        t_lsm_dump.append(DMA("sync", lsm_h[gg, :, :], lsm16[:, :], after=[t_ci]))
    t_libls = OP("gpsimd", lambda h: h.load_library(library_config.local_scatter),
                 after=t_lsm_dump + t_gi + [t_agx2])
    t_scats = []
    t_scprev = None
    for gg in range(2):
        halves = []
        for hv in range(2):
            t_li = DMA("gpsimd", idxt[0:1, 0:NCH2],
                       lsm_h[gg, :, :].rearrange("a b -> (a b)")
                       .rearrange("(a b) -> a b", a=2)[hv:hv + 1, :],
                       after=[t_libls, t_mset] + ([t_scprev] if t_scprev else []))
            t_ld = DMA("gpsimd", ids_sb[0:1, 0:NCH2], ids[gg][:, hv * NCH2:(hv + 1) * NCH2],
                       after=[t_libls, t_mset2] + ([t_scprev] if t_scprev else []))
            t_sc = OP("gpsimd", lambda h, gg=gg, hv=hv: h.local_scatter(
                tokt[:, (2 * gg + hv) * CAP:(2 * gg + hv + 1) * CAP],
                ids_sb[:, :], idxt[:, :], 16, CAP, NCH2),
                after=[t_li, t_ld])
            t_scprev = t_sc
            halves.append(t_sc)
        t_mrg = OP("vector", lambda h, gg=gg: h.tensor_tensor(
            tokt[0:1, 2 * gg * CAP:(2 * gg) * CAP + CAP],
            tokt[0:1, 2 * gg * CAP:(2 * gg) * CAP + CAP],
            tokt[0:1, (2 * gg + 1) * CAP:(2 * gg + 1) * CAP + CAP], ALU.max),
            after=halves)
        t_scats.append(t_mrg)
    t_libm = OP("gpsimd", lambda h: h.load_library(library_config.mlp), after=t_scats)
    t_eo_sts = []
    t_h1ev = [None, None]
    t_eoev = [None, None]
    t_xinu = None
    t_gi2e = None
    t_prev_eost = None
    t_wolast = [None]
    for gg in range(2):
        t_tf = OP("vector", lambda h, gg=gg: h.tensor_copy(
            tokf[:, :], tokt[0:1, 2 * gg * CAP:2 * gg * CAP + CAP]), after=[t_scats[gg], t_gi2e])
        t_tf2 = OP("vector", lambda h: h.tensor_scalar(tokf[:, :], tokf[:, :], -1.0, None, ALU.add))
        t_tf3 = OP("vector", lambda h: h.tensor_scalar(tokf[:, :], tokf[:, :], 0.0, None, ALU.max))
        t_ti16 = OP("vector", lambda h: h.tensor_copy(toki[:, :], tokf[:, :]))
        t_ts = DMA("sync", tok_h[gg:gg + 1, :], toki[:, :], after=[t_ti16])
        t_tl = DMA("sync", tk40[:, :],
                   tok_h[gg:gg + 1, :].rearrange("a (r f) -> (a r) f", f=16), after=[t_ts])
        t_t4f = OP("vector", lambda h: h.tensor_copy(tk40f[:, :], tk40[:, :]), after=[t_tl])
        t_t4t = OP("tensor", lambda h, gg=gg: h.transpose(pb[5][0:16, 0:CB], tk40f[:, :],
                   idnf[0:CB, 0:CB]), after=[t_t4f])
        t_gi2e = OP("vector", lambda h, gg=gg: h.tensor_copy(gi2[0:16, gg * CB:(gg + 1) * CB],
                    pb[5][0:16, 0:CB]), after=[t_t4t, t_mset3])
        t_g2sv = DMA("sync", git_h[:, gg * CB:(gg + 1) * CB],
                     gi2[0:16, gg * CB:(gg + 1) * CB], after=[t_gi2e])
        t_gi2e = DMA("sync", gi2[16:32, gg * CB:(gg + 1) * CB],
                     git_h[:, gg * CB:(gg + 1) * CB], after=[t_g2sv])
        xo_g = (gg % 2) * SC5 * DM
        xo_f = (gg % 2) * 8 * CAP
        t_g = m.dmaop("gpsimd", lambda h, gg=gg, xo_g=xo_g: h.dma_gather(
            out_ap=xin_g[:, xo_g:xo_g + SC5 * DM].rearrange("p (a d) -> p a d", a=SC5),
            in_ap=x2f, idxs_ap=gi2[:, gg * CB:(gg + 1) * CB],
            num_idxs=CAP, num_idxs_reg=CAP, elem_size=DM),
            after=[t_gi2e, t_libm, t_agx2])
        t_fmev = []
        t_trp = None
        t_cb = None
        for sb5 in range(SC5):
            for dmb in range(8):
                t_cb = OP("vector", lambda h, sb5=sb5, dmb=dmb, xo_g=xo_g: h.tensor_copy(
                    xin32b[:, :], xin_g[:, xo_g + sb5 * DM + dmb * 128:xo_g + sb5 * DM + (dmb + 1) * 128]),
                    after=[t_g] + ([t_tp] if t_fmev else []))
                t_tp = OP("tensor", lambda h: h.transpose(
                    pb[4][0:128, 128:256], xin32b[:, :], idnf[:, :]),
                    after=[t_cb] + ([t_trp] if t_trp else []))
                t_trp = OP("vector", lambda h, sb5=sb5, dmb=dmb, xo_f=xo_f: h.tensor_copy(
                    xin_fm[:, xo_f + dmb * CAP + sb5 * 128:xo_f + dmb * CAP + (sb5 + 1) * 128],
                    pb[4][0:128, 128:256]), after=[t_tp])
                t_fmev.append(t_trp)
        t_wi = [None, None]
        t_h1 = None
        slot_chunks = []
        off = 0
        while off < CAP:
            w = min(512, CAP - off)
            slot_chunks.append((off, w))
            off += w
        for fM in range(FK * F // F if False else (F // 128)):
            buf = fM % 2
            deps = [t_fmev[-1]] if fM == 0 else []
            if t_wi[buf]:
                deps.append(t_wi[buf])
            t_wis = [DMA("sync",
                         wi_buf[:, (buf * 8) * 128:(buf + 1) * 8 * 128]
                         .rearrange("p (a c) -> p a c", a=8),
                         wi_b.rearrange("(a p) c -> p a c", p=128)[:, :, fM * 128:(fM + 1) * 128],
                         after=deps + ([t_h1] if t_h1 else []) + t_wib)]
            t_wi[buf] = t_wis[-1]
            for (so, sw) in slot_chunks:
                bank = pb[0] if (so == 0) else pb[1]

                def h1mm(h, fM=fM, buf=buf, so=so, sw=sw, bank=bank, xo_f=xo_f):
                    r2 = None
                    for kk in range(8):
                        r2 = h.matmul(bank[:, 0:sw],
                                      wi_buf[:, (buf * 8 + kk) * 128:(buf * 8 + kk + 1) * 128],
                                      xin_fm[:, xo_f + kk * CAP + so:xo_f + kk * CAP + so + sw],
                                      start=(kk == 0), stop=(kk == 7))
                    return r2
                dep2 = t_wis + [t_fmev[-1]]
                if t_prev_eost:
                    dep2.append(t_prev_eost)
                if t_h1ev[0 if so == 0 else 1]:
                    dep2.append(t_h1ev[0 if so == 0 else 1])
                t_mm = OP("tensor", h1mm, after=dep2)
                t_h1ev[0 if so == 0 else 1] = OP("scalar",
                    lambda h, fM=fM, so=so, sw=sw, bank=bank: h.activation(
                        h1_sb[:, fM * CAP + so:fM * CAP + so + sw], bank[:, 0:sw], AF.Relu),
                    after=[t_mm])
                t_h1 = t_h1ev[0 if so == 0 else 1]
        for dmc in range(2):
            t_woh = []
            woh_dep = [t_h1ev[0], t_h1ev[1]]
            if t_wolast[0]:
                woh_dep.append(t_wolast[0])
            t_woh.append(DMA("sync",
                             wo_sb[:, :].rearrange("p (a c) -> p a c", a=FK),
                             wo_b.rearrange("(a p) c -> p a c", p=128)[:, :, dmc * 512:(dmc + 1) * 512],
                             after=woh_dep + t_wob))
            for sM in range(SC5):
                bank = pb[2] if sM % 2 == 0 else pb[3]

                def eomm(h, sM=sM, dmc=dmc, bank=bank):
                    r2 = None
                    for fK in range(FK):
                        r2 = h.matmul(bank[:, 0:512],
                                      h1_sb[:, fK * CAP + sM * 128:fK * CAP + (sM + 1) * 128],
                                      wo_sb[:, fK * 512:(fK + 1) * 512],
                                      start=(fK == 0), stop=(fK == FK - 1))
                    return r2
                dep = [t_h1ev[0], t_h1ev[1]] + t_woh
                if t_prev_eost:
                    dep.append(t_prev_eost)
                if t_eoev[sM % 2]:
                    dep.append(t_eoev[sM % 2])
                t_mm = OP("tensor", eomm, after=dep)
                t_wolast[0] = t_mm
                t_eoev[sM % 2] = OP("vector", lambda h, sM=sM, dmc=dmc, bank=bank: h.tensor_copy(
                    eo_sb[:, sM * DM + dmc * 512:sM * DM + (dmc + 1) * 512], bank[:, 0:512]),
                    after=[t_mm])
        t_eost = DMA("sync",
                     eo_loc.ap().rearrange("c g d -> c (g d)")[:, gg * DM:(gg + 1) * DM]
                     .rearrange("(a b) d -> b a d", b=128),
                     eo_sb[:, :].rearrange("p (a d) -> p a d", a=SC5),
                     after=[t_eoev[0], t_eoev[1]])
        t_eo_sts.append(t_eost)
        t_xinu = t_h1
        t_prev_eost = t_eost
    t_ageo = CC(lambda h: h.collective_compute(
        "AllGather", ALU.bypass, replica_groups=rg,
        ins=[eo_loc.ap().opt()], outs=[eo_all.ap().opt()]), after=t_eo_sts)

    if c.get("upto") == "p4":
        _maybe_dump(locals())
        t_end = m.join_all(); m.fence(t_end); m.finalize(); return nc, m, locals()
    g5 = sb("g5", [128, 2 * DM], BF16)
    x2t5 = sb("x2t5", [128, DM], F32)
    yt5 = sb("yt5", [128, DM], F32)
    x3t = sb("x3t", [128, DM], F32)
    sq5 = sb("sq5", [128, DM], F32)
    st5 = sb("st5", [128, 6], F32)
    ot5 = sb("ot5", [128, DM], F32)
    p5 = dict(s=None, x2=None)
    t_c2p = None
    for ti in range(TT):
        t_g5 = []
        for k in range(2):
            t_g5.append(m.dmaop("gpsimd", lambda h, k=k, ti=ti: h.dma_gather(
                out_ap=g5[:, k * DM:(k + 1) * DM].rearrange("p (a d) -> p a d", a=1),
                in_ap=eof, idxs_ap=gidx_p[:, k * NKIc + 8 * ti:k * NKIc + 8 * ti + 8],
                num_idxs=128, num_idxs_reg=128, elem_size=DM),
                after=[t_ageo] + t_gi + ([t_c2p] if t_c2p else [])))
        t_xl = DMA("sync", x2t5[:, :], x2_h[ti * 128:(ti + 1) * 128, :],
                   after=([p5["x2"]] if p5["x2"] else []) + t_x2_sts)
        t_c1 = OP("vector", lambda h, ti=ti: h.tensor_scalar(
            yt5[:, :], g5[:, 0:DM], gates_p[:, 2 * ti:2 * ti + 1],
            None, ALU.mult), after=t_g5)
        t_c2 = OP("vector", lambda h, ti=ti: h.scalar_tensor_tensor(
            yt5[:, :], g5[:, DM:2 * DM],
            gates_p[:, 2 * ti + 1:2 * ti + 2], yt5[:, :], ALU.mult, ALU.add))
        t_c2p = t_c2
        t_x3 = OP("vector", lambda h: h.tensor_tensor(x3t[:, :], yt5[:, :], x2t5[:, :],
                  ALU.add), after=[t_xl])
        p5["x2"] = t_x3
        t_sq = OP("scalar", lambda h: h.activation(sq5[:, :], x3t[:, :], AF.Square,
                  accum_out=st5[:, 0:1]), after=[t_x3])
        t_mu = OP("vector", lambda h: h.reduce_sum(st5[:, 1:2], x3t[:, :], AX.X))
        t_a = OP("vector", lambda h: h.tensor_scalar(st5[:, 2:3], st5[:, 0:1],
                 1.0 / DM, 1e-5, ALU.mult, ALU.add), after=[t_sq])
        t_mu2 = OP("vector", lambda h: h.tensor_scalar(st5[:, 1:2], st5[:, 1:2],
                   1.0 / DM, None, ALU.mult))
        FIL()
        t_b2 = OP("vector", lambda h: h.tensor_tensor(st5[:, 3:4], st5[:, 1:2],
                  st5[:, 1:2], ALU.mult))
        FIL()
        t_va = OP("vector", lambda h: h.tensor_tensor(st5[:, 2:3], st5[:, 2:3],
                  st5[:, 3:4], ALU.subtract))
        t_sd = OP("scalar", lambda h: h.activation(st5[:, 4:5], st5[:, 2:3], AF.Sqrt),
                  after=[t_va])
        t_rs = OP("vector", lambda h: h.reciprocal(st5[:, 5:6], st5[:, 4:5]),
                  after=[t_sd])
        FIL()
        t_o = OP("vector", lambda h: h.tensor_scalar(ot5[:, :], x3t[:, :],
                 st5[:, 1:2], st5[:, 5:6], ALU.subtract, ALU.mult),
                 after=[p5["s"]])
        p5["s"] = DMA("sync", out_t[ti * 128:(ti + 1) * 128, :], ot5[:, :], after=[t_o])
    _maybe_dump(locals())
    t_end = m.join_all()
    m.fence(t_end)

    m.finalize()
    return nc, m, locals()


def shard_inputs(c, inputs):
    DM, DI, NT = c["DM"], c["DI"], c["NT"]
    TSL, DSL = c["TSL"], c["DSL"]
    x = np.asarray(inputs["L0_attn_x"], np.float32).reshape(c["T"], DM)
    ipw = np.asarray(inputs["L0_attn_in_proj_w"], np.float32)
    in_maps = []
    for cc in range(c["NC"]):
        d0, d1 = DSL * cc, DSL * (cc + 1)
        al = np.asarray(inputs["L0_attn_A_log"], np.float32)[d0:d1].reshape(-1)
        al = np.ascontiguousarray(al.reshape(NT, 128).T)
        mt = np.zeros((1, 8), np.float32)
        mt[0, 0] = cc; mt[0, 1] = cc // 4; mt[0, 2] = (cc // 4) * (c["E"] * c["CAP"])
        mt[0, 3] = c["CAP"] * cc
        mt = np.ascontiguousarray(np.broadcast_to(mt, (128, 8)))
        m_ = {
            "x_sl": np.ascontiguousarray(x[TSL * cc:TSL * (cc + 1)]),
            "normw": np.ascontiguousarray(np.broadcast_to(
                np.asarray(inputs["L0_attn_norm_w"], np.float32).reshape(1, DM), (128, DM))),
            "winp": np.ascontiguousarray(np.concatenate(
                [ipw[:, d0:d1], ipw[:, DI + d0:DI + d1]], axis=1)),
            "convw": np.ascontiguousarray(np.asarray(inputs["L0_attn_conv_w"], np.float32)[d0:d1]),
            "convb": np.ascontiguousarray(np.asarray(inputs["L0_attn_conv_b"], np.float32)[d0:d1].reshape(-1, 1)),
            "xpw": np.ascontiguousarray(np.asarray(inputs["L0_attn_x_proj_w"], np.float32)[d0:d1]),
            "dtw": np.ascontiguousarray(np.asarray(inputs["L0_attn_dt_proj_w"], np.float32)[:, d0:d1]),
            "dtb": np.ascontiguousarray(np.asarray(inputs["L0_attn_dt_proj_b"], np.float32)[d0:d1].reshape(-1, 1)),
            "alog": np.ascontiguousarray(al),
            "dvec": np.ascontiguousarray(np.asarray(inputs["L0_attn_D"], np.float32)[d0:d1].reshape(-1, 1)),
            "outw": np.ascontiguousarray(np.asarray(inputs["L0_attn_out_proj_w"], np.float32)),
            "wr": np.ascontiguousarray(np.asarray(inputs["L0_ffn_w_router"], np.float32)),
            "wi_e": np.ascontiguousarray(np.asarray(inputs["L0_ffn_wi"], np.float32)[cc]),
            "wo_e": np.ascontiguousarray(np.asarray(inputs["L0_ffn_wo"], np.float32)[cc]),
            "meta": mt,
        }
        in_maps.append(m_)
    return in_maps


def assemble(c, outs):
    return np.concatenate(outs, axis=0).reshape(c["B"], c["L"], c["DM"])


from concourse.bass_utils import run_bass_kernel_spmd


def kernel(**inputs):
    c = make_cfg()
    nc, m, env = build_nc(c)
    in_maps = shard_inputs(c, inputs)
    res = run_bass_kernel_spmd(nc, in_maps, core_ids=list(range(c["NC"])))
    outs = [np.asarray(r["out"], np.float32) for r in res.results]
    return assemble(c, outs).astype(np.float32)
